# version 24
# speedup vs baseline: 1.1493x; 1.0043x over previous
import numpy as np
import ml_dtypes

import concourse.bass as bass
import concourse.mybir as mybir
from concourse.bass_utils import run_bass_kernel_spmd
from concourse.bass_isa import ReduceOp
import concourse.library_config as libcfg

F32 = mybir.dt.float32
BF16 = mybir.dt.bfloat16
I32 = mybir.dt.int32
AF = mybir.ActivationFunctionType
ALU = mybir.AluOpType
AXX = mybir.AxisListType.X

S = 2048
D = 1024
NT = S // 128
ND = D // 128
NHEAD = 16
MOE_E = 8
MOE_HID = 2816
SH_HID = 1536
L1_HID = 3072
NF = 513

ENGS = ("sync", "tensor", "vector", "scalar", "gpsimd")


class Prog:

    ALIAS = {}
    for _n in ("d_x", "d_sm", "d_dq", "d_dqo", "d_l1b", "d_l1f", "d_l1w",
               "d_l1g", "d_wg1", "d_wu1", "d_wu2", "d_dn", "d_out", "dbgd"):
        ALIAS[_n] = "dS"
    for _n in ("d_smg", "d_wstage", "d_gw", "d_wg", "d_cc", "d_cc2", "d_l1wg"):
        ALIAS[_n] = "dG"
    for _n in ("x_pin", "gu_pin", "t2_pin", "oo_pin"):
        ALIAS[_n] = "ar_pin"
    for _n in ("x_par", "gu_par", "t2_par", "oo_par"):
        ALIAS[_n] = "ar_par"
    for _n in ("x_sc", "gu_sc", "t2_sc", "oo_sc", "oq_sc"):
        ALIAS[_n] = "ar_sc"
    for _n in ("x_bc", "gu_bc", "t2_bc", "oq_bc"):
        ALIAS[_n] = "ar_bc"
    for _n in ("ga_z", "g2_z", "o_z", "l1s_z", "v_ones"):
        ALIAS[_n] = "zz"
    ALIAS["c_one"] = "c_id"
    ALIAS["gw_tr"] = "dq_tr"
    ALIAS["gw_ev"] = "dq_ev"
    ALIAS["g_tr"] = "dq_tr"
    ALIAS["s_tr"] = "dq_tr"
    ALIAS["s_te"] = "g_gf"
    ALIAS["sc_ev"] = "my_ev"
    ALIAS["d_xt"] = "dS"
    ALIAS["d_ct"] = "dS"
    ALIAS["g_pt"] = "rt"
    ALIAS["o_stt"] = "rt"
    ALIAS["g2_dq"] = "rt"
    ALIAS["g2_ts"] = "g_gf"
    ALIAS["s_pt"] = "rt"
    ALIAS["pos_d"] = "rt"
    ALIAS["g_ev"] = "my_ev"
    ALIAS["s_mm"] = "my_mm"

    def __init__(self, nc):
        self.nc = nc
        self.items = {e: [] for e in ENGS}
        self.sems = {}
        self.counts = {}

    def sem(self, name):
        name = self.ALIAS.get(name, name)
        if name not in self.sems:
            self.sems[name] = self.nc.alloc_semaphore(name)
            self.counts[name] = 0
        return self.sems[name]

    def cnt(self, name):
        self.sem(name)
        return self.counts[self.ALIAS.get(name, name)]

    def on(self, eng, fn, wait=None, wait_exact=None, inc=None):
        waits = []
        for nm in (wait or []):
            h = self.sem(nm)
            waits.append((h, self.counts[self.ALIAS.get(nm, nm)]))
        for nm, v in (wait_exact or []):
            waits.append((self.sem(nm), v))
        incs = []
        for nm, n in (inc or []):
            incs.append((self.sem(nm), n))
            self.counts[self.ALIAS.get(nm, nm)] += n
        self.items[eng].append((waits, fn, incs))

    def wait(self, eng, names):
        self.on(eng, None, wait=names)

    def vf(self):
        self.on("vector", lambda e: e.nop(), inc=[("vfen", 1)])
        self.on("vector", None, wait=["vfen"])

    def barrier(self, engs=ENGS):
        nm = "bar"
        for e in engs:
            self.on(e, lambda en: en.nop(), inc=[(nm, 1)])
        tgt = self.counts[nm]
        for e in engs:
            self.on(e, None, wait_exact=[(nm, tgt)])

    def finalize(self):
        nc = self.nc
        with nc.Block() as block:
            def runner(items):
                def run(engine):
                    for waits, fn, incs in items:
                        for h, v in waits:
                            if v > 0:
                                engine.wait_ge(h, v)
                        if fn is not None:
                            inst = fn(engine)
                            if incs:
                                inst.then_inc(*incs[0])
                                for h, n in incs[1:]:
                                    engine.nop().then_inc(h, n)
                return run
            block.sync(runner(self.items["sync"]))
            block.tensor(runner(self.items["tensor"]))
            block.vector(runner(self.items["vector"]))
            block.scalar(runner(self.items["scalar"]))
            block.gpsimd(runner(self.items["gpsimd"]))


def ts(i, n=128):
    return slice(i * n, (i + 1) * n)


def build_nc(dbg=False):
    nc = bass.Bass("TRN2", target_bir_lowering=False, debug=False,
                   detect_race_conditions=False)

    dram = {}
    def din(name, shape, dtype=F32):
        dram[name] = nc.dram_tensor(name, list(shape), dtype, kind="ExternalInput")
        return dram[name]

    din("x", (S, D))
    for w in ("Wq", "Wk", "Wv", "Wo"):
        din("L0_" + w, (D, D))
    for b in ("bq", "bk", "bv", "bo"):
        din("L0_" + b, (D,))
    din("L0_slopes", (NHEAD,))
    din("gate_w", (MOE_E, D))
    din("w1", (MOE_E, D, MOE_HID))
    din("w2", (MOE_E, MOE_HID, D))
    din("w3", (MOE_E, D, MOE_HID))
    din("sw1", (D, SH_HID))
    din("sw2", (SH_HID, D))
    din("sw3", (D, SH_HID))
    for w in ("Wq", "Wk", "Wv"):
        din("L1_" + w, (D, D))
    for b in ("bq", "bk", "bv"):
        din("L1_" + b, (D,))
    din("w_gate", (L1_HID, D))
    din("w_up", (L1_HID, D))
    din("w_down", (D, L1_HID))
    din("tri", (128, 128), BF16)
    din("stri", (128, 128), BF16)
    din("iotaw", (128, 768), F32)
    din("fre", (D, NF), BF16)
    din("fim", (D, NF), BF16)
    din("gre", (NF, D), BF16)
    din("gim", (NF, D), BF16)

    out = nc.dram_tensor("out", [S, D], F32, kind="ExternalOutput")
    dbg_t = {}
    if dbg:
        dbg_t["stats"] = nc.dram_tensor("dbg_stats", [128, 6 * NT], F32, kind="ExternalOutput")
        dbg_t["xn"] = nc.dram_tensor("dbg_xn", [128, ND * S], BF16, kind="ExternalOutput")
        dbg_t["attn"] = nc.dram_tensor("dbg_attn", [128, NT * D], F32, kind="ExternalOutput")
        dbg_t["moe"] = nc.dram_tensor("dbg_moe", [128, NT * D], F32, kind="ExternalOutput")
        dbg_t["l1a"] = nc.dram_tensor("dbg_l1a", [128, NT * D], F32, kind="ExternalOutput")
        dbg_t["scr"] = nc.dram_tensor("dbg_scr", [ND, 128, L1_HID], BF16, kind="ExternalOutput")
        dbg_t["wt"] = nc.dram_tensor("dbg_wt", [128, ND * D], BF16, kind="ExternalOutput")
        dbg_t["qp"] = nc.dram_tensor("dbg_qp", [128, S], BF16, kind="ExternalOutput")
        dbg_t["kp"] = nc.dram_tensor("dbg_kp", [128, S], BF16, kind="ExternalOutput")
        dbg_t["va"] = nc.dram_tensor("dbg_va", [128, NT * 130], BF16, kind="ExternalOutput")
        dbg_t["ctxp"] = nc.dram_tensor("dbg_ctxp", [128, S], BF16, kind="ExternalOutput")

    scr = {
        "up": nc.dram_tensor("scr_upT", [ND, 128, L1_HID], BF16, kind="Internal"),
        "gate": nc.dram_tensor("scr_gateT", [ND, 128, L1_HID], BF16, kind="Internal"),
        "down": nc.dram_tensor("scr_downT", [L1_HID // 128, 128, D], BF16, kind="Internal"),
        "xn": nc.dram_tensor("scr_xn", [NT, 128, D], BF16, kind="Internal"),
    }
    cc_in = nc.dram_tensor("cc_in", [1, 2], F32, kind="Internal")
    cc_out = nc.dram_tensor("cc_out", [1, 2], F32, kind="Internal")

    p = Prog(nc)

    A = nc.alloc_sbuf_tensor
    x_res = A("x_res", [128, NT * D], F32)
    xn_fm = A("xn_fm", [128, ND * S], BF16)
    ident = A("ident", [128, 128], BF16)
    bq_sb = A("bq_sb", [128, ND], F32)
    bk_sb = A("bk_sb", [128, ND], F32)
    slopes_sb = A("slopes_sb", [1, NHEAD], BF16)
    identF = A("identF", [128, 128], F32)
    onesf = A("onesf", [1, 128], F32)
    iot = A("iot", [128, NT], F32)
    iot_i = A("iot_i", [128, NT], I32)
    sums = A("sums", [128, NT], F32)
    sumsq = A("sumsq", [128, NT], F32)
    m_sb = A("m_sb", [128, NT], F32)
    r_sb = A("r_sb", [128, NT], F32)
    stat_a = A("stat_a", [128, NT], F32)
    stat_b = A("stat_b", [128, NT], F32)
    dbg_s1 = A("dbg_s1", [128, NT], F32)
    dbg_s2 = A("dbg_s2", [128, NT], F32)
    dbg_s3 = A("dbg_s3", [128, NT], F32)
    dbg_s4 = A("dbg_s4", [128, NT], F32)

    PB = [nc.alloc_psum_tensor("pb%d" % i, [128, 512], F32) for i in range(8)]

    p.on("sync", lambda e: e.dma_start(
        out=x_res.ap().rearrange("p (t d) -> p t d", d=D),
        in_=dram["x"].ap().rearrange("(t p) d -> p t d", p=128)),
        inc=[("d_x", 16)])
    for nm, dst, rr in (("L0_bq", bq_sb, True), ("L0_bk", bk_sb, True)):
        p.on("sync", lambda e, nm=nm, dst=dst: e.dma_start(
            out=dst.ap(), in_=dram[nm].ap().rearrange("(c p) -> p c", p=128),
            allow_slow_non_contiguous=True),
            inc=[("d_sm", 16)])
    p.on("gpsimd", lambda e: e.dma_start(
        out=slopes_sb.ap(), in_=dram["L0_slopes"].ap().rearrange("(o d) -> o d", o=1)),
        inc=[("d_smg", 16)])
    p.on("vector", lambda e: e.memset(ident[:], 1.0), inc=[("c_id", 1)])
    p.on("gpsimd", lambda e: e.affine_select(
        ident[:], ident[:], [[1, 128]], ALU.is_equal, 0.0, base=0,
        channel_multiplier=-1), wait=["c_id"], inc=[("c_id", 1)])
    p.on("vector", lambda e: e.memset(identF[:], 1.0), inc=[("c_id", 1)])
    p.on("gpsimd", lambda e: e.affine_select(
        identF[:], identF[:], [[1, 128]], ALU.is_equal, 0.0, base=0,
        channel_multiplier=-1), wait=["c_id"], inc=[("c_id", 1)])
    p.on("vector", lambda e: e.memset(onesf[:], 1.0), inc=[("c_id", 1)])
    p.on("gpsimd", lambda e: e.iota(iot_i[:], [[128, NT]], base=-2047,
                                    channel_multiplier=1),
         inc=[("c_id", 1)])
    p.on("vector", lambda e: e.tensor_copy(iot[:], iot_i[:]), wait=["c_id"],
         inc=[("c_id", 1)])

    def emit_layernorm(tg, sqdump, xn_tm, first=False, spill=None):
        xnm = []
        s_sum, s_sq = "ln_s", "ln_q"
        s_v, s_r = "ln_v", "ln_r"
        s_tm, s_tr = "ln_tm", "ln_tr"
        s_evS, s_evV = "ln_evS", "ln_evV"
        p.on("scalar", lambda e: e.memzero(sumsq[:]), inc=[(s_sq, 1)])
        for t in range(NT):
            p.on("vector", lambda e, t=t: e.reduce_sum(
                sums[:, t:t + 1], x_res[:, ts(t, D)], axis=AXX),
                wait=(["d_x"] if (first and t == 0) else None), inc=[(s_sum, 1)])
            p.on("scalar", lambda e, t=t: e.activation(
                out=sqdump[:], in_=x_res[:, ts(t, D)], func=AF.Square,
                accum_out=sumsq[:, t:t + 1]),
                wait=(["d_x"] if (first and t == 0) else None), inc=[(s_sq, 1)])
        p.wait("vector", [s_sum, s_sq])
        p.on("vector", lambda e: e.tensor_scalar_mul(m_sb[:], sums[:], 1.0 / D))
        p.on("vector", lambda e: e.tensor_scalar_mul(stat_a[:], sumsq[:], 1.0 / D))
        p.vf()
        p.on("vector", lambda e: e.tensor_mul(stat_b[:], m_sb[:], m_sb[:]))
        p.vf()
        p.on("vector", lambda e: e.tensor_sub(stat_a[:], stat_a[:], stat_b[:]))
        p.vf()
        p.on("vector", lambda e: e.tensor_scalar_add(stat_a[:], stat_a[:], 1e-5),
             inc=[(s_v, 1)])
        p.on("vector", lambda e: e.tensor_copy(dbg_s1[:], stat_a[:]))
        p.on("scalar", lambda e: e.sqrt(stat_b[:], stat_a[:]), wait=[s_v],
             inc=[(s_v, 1)])
        p.on("vector", lambda e: e.reciprocal(r_sb[:], stat_b[:]), wait=[s_v],
             inc=[("rcp", 1)])
        p.wait("vector", ["rcp"])
        p.on("vector", lambda e: e.tensor_mul(stat_b[:], r_sb[:], r_sb[:]))
        p.vf()
        p.on("vector", lambda e: e.tensor_mul(stat_b[:], stat_b[:], stat_a[:]))
        p.vf()
        p.on("vector", lambda e: e.tensor_scalar(
            stat_b[:], stat_b[:], -0.5, 1.5, ALU.mult, ALU.add))
        p.vf()
        p.on("vector", lambda e: e.tensor_mul(r_sb[:], r_sb[:], stat_b[:]),
             inc=[(s_r, 1)])
        for t in range(NT):
            buf = xn_tm[:, ts(t % 2, D)]
            we = [(s_tr, p.cnt(s_tr) - ND)] if t >= 2 else []
            if spill is not None and t >= 2:
                we = we + [("d_xn", xnm[t - 2])]
            p.on("vector", lambda e, t=t, buf=buf: e.tensor_scalar(
                buf, x_res[:, ts(t, D)], m_sb[:, t:t + 1], r_sb[:, t:t + 1],
                ALU.subtract, ALU.mult),
                wait=([s_r] if t == 0 else None), wait_exact=we, inc=[(s_tm, 1)])
            if spill is not None:
                p.on("sync", lambda e, t=t, buf=buf: e.dma_start(
                    out=spill.ap()[t], in_=buf),
                    wait_exact=[(s_tm, p.cnt(s_tm))], inc=[("d_xn", 16)])
                xnm.append(p.cnt("d_xn"))
            for dc in range(ND):
                k = t * ND + dc
                pb = PB[k % 2]
                sev = s_evS if k % 2 == 0 else s_evV
                we2 = [(sev, p.cnt(sev))] if k >= 2 else []
                p.on("tensor", lambda e, buf=buf, dc=dc, pb=pb: e.matmul(
                    pb[:, 0:128], buf[:, ts(dc, 128)], ident[:],
                    start=True, stop=True),
                    wait_exact=[(s_tm, p.cnt(s_tm))] + we2, inc=[(s_tr, 1)])
                dst_lo = dc * S + t * 128
                if k % 2 == 0:
                    p.on("scalar", lambda e, pb=pb, dst_lo=dst_lo: e.activation(
                        out=xn_fm[:, dst_lo:dst_lo + 128], in_=pb[:, 0:128],
                        func=AF.Identity),
                        wait_exact=[(s_tr, p.cnt(s_tr))], inc=[(s_evS, 1)])
                else:
                    p.on("vector", lambda e, pb=pb, dst_lo=dst_lo: e.tensor_copy(
                        xn_fm[:, dst_lo:dst_lo + 128], pb[:, 0:128]),
                        wait_exact=[(s_tr, p.cnt(s_tr))], inc=[(s_evV, 1)])

    def emit_attn_l0(st):
        WT = st["WT"]; wstage = st["wstage"]
        q_pair = st["q_pair"]; k_pair = st["k_pair"]
        v_aug = st["v_aug"]; ctx_pair = st["ctx_pair"]
        e_sb = st["e_sb"]
        ones_row = st["ones_row"]; bv_row = st["bv_row"]; bo_row = st["bo_row"]
        alibi = st["alibi"]; slopes_bc = st["slopes_bc"]

        for nm, dst in (("L0_bv", bv_row), ("L0_bo", bo_row)):
            p.on("gpsimd", lambda e, nm=nm, dst=dst: e.dma_start(
                out=dst.ap(), in_=dram[nm].ap().rearrange("(o d) -> o d", o=1)),
                inc=[("d_smg", 16)])
        p.on("vector", lambda e: e.memset(ones_row[:], 1.0), inc=[("c_one", 1)])
        p.on("tensor", lambda e: e.matmul(
            PB[4][:, 0:NHEAD], ones_row[0:1, 0:128], slopes_sb[:],
            start=True, stop=True),
            wait=["d_smg", "c_one", "c_id"], inc=[("c_alibi", 1)])
        p.on("vector", lambda e: e.tensor_copy(slopes_bc[:], PB[4][:, 0:NHEAD]),
             wait_exact=[("c_alibi", p.cnt("c_alibi"))], inc=[("c_alibi", 1)])
        p.vf()
        for h in range(NHEAD):
            p.on("vector", lambda e, h=h: e.tensor_mul(
                alibi[:, h * NT:(h + 1) * NT], iot[:],
                slopes_bc[:, h:h + 1].broadcast_to([128, NT])),
                inc=[("c_alibi", 1)])

        for wi, w in enumerate(("Wq", "Wk", "Wv", "Wo")):
            wst_marks = []
            for half in range(2):
                p.on("gpsimd", lambda e, w=w, half=half: e.dma_start(
                    out=wstage.ap().rearrange("p (c i) -> p c i", i=D),
                    in_=dram["L0_" + w].ap().rearrange("(c p) i -> p c i", p=128)
                        [:, half * 4:(half + 1) * 4, :]),
                    wait_exact=[("wt_tr", wi * 64 + half * 32)],
                    inc=[("d_wstage", 16)])
                wst_marks.append(p.cnt("d_wstage"))
            for oc in range(ND):
                for ic in range(ND):
                    k = wi * 64 + oc * ND + ic
                    pb = PB[2 + (k % 2)]
                    sev = "wt_evS" if k % 2 == 0 else "wt_evV"
                    we = [(sev, p.cnt(sev))] if k >= 2 else []
                    p.on("tensor", lambda e, oc=oc, ic=ic, pb=pb: e.matmul(
                        pb[:, 0:128],
                        wstage[:, (oc % 4) * D + ic * 128: (oc % 4) * D + (ic + 1) * 128],
                        ident[:], start=True, stop=True),
                        wait_exact=[("d_wstage", wst_marks[oc // 4])] + we,
                        inc=[("wt_tr", 1)])
                    dst = WT[w][:, ic * D + oc * 128: ic * D + (oc + 1) * 128]
                    if k % 2 == 0:
                        p.on("scalar", lambda e, pb=pb, dst=dst: e.activation(
                            out=dst, in_=pb[:, 0:128], func=AF.Identity),
                            wait_exact=[("wt_tr", p.cnt("wt_tr"))],
                            inc=[("wt_evS", 1), ("wt_done", 1)])
                    else:
                        p.on("vector", lambda e, pb=pb, dst=dst: e.tensor_copy(
                            dst, pb[:, 0:128]),
                            wait_exact=[("wt_tr", p.cnt("wt_tr"))],
                            inc=[("wt_evV", 1), ("wt_done", 1)])

        for pr in range(NHEAD // 2):
            for name, dstt, bias, pbq in (("Wq", q_pair, bq_sb, PB[0]),
                                          ("Wk", k_pair, bk_sb, PB[1])):
                for tc in range(4):
                    for kc in range(ND):
                        wex = []
                        if tc == 0 and kc == 0:
                            wex = [("wt_done", (1 if name == "Wq" else 2) * 64),
                                   ("ex_0", p.cnt("ex_0")), ("ex_1", p.cnt("ex_1")),
                                   ("wo_ev", p.cnt("wo_ev"))]
                            if dbg:
                                wex += [("dbgd", p.cnt("dbgd"))]
                        elif kc == 0:
                            wex = [("qk_ev", p.cnt("qk_ev"))]
                        p.on("tensor", lambda e, pbq=pbq, name=name, kc=kc, pr=pr, tc=tc: e.matmul(
                            pbq[:], WT[name][:, kc * D + pr * 128: kc * D + (pr + 1) * 128],
                            xn_fm[:, kc * S + tc * 512: kc * S + (tc + 1) * 512],
                            start=(kc == 0), stop=(kc == ND - 1)),
                            wait_exact=wex,
                            inc=([("qk_mm", 1)] if kc == ND - 1 else None))
                    p.on("scalar", lambda e, pbq=pbq, dstt=dstt, bias=bias, pr=pr, tc=tc: e.activation(
                        out=dstt[:, tc * 512:(tc + 1) * 512], in_=pbq[:],
                        func=AF.Identity, bias=bias[:, pr:pr + 1]),
                        wait=(["d_sm"] if (pr == 0 and tc == 0) else None),
                        wait_exact=[("qk_mm", p.cnt("qk_mm"))], inc=[("qk_ev", 1)])
            p.on("vector", lambda e: e.memset(
                v_aug.ap().rearrange("p (t c) -> p t c", c=130)[:, :, 64:65], 1.0),
                wait_exact=[("cx_0", p.cnt("cx_0")), ("cx_1", p.cnt("cx_1"))],
                inc=[("v_ones", 1)])
            p.on("vector", lambda e: e.memset(
                v_aug.ap().rearrange("p (t c) -> p t c", c=130)[:, :, 129:130], 1.0),
                inc=[("v_ones", 1)])
            for tt in range(NT):
                pb = PB[6 + tt % 2]
                for kc in range(ND):
                    wex = []
                    if tt == 0 and kc == 0:
                        wex = [("wt_done", 3 * 64),
                               ("cx_0", p.cnt("cx_0")), ("cx_1", p.cnt("cx_1"))]
                    elif kc == 0 and tt >= 2:
                        wex = [("v_ev", p.cnt("v_ev") - 1)]
                    p.on("tensor", lambda e, pb=pb, kc=kc, tt=tt, pr=pr: e.matmul(
                        pb[:, 0:128],
                        xn_fm[:, kc * S + tt * 128: kc * S + (tt + 1) * 128],
                        WT["Wv"][:, kc * D + pr * 128: kc * D + (pr + 1) * 128],
                        start=(kc == 0), stop=False), wait_exact=wex)
                p.on("tensor", lambda e, pb=pb, pr=pr: e.matmul(
                    pb[:, 0:128], ones_row[0:1, 0:128],
                    bv_row[0:1, pr * 128:(pr + 1) * 128], start=False, stop=True),
                    wait=((["d_smg", "c_one"]) if pr == 0 and tt == 0 else None),
                    inc=[("v_mm", 1)])
                p.on("vector", lambda e, pb=pb, tt=tt: e.tensor_copy(
                    v_aug.ap().rearrange("p (t c) -> p t c", c=130)
                         [:, tt:tt + 1, :].rearrange("p t (h c) -> p (t h) c", c=65)
                         [:, :, 0:64],
                    pb[:, 0:128].rearrange("p (h c) -> p h c", c=64)),
                    wait_exact=[("v_mm", p.cnt("v_mm"))], inc=[("v_ev", 1)])
            pending_evac = []
            for qc in range(4):
                for hh in range(2):
                    hr = slice(hh * 64, hh * 64 + 64)
                    h = pr * 2 + hh
                    pb_s = (PB[0 + hh], PB[2 + hh])
                    pb_ctx = PB[4 + hh]
                    pb_bc = PB[6 + hh]
                    ctxr = st["ctxr"][hh]
                    r_row = st["r_row"][hh]
                    sc_s, ex_s = "sc_%d" % hh, "ex_%d" % hh
                    pvr, cx_s = "pvr_%d" % hh, "cx_%d" % hh
                    cp_s = "cp_%d" % hh
                    def _pv(kc, exc):
                        eb = e_sb[:, (hh * 4 + kc % 4) * 512:(hh * 4 + kc % 4 + 1) * 512]
                        wex2 = [(ex_s, exc)]
                        if kc == 0:
                            wex2 += [(cp_s, p.cnt(cp_s))]
                            if qc <= 1:
                                wex2 += [("v_ev", p.cnt("v_ev"))]
                        p.on("tensor", lambda e, pb_ctx=pb_ctx, eb=eb, kc=kc, hh=hh: e.matmul(
                            pb_ctx[0:65, :],
                            v_aug[:, kc * 130 + hh * 65: kc * 130 + hh * 65 + 65],
                            eb, start=(kc == 0), stop=(kc == NT - 1)),
                            wait_exact=wex2, inc=[(pvr, 1)])
                    exmarks = {}
                    for kc in range(NT):
                        pbs = pb_s[kc % 2]
                        wex = [(ex_s, max(0, p.cnt(ex_s) - 1))]
                        if kc == 0 and qc == 0:
                            wex += [("qk_ev", p.cnt("qk_ev")),
                                    ("v_ev", p.cnt("v_ev")),
                                    ("v_ones", p.cnt("v_ones")),
                                    ("c_alibi", 18)]
                        p.on("tensor", lambda e, pbs=pbs, hr=hr, kc=kc, qc=qc: e.matmul(
                            pbs[:], k_pair[hr, ts(kc, 128)], q_pair[hr, ts(qc, 512)],
                            start=True, stop=True),
                            wait_exact=wex, inc=[(sc_s, 1)])
                        eb = e_sb[:, (hh * 4 + kc % 4) * 512:(hh * 4 + kc % 4 + 1) * 512]
                        p.on("scalar", lambda e, pbs=pbs, eb=eb, h=h, kc=kc: e.activation(
                            out=eb, in_=pbs[:], func=AF.Exp, scale=0.125,
                            bias=alibi[:, h * NT + kc: h * NT + kc + 1]),
                            wait_exact=[(sc_s, p.cnt(sc_s))], inc=[(ex_s, 1)])
                        exmarks[kc] = p.cnt(ex_s)
                        if kc >= 2:
                            _pv(kc - 2, exmarks[kc - 2])
                    _pv(NT - 2, exmarks[NT - 2])
                    _pv(NT - 1, exmarks[NT - 1])
                    p.on("vector", lambda e, pb_ctx=pb_ctx, ctxr=ctxr: e.tensor_copy(
                        ctxr[:], pb_ctx[0:65, :]),
                        wait_exact=[(pvr, p.cnt(pvr)),
                                    (cx_s, p.cnt(cx_s))], inc=[(cp_s, 1)])
                    p.vf()
                    p.on("vector", lambda e, ctxr=ctxr: e.reciprocal(
                        ctxr[64:65, :], ctxr[64:65, :]), inc=[("rcp", 1)])
                    p.wait("vector", ["rcp"])
                    p.on("vector", lambda e, ctxr=ctxr, r_row=r_row: e.tensor_copy(
                        r_row[:], ctxr[64:65, :]), inc=[("rc_%d" % hh, 1)])

                    def _evac(hh=hh, qc=qc, pb_bc=pb_bc, ctxr=ctxr, r_row=r_row,
                              rc_t=p.cnt("rc_%d" % hh)):
                        cx_s2 = "cx_%d" % hh
                        p.on("tensor", lambda e: e.matmul(
                            pb_bc[0:64, :], ones_row[0:1, 0:64], r_row[:],
                            start=True, stop=True),
                            wait_exact=[("rc_%d" % hh, rc_t),
                                        (cx_s2, p.cnt(cx_s2))],
                            inc=[("rb_%d" % hh, 1)])
                        p.on("vector", lambda e: e.tensor_mul(
                            ctx_pair[slice(hh * 64, hh * 64 + 64), ts(qc, 512)],
                            ctxr[0:64, :], pb_bc[0:64, :]),
                            wait_exact=[("rb_%d" % hh, p.cnt("rb_%d" % hh))],
                            inc=[(cx_s2, 1)])
                    pending_evac.append(_evac)
                    if len(pending_evac) > 1:
                        pending_evac.pop(0)()
            while pending_evac:
                pending_evac.pop(0)()
            if dbg and pr == 0:
                p.on("sync", lambda e: e.dma_start(
                    out=dbg_t["wt"].ap(), in_=WT["Wq"].ap()),
                    wait_exact=[("wt_done", 64)], inc=[("dbgd", 16)])
                p.on("sync", lambda e: e.dma_start(
                    out=dbg_t["qp"].ap(), in_=q_pair.ap()),
                    wait_exact=[("qk_ev", p.cnt("qk_ev"))], inc=[("dbgd", 16)])
                p.on("sync", lambda e: e.dma_start(
                    out=dbg_t["kp"].ap(), in_=k_pair.ap()), inc=[("dbgd", 16)])
                p.on("sync", lambda e: e.dma_start(
                    out=dbg_t["va"].ap(), in_=v_aug.ap()),
                    wait_exact=[("v_ev", p.cnt("v_ev"))], inc=[("dbgd", 16)])
                p.on("sync", lambda e: e.dma_start(
                    out=dbg_t["ctxp"].ap(), in_=ctx_pair.ap()),
                    wait_exact=[("cx_0", p.cnt("cx_0")), ("cx_1", p.cnt("cx_1"))],
                    inc=[("dbgd", 16)])
            for tt in range(NT):
                for oc in range(2):
                    j = tt * 2 + oc
                    pbo = PB[j % 2]
                    wex = []
                    if j == 0:
                        wex = [("cx_0", p.cnt("cx_0")), ("cx_1", p.cnt("cx_1")),
                               ("wt_done", 4 * 64),
                               ("ex_0", p.cnt("ex_0")), ("ex_1", p.cnt("ex_1"))]
                    elif j >= 2:
                        wex = [("wo_ev", p.cnt("wo_ev") - 1)]
                    if pr == 0:
                        p.on("tensor", lambda e, pbo=pbo, oc=oc: e.matmul(
                            pbo[:], ones_row[0:1, 0:128],
                            bo_row[0:1, oc * 512:(oc + 1) * 512],
                            start=True, stop=False),
                            wait=(["d_smg"] if j == 0 else None), wait_exact=wex)
                        wex = []
                    p.on("tensor", lambda e, pbo=pbo, tt=tt, oc=oc, pr=pr: e.matmul(
                        pbo[:], ctx_pair[:, ts(tt, 128)],
                        WT["Wo"][:, pr * D + oc * 512: pr * D + (oc + 1) * 512],
                        start=(pr != 0), stop=True),
                        wait_exact=wex, inc=[("wo_mm", 1)])
                    p.on("vector", lambda e, pbo=pbo, tt=tt, oc=oc: e.scalar_tensor_tensor(
                        x_res[:, tt * D + oc * 512: tt * D + (oc + 1) * 512],
                        pbo[:], 1.0,
                        x_res[:, tt * D + oc * 512: tt * D + (oc + 1) * 512],
                        ALU.mult, ALU.add),
                        wait_exact=[("wo_mm", p.cnt("wo_mm"))], inc=[("wo_ev", 1)])

    def emit_moe(st, filler=None):
        gw_stage = st["gw_stage"]; gwT = st["gwT"]
        logit_sb = st["logit_sb"]; comb = st["comb"]
        g_t0 = st["g_t0"]; g_t1 = st["g_t1"]; g_c0 = st["g_c0"]; g_c1 = st["g_c1"]
        w1g = st["w1g"]; w3g = st["w3g"]; w2g = st["w2g"]
        h_sb = st["h_sb"]
        hsil = st["hsil"]
        tri_sb = st["tri_sb"]; stri_sb = st["stri_sb"]; iota_sb = st["iota_sb"]
        route_m = st["route_m"]; t_sb = st["t_sb"]; pos = st["pos"]
        onesb = st["onesb"]; ccol_sb = st["ccol_sb"]; base_sb = st["base_sb"]
        pt_sb = st["pt_sb"]; ptT_sb = st["ptT_sb"]
        xt_sb = st["xt_sb"]; gth_sb = st["gth_sb"]
        gath_fm = st["gath_fm"]; y_sb = st["y_sb"]
        yb = gath_fm

        p.on("gpsimd", lambda e: e.dma_start(
            out=gw_stage[0:MOE_E, :], in_=dram["gate_w"].ap()),
            inc=[("d_gw", 16)])
        gw_mark = p.cnt("d_gw")
        for kc in range(ND):
            p.on("tensor", lambda e, kc=kc: e.matmul(
                PB[0][:, 0:8], gw_stage[0:MOE_E, ts(kc, 128)], ident[0:8, 0:8],
                start=True, stop=True),
                wait_exact=([("d_gw", gw_mark)] if kc == 0 else
                            [("gw_ev", p.cnt("gw_ev"))]),
                inc=[("gw_tr", 1)])
            p.on("vector", lambda e, kc=kc: e.tensor_copy(
                gwT[:, kc * 8:(kc + 1) * 8], PB[0][:, 0:8]),
                wait_exact=[("gw_tr", p.cnt("gw_tr"))], inc=[("gw_ev", 1)])
        for tt in range(NT):
            for kc in range(ND):
                p.on("tensor", lambda e, tt=tt, kc=kc: e.matmul(
                    PB[1][:, 0:8], xn_fm[:, kc * S + tt * 128: kc * S + (tt + 1) * 128],
                    gwT[:, kc * 8:(kc + 1) * 8],
                    start=(kc == 0), stop=(kc == ND - 1)),
                    wait_exact=([("gw_ev", p.cnt("gw_ev"))] if (tt == 0 and kc == 0) else
                                ([("gl_ev", p.cnt("gl_ev"))] if kc == 0 else [])),
                    inc=([("gl_mm", 1)] if kc == ND - 1 else None))
            p.on("vector", lambda e: e.tensor_copy(logit_sb[:], PB[1][:, 0:8]),
                 wait_exact=[("gl_mm", p.cnt("gl_mm"))], inc=[("gl_ev", 1)])
            p.vf()
            p.on("vector", lambda e: e.reduce_max(g_c0[:], logit_sb[:], axis=AXX))
            p.vf()
            p.on("vector", lambda e: e.tensor_scalar_mul(g_c0[:], g_c0[:], -1.0),
                 inc=[("g_nm", 1)])
            p.on("scalar", lambda e: e.activation(
                out=g_t0[:], in_=logit_sb[:], func=AF.Exp, bias=g_c0[:]),
                wait_exact=[("g_nm", p.cnt("g_nm"))], inc=[("g_ex", 1)])
            p.on("vector", lambda e: e.reduce_sum(g_c0[:], g_t0[:], axis=AXX),
                 wait_exact=[("g_ex", p.cnt("g_ex"))])
            p.vf()
            p.on("vector", lambda e: e.reciprocal(g_c1[:], g_c0[:]),
                 inc=[("rcp", 1)])
            p.wait("vector", ["rcp"])
            p.on("vector", lambda e: e.tensor_scalar_mul(g_t0[:], g_t0[:], g_c1[:]))
            p.vf()
            p.on("vector", lambda e: e.reduce_max(g_c0[:], g_t0[:], axis=AXX))
            p.vf()
            p.on("vector", lambda e: e.tensor_single_scalar(
                g_t1[:], g_t0[:], g_c0[:], ALU.is_lt))
            p.vf()
            p.on("vector", lambda e: e.tensor_mul(g_t1[:], g_t0[:], g_t1[:]))
            p.vf()
            p.on("vector", lambda e: e.reduce_max(g_c1[:], g_t1[:], axis=AXX))
            p.vf()
            p.on("vector", lambda e: e.tensor_single_scalar(
                g_t1[:], g_t0[:], g_c1[:], ALU.is_ge))
            p.vf()
            p.on("vector", lambda e, tt=tt: e.tensor_mul(
                comb[:, tt * 8:(tt + 1) * 8], g_t0[:], g_t1[:]),
                inc=[("comb_d", 1)])

        for nm, dst in (("tri", tri_sb), ("stri", stri_sb), ("iotaw", iota_sb)):
            p.on("sync", lambda e, nm=nm, dst=dst: e.dma_start(
                out=dst.ap(), in_=dram[nm].ap()), inc=[("d_ct", 16)])
        ct_mark = p.cnt("d_ct")
        p.on("vector", lambda e: e.memset(onesb[:], 1.0), inc=[("rt", 1)])
        p.on("vector", lambda e: e.tensor_scalar(
            route_m[:], comb[:], 0.0, 0.0, ALU.is_gt, ALU.bypass),
            wait_exact=[("comb_d", 16), ("d_ct", ct_mark)], inc=[("rt", 1)])
        p.on("tensor", lambda e: e.matmul(PB[0][:, 0:128], tri_sb[:], route_m[:],
                                          start=True, stop=True),
             wait_exact=[("rt", p.cnt("rt")), ("dq_ev", p.cnt("dq_ev"))],
             inc=[("g_tr", 1)])
        p.on("tensor", lambda e: e.matmul(PB[1][:, 0:1], route_m[:], onesb[:],
                                          start=True, stop=True),
             wait_exact=[("gl_ev", p.cnt("gl_ev"))], inc=[("g_tr", 1)])
        p.on("vector", lambda e: e.tensor_sub(t_sb[:], PB[0][:, 0:128], route_m[:]),
             wait_exact=[("g_tr", p.cnt("g_tr"))], inc=[("rt", 1)])
        p.on("vector", lambda e: e.tensor_copy(ccol_sb[:], PB[1][:, 0:1]),
             inc=[("rt", 1)])
        p.on("tensor", lambda e: e.matmul(PB[2][0:1, 0:128], ccol_sb[:], stri_sb[:],
                                          start=True, stop=True),
             wait_exact=[("rt", p.cnt("rt"))], inc=[("g_tr", 1)])
        p.on("vector", lambda e: e.tensor_copy(base_sb[0:1, :], PB[2][0:1, 0:128]),
             wait_exact=[("g_tr", p.cnt("g_tr"))], inc=[("rt", 1)])
        p.on("tensor", lambda e: e.matmul(
            PB[3][:, 0:128], onesf[0:1, 0:128], base_sb[0:1, :],
            start=True, stop=True),
             wait_exact=[("rt", p.cnt("rt"))], inc=[("g_tr", 1)])
        p.on("vector", lambda e: e.tensor_add(t_sb[:], t_sb[:], PB[3][:, 0:128]),
             wait_exact=[("g_tr", p.cnt("g_tr"))])
        p.vf()
        p.on("vector", lambda e: e.scalar_tensor_tensor(
            pos[:], t_sb[:], 1.0, route_m[:], ALU.add, ALU.mult))
        p.vf()
        p.on("vector", lambda e: e.tensor_scalar_add(pos[:], pos[:], -1.0),
             inc=[("pos_d", 1)])
        pmark = p.cnt("pos_d")
        p.vf()

        CAP = 768
        state = {"drained": 0, "gi": 0, "wy": 0, "hseq": 0, "dpt": 0}
        fh = [(0, 0), (0, 0)]
        NGRP = 8 * 8 + 4

        NPT = 8 * 4 + 4
        def drain():
            state["dpt"] += 1
            tgt = (144 * state["dpt"] + NPT - 1) // NPT
            while filler is not None and state["drained"] < tgt:
                next(filler, None)
                state["drained"] += 1

        for e_i in range(MOE_E):
            for dc in range(2):
                for tt in range(NT):
                    s = e_i * 32 + dc * NT + tt
                    ptb = pt_sb[tt % 2]
                    xt = xt_sb[tt % 2]
                    wex = [("gstep", s - 1)] if s >= 2 else [("d_xn", 16 * NT)]
                    p.on("sync", lambda e, xt=xt, tt=tt, dc=dc: e.dma_start(
                        out=xt[:], in_=scr["xn"].ap()[tt][:, dc * 512:(dc + 1) * 512]),
                        wait_exact=wex, inc=[("d_xt", 16)])
                    xtmark = p.cnt("d_xt")
                    for sc in range(6):
                        wex = []
                        if sc == 0:
                            if s >= 2:
                                wex = [("gstep", s - 1)]
                            if dc == 0 and tt == 0:
                                wex += [("pos_d", pmark), ("s_tr", p.cnt("s_tr"))]
                        p.on("vector", lambda e, ptb=ptb, sc=sc, tt=tt, e_i=e_i:
                             e.tensor_single_scalar(
                                 ptb[:, ts(sc, 128)], iota_sb[:, ts(sc, 128)],
                                 pos[:, tt * 8 + e_i: tt * 8 + e_i + 1], ALU.is_equal),
                             wait_exact=wex, inc=([("g_pt", 1)] if sc == 5 else None))
                    for sc in range(6):
                        wex = []
                        if sc == 0:
                            wex = [("g_pt", p.cnt("g_pt")), ("d_xt", xtmark)]
                            if tt == 0 and dc == 0:
                                wex += [("my_ev", p.cnt("my_ev")),
                                        ("mh_ev", p.cnt("mh_ev")),
                                        ("mh_sil", p.cnt("mh_sil"))]
                            if tt == 0 and dc == 1:
                                wex += [("g_ev", p.cnt("g_ev"))]
                        p.on("tensor", lambda e, ptb=ptb, sc=sc, tt=tt, dc=dc, xt=xt:
                             e.matmul(PB[sc][:, 0:512], ptb[:, ts(sc, 128)], xt[:],
                                      start=(tt == 0), stop=(tt == NT - 1)),
                             wait_exact=wex,
                             inc=([("gstep", 1)] if sc == 5 else None))
                evmark = {}
                for sc in range(7):
                    if sc < 6:
                        g = gth_sb[sc % 2]
                        jl = dc * 6 + sc
                        wex = [("gstep", p.cnt("gstep"))]
                        if jl >= 2:
                            wex += [("g_gf", p.cnt("g_gf"))]
                        p.on("vector", lambda e, g=g, sc=sc: e.tensor_copy(
                            g[:], PB[sc][:, 0:512]), wait_exact=wex, inc=[("g_ev", 1)])
                        evmark[sc] = p.cnt("g_ev")
                    if sc >= 1:
                        sp = sc - 1
                        g = gth_sb[sp % 2]
                        jl = dc * 6 + sp
                        for ds in range(4):
                            k4 = jl * 4 + ds
                            pbt = PB[6 + k4 % 2]
                            wex = [("g_ev", evmark[sp])]
                            if jl == 0 and ds == 0:
                                wex += [("dq_ev", p.cnt("dq_ev")),
                                        ("mh_ev", p.cnt("mh_ev")),
                                        ("mh_sil", p.cnt("mh_sil")),
                                        ("g_gf", p.cnt("g_gf"))]
                            elif k4 >= 2:
                                wex += [("g_gf", p.cnt("g_gf") - 1)]
                            p.on("tensor", lambda e, pbt=pbt, g=g, ds=ds: e.matmul(
                                pbt[:, 0:128], g[:, ts(ds, 128)], ident[:],
                                start=True, stop=True),
                                wait_exact=wex, inc=[("g_tr", 1)])
                            kc = dc * 4 + ds
                            p.on("scalar", lambda e, pbt=pbt, kc=kc, sp=sp: e.activation(
                                out=gath_fm[:, kc * CAP + sp * 128: kc * CAP + sp * 128 + 128],
                                in_=pbt[:, 0:128], func=AF.Identity),
                                wait_exact=[("g_tr", p.cnt("g_tr"))], inc=[("g_gf", 1)])

            gplan = [3, 3, 3, 3, 3, 3, 3, 1]
            g0 = 0
            hb = h_sb
            for g, ghc in enumerate(gplan):
                q = state["gi"] % 2
                in1 = dram["w1"].ap()[e_i].rearrange(
                    "(kc p) h -> p kc h", p=128)[:, :, g0 * 128:(g0 + ghc) * 128]
                in3 = dram["w3"].ap()[e_i].rearrange(
                    "(kc p) h -> p kc h", p=128)[:, :, g0 * 128:(g0 + ghc) * 128]
                wex = [("mh1_mm", fh[0][0]), ("mh3_mm", fh[0][1])]
                p.on("gpsimd", lambda e, q=q, ghc=ghc, in1=in1: e.dma_start(
                    out=w1g[q].ap().rearrange("p (kc h) -> p kc h", kc=ND)[:, :, 0:ghc * 128],
                    in_=in1), wait_exact=wex, inc=[("d_wg", 16)])
                p.on("gpsimd", lambda e, q=q, ghc=ghc, in3=in3: e.dma_start(
                    out=w3g[q].ap().rearrange("p (kc h) -> p kc h", kc=ND)[:, :, 0:ghc * 128],
                    in_=in3), inc=[("d_wg", 16)])
                if g % 2 == 0:
                    yhc = ghc + (gplan[g + 1] if g + 1 < len(gplan) else 0)
                    in2 = dram["w2"].ap()[e_i].rearrange(
                        "(hc p) o -> p hc o", p=128)[:, g0:g0 + yhc, :]
                    p.on("gpsimd", lambda e, yhc=yhc, in2=in2: e.dma_start(
                        out=w2g.ap().rearrange("p (hc o) -> p hc o", o=D)[:, 0:yhc, :],
                        in_=in2), wait_exact=[("my_mm", state["wy"])],
                        inc=[("d_wg", 16)])
                dma_tgt = p.cnt("d_wg")
                for tg in range(2):
                    for hcl in range(ghc):
                        hcp = (g % 2) * 3 + hcl
                        hsq = state["hseq"]
                        state["hseq"] += 1
                        pb1 = PB[4 + 2 * (hsq % 2)]
                        pb3 = PB[5 + 2 * (hsq % 2)]
                        for kc in range(ND):
                            wex = [("d_wg", dma_tgt)] if (tg == 0 and hcl == 0 and kc == 0) else []
                            if kc == 0:
                                wex += [("mh_sil", max(0, p.cnt("mh_sil") - 1))]
                                if hsq % 2 == 1:
                                    wex += [("dq_ev", p.cnt("dq_ev")),
                                            ("g_gf", p.cnt("g_gf"))]
                            p.on("tensor", lambda e, pb1=pb1, q=q, kc=kc, hcl=hcl, tg=tg:
                                 e.matmul(pb1[:, 0:384],
                                          w1g[q][:, kc * 384 + hcl * 128: kc * 384 + (hcl + 1) * 128],
                                          gath_fm[:, kc * CAP + tg * 384: kc * CAP + (tg + 1) * 384],
                                          start=(kc == 0), stop=(kc == ND - 1)),
                                 wait_exact=wex,
                                 inc=([("mh1_mm", 1)] if kc == ND - 1 else None))
                        for kc in range(ND):
                            wex = []
                            if kc == 0:
                                wex = [("mh_ev", max(0, p.cnt("mh_ev") - 1))]
                                if hsq % 2 == 1:
                                    wex += [("dq_ev", p.cnt("dq_ev")),
                                            ("g_gf", p.cnt("g_gf"))]
                            p.on("tensor", lambda e, pb3=pb3, q=q, kc=kc, hcl=hcl, tg=tg:
                                 e.matmul(pb3[:, 0:384],
                                          w3g[q][:, kc * 384 + hcl * 128: kc * 384 + (hcl + 1) * 128],
                                          gath_fm[:, kc * CAP + tg * 384: kc * CAP + (tg + 1) * 384],
                                          start=(kc == 0), stop=(kc == ND - 1)),
                                 wait_exact=wex,
                                 inc=([("mh3_mm", 1)] if kc == ND - 1 else None))
                        hs = hsil[hsq % 2]
                        p.on("scalar", lambda e, pb1=pb1, hs=hs: e.activation(
                            out=hs[:, 0:384], in_=pb1[:, 0:384], func=AF.Silu),
                            wait_exact=[("mh1_mm", p.cnt("mh1_mm")),
                                        ("mh_ev", max(0, p.cnt("mh_ev") - 1))],
                            inc=[("mh_sil", 1)])
                        wex = [("mh3_mm", p.cnt("mh3_mm")),
                               ("mh_sil", p.cnt("mh_sil"))]
                        if g % 2 == 0 and tg == 0 and hcl == 0:
                            wex.append(("my_mm", p.cnt("my_mm")))
                        p.on("vector", lambda e, pb3=pb3, hs=hs, hcp=hcp, tg=tg: e.tensor_mul(
                            hb[:, hcp * CAP + tg * 384: hcp * CAP + (tg + 1) * 384],
                            hs[:, 0:384], pb3[:, 0:384]),
                            wait_exact=wex, inc=[("mh_ev", 1)])
                fh = [fh[1], (p.cnt("mh1_mm"), p.cnt("mh3_mm"))]
                g0 += ghc
                state["gi"] += 1
                if g % 2 == 1:
                    drain()
                if g % 2 == 1:
                    yp = g // 2
                    yhc = gplan[g - 1] + gplan[g]
                    ysnap = p.cnt("my_ev")
                    ymark = {}
                    for oc in range(2):
                        for m in range(6):
                            k = oc * 6 + m
                            pby = PB[k % 4]
                            for hcp in range(yhc):
                                wex = []
                                if hcp == 0:
                                    if k == 0:
                                        wex = [("mh_ev", p.cnt("mh_ev")),
                                               ("my_ev", ysnap),
                                               ("g_ev", p.cnt("g_ev"))]
                                    elif k >= 4:
                                        wex = [("my_ev", ymark[k - 4])]
                                    else:
                                        wex = [("my_ev", ysnap)]
                                p.on("tensor", lambda e, pby=pby, hcp=hcp, m=m, oc=oc, yhc=yhc:
                                     e.matmul(pby[:, 0:512],
                                              hb[:, hcp * CAP + m * 128: hcp * CAP + (m + 1) * 128],
                                              w2g[:, hcp * D + oc * 512: hcp * D + oc * 512 + 512],
                                              start=(hcp == 0), stop=(hcp == yhc - 1)),
                                     wait_exact=wex,
                                     inc=([("my_mm", 1)] if hcp == yhc - 1 else None))
                            lo = m * D + oc * 512
                            if yp == 0:
                                p.on("vector", lambda e, pby=pby, lo=lo: e.tensor_copy(
                                    y_sb[:, lo:lo + 512], pby[:, 0:512]),
                                    wait_exact=[("my_mm", p.cnt("my_mm"))],
                                    inc=[("my_ev", 1)])
                            elif yp < 3:
                                p.on("vector", lambda e, pby=pby, lo=lo: e.tensor_add(
                                    y_sb[:, lo:lo + 512], y_sb[:, lo:lo + 512],
                                    pby[:, 0:512]),
                                    wait_exact=[("my_mm", p.cnt("my_mm"))],
                                    inc=[("my_ev", 1)])
                            else:
                                p.on("vector", lambda e, pby=pby, lo=lo: e.tensor_add(
                                    yb[:, lo:lo + 512], y_sb[:, lo:lo + 512],
                                    pby[:, 0:512]),
                                    wait_exact=[("my_mm", p.cnt("my_mm"))],
                                    inc=[("my_ev", 1)])
                            ymark[k] = p.cnt("my_ev")
                    state["wy"] = p.cnt("my_mm")

            ssnap = p.cnt("my_ev")
            scmark = {}
            smmark = {}
            strmark = {}
            temark = {}
            for tt in range(NT + 1):
                if tt < NT:
                    ptb = ptT_sb[tt % 2]
                    srcb = pt_sb[tt % 2]
                    for sc in range(6):
                        wex = []
                        if sc == 0:
                            if tt >= 2:
                                wex = [("s_tr", strmark[tt - 2])]
                            else:
                                wex = [("gstep", p.cnt("gstep"))]
                        p.on("vector", lambda e, srcb=srcb, sc=sc, tt=tt, e_i=e_i:
                             e.tensor_single_scalar(
                                 srcb[:, ts(sc, 128)], iota_sb[:, ts(sc, 128)],
                                 pos[:, tt * 8 + e_i: tt * 8 + e_i + 1], ALU.is_equal),
                             wait_exact=wex, inc=([("s_pt", 1)] if sc == 5 else None))
                    for sc in range(6):
                        k6 = tt * 6 + sc
                        pbt = PB[6 + k6 % 2]
                        wex = [("s_pt", p.cnt("s_pt"))] if sc == 0 else []
                        if tt == 0 and sc == 0:
                            wex += [("dq_ev", p.cnt("dq_ev")),
                                    ("mh_ev", p.cnt("mh_ev")),
                                    ("mh_sil", p.cnt("mh_sil")),
                                    ("g_gf", p.cnt("g_gf"))]
                        elif k6 >= 2:
                            wex += [("s_te", p.cnt("s_te") - 1)]
                        p.on("tensor", lambda e, pbt=pbt, srcb=srcb, sc=sc: e.matmul(
                            pbt[:, 0:128], srcb[:, ts(sc, 128)], ident[:],
                            start=True, stop=True),
                            wait_exact=wex, inc=[("s_tr", 1)])
                        wex = [("s_tr", p.cnt("s_tr"))]
                        if sc == 0 and tt >= 2:
                            wex += [("s_mm", smmark[tt - 2])]
                        p.on("scalar", lambda e, pbt=pbt, ptb=ptb, sc=sc: e.activation(
                            out=ptb[:, ts(sc, 128)], in_=pbt[:, 0:128], func=AF.Identity),
                            wait_exact=wex, inc=[("s_te", 1)])
                    strmark[tt] = p.cnt("s_tr")
                    temark[tt] = p.cnt("s_te")
                if tt >= 1:
                    tp = tt - 1
                    ptb = ptT_sb[tp % 2]
                    for dc in range(2):
                        k2 = tp * 2 + dc
                        pbs = PB[k2 % 4]
                        for sc in range(6):
                            wex = []
                            if sc == 0:
                                wex = [("s_te", temark[tp])]
                                if k2 < 4:
                                    wex += [("my_ev", ssnap)]
                                else:
                                    wex += [("my_ev", scmark[k2 - 4])]
                            p.on("tensor", lambda e, pbs=pbs, ptb=ptb, sc=sc, dc=dc:
                                 e.matmul(pbs[:, 0:512], ptb[:, ts(sc, 128)],
                                          yb[:, sc * D + dc * 512: sc * D + dc * 512 + 512],
                                          start=(sc == 0), stop=(sc == 5)),
                                 wait_exact=wex,
                                 inc=([("s_mm", 1)] if sc == 5 else None))
                        p.on("vector", lambda e, pbs=pbs, tp=tp, dc=dc, e_i=e_i:
                             e.scalar_tensor_tensor(
                                 x_res[:, tp * D + dc * 512: tp * D + dc * 512 + 512],
                                 pbs[:, 0:512],
                                 comb[:, tp * 8 + e_i: tp * 8 + e_i + 1],
                                 x_res[:, tp * D + dc * 512: tp * D + dc * 512 + 512],
                                 ALU.mult, ALU.add),
                             wait_exact=[("s_mm", p.cnt("s_mm"))], inc=[("sc_ev", 1)])
                        scmark[k2] = p.cnt("my_ev")
                    smmark[tp] = p.cnt("s_mm")

        g0 = 0
        for gl, ghc in enumerate([3, 3, 3, 3]):
            q = state["gi"] % 2
            in1 = dram["sw1"].ap().rearrange(
                "(kc p) h -> p kc h", p=128)[:, :, g0 * 128:(g0 + ghc) * 128]
            in3 = dram["sw3"].ap().rearrange(
                "(kc p) h -> p kc h", p=128)[:, :, g0 * 128:(g0 + ghc) * 128]
            in2 = dram["sw2"].ap().rearrange(
                "(hc p) o -> p hc o", p=128)[:, g0:g0 + ghc, :]
            wex = [("mh1_mm", fh[0][0]), ("mh3_mm", fh[0][1])]
            p.on("gpsimd", lambda e, q=q, ghc=ghc, in1=in1: e.dma_start(
                out=w1g[q].ap().rearrange("p (kc h) -> p kc h", kc=ND)[:, :, 0:ghc * 128],
                in_=in1), wait_exact=wex, inc=[("d_wg", 16)])
            p.on("gpsimd", lambda e, q=q, ghc=ghc, in3=in3: e.dma_start(
                out=w3g[q].ap().rearrange("p (kc h) -> p kc h", kc=ND)[:, :, 0:ghc * 128],
                in_=in3), inc=[("d_wg", 16)])
            p.on("gpsimd", lambda e, ghc=ghc, in2=in2: e.dma_start(
                out=w2g.ap().rearrange("p (hc o) -> p hc o", o=D)[:, 0:ghc, :],
                in_=in2), wait_exact=[("my_mm", state["wy"])], inc=[("d_wg", 16)])
            dma_tgt = p.cnt("d_wg")
            for tg in range(4):
                for hcl in range(ghc):
                    hsq = state["hseq"]
                    state["hseq"] += 1
                    pb1 = PB[4 + 2 * (hsq % 2)]
                    pb3 = PB[5 + 2 * (hsq % 2)]
                    for kc in range(ND):
                        wex = [("d_wg", dma_tgt)] if (tg == 0 and hcl == 0 and kc == 0) else []
                        if kc == 0:
                            wex += [("mh_sil", max(0, p.cnt("mh_sil") - 1))]
                            if hsq % 2 == 1:
                                wex += [("dq_ev", p.cnt("dq_ev")),
                                        ("g_gf", p.cnt("g_gf"))]
                        p.on("tensor", lambda e, pb1=pb1, q=q, kc=kc, hcl=hcl, tg=tg:
                             e.matmul(pb1[:, 0:512],
                                      w1g[q][:, kc * 384 + hcl * 128: kc * 384 + (hcl + 1) * 128],
                                      xn_fm[:, kc * S + tg * 512: kc * S + (tg + 1) * 512],
                                      start=(kc == 0), stop=(kc == ND - 1)),
                             wait_exact=wex,
                             inc=([("mh1_mm", 1)] if kc == ND - 1 else None))
                    for kc in range(ND):
                        wex = []
                        if kc == 0:
                            wex = [("mh_ev", max(0, p.cnt("mh_ev") - 1))]
                            if hsq % 2 == 1:
                                wex += [("dq_ev", p.cnt("dq_ev")),
                                        ("g_gf", p.cnt("g_gf"))]
                        p.on("tensor", lambda e, pb3=pb3, q=q, kc=kc, hcl=hcl, tg=tg:
                             e.matmul(pb3[:, 0:512],
                                      w3g[q][:, kc * 384 + hcl * 128: kc * 384 + (hcl + 1) * 128],
                                      xn_fm[:, kc * S + tg * 512: kc * S + (tg + 1) * 512],
                                      start=(kc == 0), stop=(kc == ND - 1)),
                             wait_exact=wex,
                             inc=([("mh3_mm", 1)] if kc == ND - 1 else None))
                    hs = hsil[hsq % 2]
                    p.on("scalar", lambda e, pb1=pb1, hs=hs: e.activation(
                        out=hs[:], in_=pb1[:, 0:512], func=AF.Silu),
                        wait_exact=[("mh1_mm", p.cnt("mh1_mm")),
                                    ("mh_ev", max(0, p.cnt("mh_ev") - 1))],
                        inc=[("mh_sil", 1)])
                    wex = [("mh3_mm", p.cnt("mh3_mm")),
                           ("mh_sil", p.cnt("mh_sil"))]
                    if tg >= 1 and hcl == 0:
                        wex.append(("my_mm", p.cnt("my_mm")))
                    if gl == 0 and tg == 0 and hcl == 0:
                        wex.append(("my_mm", p.cnt("my_mm")))
                    p.on("vector", lambda e, pb3=pb3, hs=hs, hcl=hcl: e.tensor_mul(
                        hb[:, hcl * 512:(hcl + 1) * 512], hs[:], pb3[:, 0:512]),
                        wait_exact=wex, inc=[("mh_ev", 1)])
                for oc in range(2):
                    for hcl in range(ghc):
                        for m4 in range(4):
                            pby = PB[m4]
                            wex = []
                            if hcl == 0 and m4 == 0:
                                wex = [("mh_ev", p.cnt("mh_ev")),
                                       ("my_ev", p.cnt("my_ev"))]
                            p.on("tensor", lambda e, pby=pby, hcl=hcl, m4=m4, oc=oc, ghc=ghc:
                                 e.matmul(pby[:, 0:512],
                                          hb[:, hcl * 512 + m4 * 128: hcl * 512 + (m4 + 1) * 128],
                                          w2g[:, hcl * D + oc * 512: hcl * D + (oc + 1) * 512],
                                          start=(hcl == 0), stop=(hcl == ghc - 1)),
                                 wait_exact=wex,
                                 inc=([("my_mm", 1)] if hcl == ghc - 1 else None))
                    for m4 in range(4):
                        tt4 = tg * 4 + m4
                        lo = tt4 * D + oc * 512
                        p.on("vector", lambda e, m4=m4, lo=lo: e.scalar_tensor_tensor(
                            x_res[:, lo:lo + 512], PB[m4][:, 0:512], 1.0,
                            x_res[:, lo:lo + 512], ALU.mult, ALU.add),
                            wait_exact=[("my_mm", p.cnt("my_mm"))],
                            inc=[("my_ev", 1)])
            fh = [fh[1], (p.cnt("mh1_mm"), p.cnt("mh3_mm"))]
            state["wy"] = p.cnt("my_mm")
            g0 += ghc
            state["gi"] += 1
            drain()

    def emit_dequant(st):
        wq_stage = st["wq_stage"]
        dq_s = st["dq_s"]
        dq_si = st["dq_si"]
        dq_t = st["dq_t"]
        dq_i = st["dq_i"]
        dq_w = st["dq_w"]
        tr_stage = st["tr_stage"]

        for wname, wsrc, O, I in (("up", "w_up", L1_HID, D),
                                  ("gate", "w_gate", L1_HID, D),
                                  ("down", "w_down", D, L1_HID)):
            noc = O // 128
            nh = max(1, I // 512)
            IH = I // nh
            nb = IH // 32
            nic = IH // 128
            for oc in range(noc):
                for ih in range(nh):
                    p.on("sync", lambda e, wsrc=wsrc, oc=oc, ih=ih, IH=IH: e.dma_start(
                        out=wq_stage[:, 0:IH],
                        in_=dram[wsrc].ap().rearrange("(c p) i -> p c i", p=128)
                            [:, oc, ih * IH:(ih + 1) * IH]),
                        wait_exact=[("dq_done", max(0, p.cnt("dq_done") - 1))],
                        inc=[("d_dq", 16)])
                    w3d = wq_stage.ap().rearrange("p (b k) -> p b k", k=32)[:, 0:nb, :]
                    p.on("vector", lambda e, w3d=w3d, nb=nb: e.reduce_max(
                        dq_s.ap().rearrange("p (b k) -> p b k", k=1)[:, 0:nb, :], w3d,
                        axis=AXX, apply_absolute_value=True),
                        wait_exact=[("d_dq", p.cnt("d_dq"))])
                    p.vf()
                    p.on("vector", lambda e, nb=nb: e.tensor_scalar(
                        dq_s[:, 0:nb], dq_s[:, 0:nb], 1.0 / 7.0, 1e-12, ALU.mult, ALU.max))
                    p.vf()
                    p.on("vector", lambda e, nb=nb: e.reciprocal(dq_si[:, 0:nb], dq_s[:, 0:nb]),
                         inc=[("rcp", 1)])
                    p.wait("vector", ["rcp"])
                    p.on("vector", lambda e, w3d=w3d, nb=nb: e.tensor_mul(
                        dq_t.ap().rearrange("p (b k) -> p b k", k=32)[:, 0:nb, :], w3d,
                        dq_si.ap().rearrange("p (b k) -> p b k", k=1)[:, 0:nb, :]
                        .broadcast_to([128, nb, 32])))
                    p.vf()
                    p.on("vector", lambda e, IH=IH: e.tensor_scalar_add(
                        dq_i[:, 0:IH], dq_t[:, 0:IH], 8.5))
                    p.vf()
                    p.on("vector", lambda e, nb=nb: e.scalar_tensor_tensor(
                        dq_w.ap().rearrange("p (b k) -> p b k", k=32)[:, 0:nb, :],
                        dq_i.ap().rearrange("p (b k) -> p b k", k=32)[:, 0:nb, :],
                        -8.0,
                        dq_s.ap().rearrange("p (b k) -> p b k", k=1)[:, 0:nb, :]
                        .broadcast_to([128, nb, 32]),
                        ALU.add, ALU.mult), inc=[("dq_w", 1)])
                    for ic in range(nic):
                        k = p.cnt("dq_tr")
                        pb = PB[6 + k % 2]
                        we = [("dq_ev", max(0, p.cnt("dq_ev") - 1))]
                        if ic == 0:
                            we += [("mh_ev", p.cnt("mh_ev")),
                                   ("mh_sil", p.cnt("mh_sil")),
                                   ("g_gf", p.cnt("g_gf"))]
                        p.on("tensor", lambda e, pb=pb, ic=ic: e.matmul(
                            pb[:, 0:128], dq_w[:, ts(ic, 128)], ident[:],
                            start=True, stop=True),
                            wait_exact=[("dq_w", p.cnt("dq_w"))] + we,
                            inc=[("dq_tr", 1)])
                        p.on("scalar", lambda e, pb=pb, ic=ic: e.activation(
                            out=tr_stage[:, ts(ic, 128)], in_=pb[:, 0:128],
                            func=AF.Identity),
                            wait_exact=[("dq_tr", p.cnt("dq_tr"))], inc=[("dq_ev", 1)])
                    p.on("sync", lambda e, wname=wname, oc=oc, ih=ih, nic=nic: e.dma_start(
                        out=scr[wname].ap()[ih * nic:(ih + 1) * nic, :,
                                            oc * 128:(oc + 1) * 128]
                            .rearrange("c p o -> p c o"),
                        in_=tr_stage.ap().rearrange("p (c o) -> p c o", o=128)[:, 0:nic, :]),
                        wait_exact=[("dq_ev", p.cnt("dq_ev"))], inc=[("d_dqo", 16)])
                    p.on("sync", None, wait_exact=[("d_dqo", p.cnt("d_dqo"))])
                    p.on("sync", lambda e: e.nop(), inc=[("dq_done", 1)])
                    yield

    def emit_attn_l1(st):
        wl1 = st["wl1"]
        fmA = st["fmA"]
        fmB = st["fmB"]
        fres = st["fres"]
        fims = st["fims"]
        sre = st["sre"]
        sim = st["sim"]
        tA = st["tA"]; tB = st["tB"]; u1 = st["u1"]; u2 = st["u2"]
        u3 = st["u3"]; u4 = st["u4"]
        ab_a = st["ab_a"]
        ab_b = st["ab_b"]
        bl1_sb = st["bl1_sb"]

        FW = [128, 128, 128, 128, 1]

        for wi, b in enumerate(("bq", "bk", "bv")):
            p.on("sync", lambda e, wi=wi, b=b: e.dma_start(
                out=bl1_sb[:, wi * ND:(wi + 1) * ND],
                in_=dram["L1_" + b].ap().rearrange("(c p) -> p c", p=128),
                allow_slow_non_contiguous=True), inc=[("d_l1b", 16)])
        p.on("sync", lambda e: e.dma_start(
            out=fres.ap().rearrange("p (c f) -> p c f", f=NF),
            in_=dram["fre"].ap().rearrange("(c p) f -> p c f", p=128)),
            inc=[("d_l1f", 16)])
        p.on("sync", lambda e: e.dma_start(
            out=fims.ap().rearrange("p (c f) -> p c f", f=NF),
            in_=dram["fim"].ap().rearrange("(c p) f -> p c f", p=128)),
            inc=[("d_l1f", 16)])
        f_mark = p.cnt("d_l1f")
        p.on("vector", lambda e: e.memset(sre[:], 0.0), inc=[("l1s_z", 1)])
        p.on("vector", lambda e: e.memset(sim[:], 0.0), inc=[("l1s_z", 1)])

        def project(widx, wname, dst, bias_col, semtag):
            w_marks = []
            for half in range(2):
                p.on("gpsimd", lambda e, wname=wname, half=half: e.dma_start(
                    out=wl1.ap().rearrange("p (c o) -> p c o", o=512),
                    in_=dram[wname].ap().rearrange("(c p) o -> p c o", p=128)
                        [:, :, half * 512:(half + 1) * 512]),
                    wait_exact=([("l1w_free", p.cnt("l1w_free"))] if half == 0
                                else [(semtag + "_mm", p.cnt(semtag + "_mm") + 16)]),
                    inc=[("d_l1wg", 16)])
                w_marks.append(p.cnt("d_l1wg"))
            for oc in range(ND):
                for tc in range(4):
                    pbp = PB[6 + (oc * 4 + tc) % 2]
                    wex = [("d_l1wg", w_marks[oc // 4])] if tc == 0 else []
                    k = p.cnt(semtag + "_mm")
                    if k >= 2:
                        wex += [(semtag + "_ev", k - 1)]
                    for kc in range(ND):
                        p.on("tensor", lambda e, pbp=pbp, kc=kc, oc=oc, tc=tc: e.matmul(
                            pbp[:], wl1[:, kc * 512 + (oc % 4) * 128: kc * 512 + (oc % 4 + 1) * 128],
                            xn_fm[:, kc * S + tc * 512: kc * S + (tc + 1) * 512],
                            start=(kc == 0), stop=(kc == ND - 1)),
                            wait_exact=(wex if kc == 0 else []),
                            inc=([(semtag + "_mm", 1)] if kc == ND - 1 else None))
                    p.on("scalar", lambda e, pbp=pbp, dst=dst, oc=oc, tc=tc, bias_col=bias_col: e.activation(
                        out=dst[:, oc * S + tc * 512: oc * S + (tc + 1) * 512],
                        in_=pbp[:], func=AF.Identity,
                        bias=bl1_sb[:, bias_col + oc: bias_col + oc + 1]),
                        wait=(["d_l1b"] if oc == 0 and tc == 0 else None),
                        wait_exact=[(semtag + "_mm", p.cnt(semtag + "_mm"))],
                        inc=[(semtag + "_ev", 1)])
            p.on("sync", lambda e: e.nop(),
                 wait_exact=[(semtag + "_ev", p.cnt(semtag + "_ev"))],
                 inc=[("l1w_free", 1)])

        project(0, "L1_Wk", fmA, 1 * ND, "l1k")
        project(1, "L1_Wv", fmB, 2 * ND, "l1v")

        f0 = 0
        fcnt = 0
        for fc in range(5):
            fw = FW[fc]
            for tc in range(4):
                par = 4 * (fcnt % 2)
                fcnt += 1
                wex0 = [("l1k_ev", 32), ("l1v_ev", 32), ("l1s_z", p.cnt("l1s_z")),
                        ("d_l1f", f_mark)] if (fc == 0 and tc == 0) else \
                       [("l1h_v", max(0, p.cnt("l1h_v") - 1))]
                for ci, (pb, fsb, xin) in enumerate((
                        (PB[par + 0], fres, fmA), (PB[par + 1], fims, fmA),
                        (PB[par + 2], fres, fmB), (PB[par + 3], fims, fmB))):
                    for kc in range(ND):
                        p.on("tensor", lambda e, pb=pb, fsb=fsb, xin=xin, kc=kc, fc=fc, fw=fw, f0=f0, tc=tc: e.matmul(
                            pb[0:fw, :], fsb[:, kc * NF + f0: kc * NF + f0 + fw],
                            xin[:, kc * S + tc * 512: kc * S + (tc + 1) * 512],
                            start=(kc == 0), stop=(kc == ND - 1)),
                            wait_exact=(wex0 if (ci == 0 and kc == 0) else []),
                            inc=([("l1h_mm", 1)] if (ci == 3 and kc == ND - 1) else None))
                mmt = p.cnt("l1h_mm")
                fwsl = slice(0, fw)
                p.on("scalar", lambda e, fwsl=fwsl, par=par: e.activation(
                    out=tA[fwsl, :], in_=PB[par + 0][fwsl, :], func=AF.Identity),
                    wait_exact=[("l1h_mm", mmt), ("l1h_v", p.cnt("l1h_v"))],
                    inc=[("l1h_s", 1)])
                p.on("scalar", lambda e, fwsl=fwsl, par=par: e.activation(
                    out=tB[fwsl, :], in_=PB[par + 1][fwsl, :], func=AF.Identity),
                    inc=[("l1h_s", 1)])
                p.on("vector", lambda e, fwsl=fwsl, par=par: e.tensor_mul(
                    u1[fwsl, :], tA[fwsl, :], PB[par + 2][fwsl, :]),
                    wait_exact=[("l1h_s", p.cnt("l1h_s"))])
                p.on("vector", lambda e, fwsl=fwsl, par=par: e.tensor_mul(
                    u2[fwsl, :], tB[fwsl, :], PB[par + 3][fwsl, :]))
                p.on("vector", lambda e, fwsl=fwsl, par=par: e.tensor_mul(
                    u3[fwsl, :], tA[fwsl, :], PB[par + 3][fwsl, :]))
                p.on("vector", lambda e, fwsl=fwsl, par=par: e.tensor_mul(
                    u4[fwsl, :], tB[fwsl, :], PB[par + 2][fwsl, :]))
                p.on("vector", lambda e, fwsl=fwsl: e.tensor_sub(
                    u1[fwsl, :], u1[fwsl, :], u2[fwsl, :]))
                p.on("vector", lambda e, fwsl=fwsl: e.tensor_add(
                    u3[fwsl, :], u3[fwsl, :], u4[fwsl, :]))
                p.on("vector", lambda e, fwsl=fwsl: e.reduce_sum(
                    u2[fwsl, 0:1], u1[fwsl, :], axis=AXX))
                p.on("vector", lambda e, fwsl=fwsl: e.reduce_sum(
                    u4[fwsl, 0:1], u3[fwsl, :], axis=AXX))
                p.on("vector", lambda e, fwsl=fwsl, fc=fc: e.tensor_add(
                    sre[fwsl, fc:fc + 1], sre[fwsl, fc:fc + 1], u2[fwsl, 0:1]))
                p.on("vector", lambda e, fwsl=fwsl, fc=fc: e.tensor_add(
                    sim[fwsl, fc:fc + 1], sim[fwsl, fc:fc + 1], u4[fwsl, 0:1]),
                    inc=[("l1h_v", 1)])
            f0 += fw

        for gi, gnm in ((0, "gre"), (1, "gim")):
            p.on("sync", lambda e, gi=gi, gnm=gnm: e.dma_start(
                out=fmB.ap().rearrange("p (c o) -> p c o", o=D)[:, gi * 5:gi * 5 + 4, :],
                in_=dram[gnm].ap()[0:512, :].rearrange("(c p) o -> p c o", p=128)),
                wait_exact=([("l1h_v", 20)] if gi == 0 else []),
                inc=[("d_l1g", 16)])
            p.on("sync", lambda e, gi=gi, gnm=gnm: e.dma_start(
                out=fmB[0:1, (gi * 5 + 4) * D:(gi * 5 + 5) * D],
                in_=dram[gnm].ap()[512:513, :]),
                inc=[("d_l1g", 16)])
        g_mark = p.cnt("d_l1g")
        p.on("sync", lambda e: e.nop(), wait_exact=[("l1h_v", 20)],
             inc=[("l1w_free", 1)])
        project(2, "L1_Wq", fmA, 0 * ND, "l1q")

        qstep = 0
        for tc in range(4):
            f0 = 0
            for fc in range(5):
                fw = FW[fc]
                fwsl = slice(0, fw)
                wex0 = [("l1q_ev", 32)] if (tc == 0 and fc == 0) else \
                       [("l1a_v", max(0, p.cnt("l1a_v") - 1))]
                qpar = 2 * (qstep % 2)
                qstep += 1
                for ci, (pb, fsb) in enumerate(((PB[qpar + 0], fres),
                                                (PB[qpar + 1], fims))):
                    for kc in range(ND):
                        p.on("tensor", lambda e, pb=pb, fsb=fsb, kc=kc, fw=fw, f0=f0, tc=tc: e.matmul(
                            pb[0:fw, :], fsb[:, kc * NF + f0: kc * NF + f0 + fw],
                            fmA[:, kc * S + tc * 512: kc * S + (tc + 1) * 512],
                            start=(kc == 0), stop=(kc == ND - 1)),
                            wait_exact=(wex0 if (ci == 0 and kc == 0) else []),
                            inc=([("l1a_mm", 1)] if (ci == 1 and kc == ND - 1) else None))
                mmt = p.cnt("l1a_mm")
                p.on("vector", lambda e, fwsl=fwsl, fc=fc, qpar=qpar: e.tensor_scalar_mul(
                    u1[fwsl, :], PB[qpar + 1][fwsl, :], sim[fwsl, fc:fc + 1]),
                    wait_exact=[("l1a_mm", mmt)])
                p.on("vector", lambda e, fwsl=fwsl, fc=fc, qpar=qpar: e.tensor_scalar_mul(
                    u2[fwsl, :], PB[qpar + 1][fwsl, :], sre[fwsl, fc:fc + 1]))
                p.on("vector", lambda e, fwsl=fwsl, fc=fc, qpar=qpar: e.scalar_tensor_tensor(
                    ab_a[fwsl, fc * 512:(fc + 1) * 512], PB[qpar + 0][fwsl, :],
                    sre[fwsl, fc:fc + 1], u1[fwsl, :], ALU.mult, ALU.add))
                p.on("vector", lambda e, fwsl=fwsl, fc=fc, qpar=qpar: e.scalar_tensor_tensor(
                    ab_b[fwsl, fc * 512:(fc + 1) * 512], PB[qpar + 0][fwsl, :],
                    sim[fwsl, fc:fc + 1], u2[fwsl, :], ALU.mult, ALU.subtract),
                    inc=[("l1a_v", 1)])
                f0 += fw
            for m4 in range(4):
                tt = tc * 4 + m4
                for oc in range(2):
                    j = p.cnt("l1o_mm")
                    pbo = PB[4 + (tt * 2 + oc) % 2]
                    first = True
                    f0 = 0
                    for fc in range(5):
                        fw = FW[fc]
                        for gi in range(2):
                            ab = ab_a if gi == 0 else ab_b
                            goff = gi * 5 * D
                            wex = []
                            if first:
                                wex = [("l1a_v", p.cnt("l1a_v")),
                                       ("l1o_ev", max(0, p.cnt("l1o_ev") - 1))]
                                if tc == 0 and tt == 0 and oc == 0:
                                    wex += [("d_l1g", g_mark)]
                            p.on("tensor", lambda e, pbo=pbo, ab=ab, fc=fc, fw=fw, m4=m4, goff=goff, oc=oc, first=first, gi=gi: e.matmul(
                                pbo[:], ab[0:fw, fc * 512 + m4 * 128: fc * 512 + (m4 + 1) * 128],
                                fmB[:, goff + fc * D + oc * 512: goff + fc * D + (oc + 1) * 512]
                                [0:fw, :] if False else
                                fmB[0:fw, goff + fc * D + oc * 512: goff + fc * D + (oc + 1) * 512],
                                start=first, stop=(fc == 4 and gi == 1)),
                                wait_exact=wex,
                                inc=([("l1o_mm", 1)] if (fc == 4 and gi == 1) else None))
                            first = False
                    p.on("vector", lambda e, pbo=pbo, tt=tt, oc=oc: e.scalar_tensor_tensor(
                        x_res[:, tt * D + oc * 512: tt * D + (oc + 1) * 512],
                        pbo[:], 1.0,
                        x_res[:, tt * D + oc * 512: tt * D + (oc + 1) * 512],
                        ALU.mult, ALU.add),
                        wait_exact=[("l1o_mm", p.cnt("l1o_mm"))],
                        inc=[("l1o_ev", 1)])

    def emit_ffn_l1(st):
        gate_sb = st["gate_sb"]
        whg = st["whg"]
        wdn = st["wdn"]
        i32t = st["i32t"]
        sgt = st["sgt"]
        i32u = st["i32u"]
        i32o = i32u
        amx = st["amx"]
        parb = st["parb"]
        paro = st["paro"]
        scal = st["scal"]
        scbc = st["scbc"]
        NH = L1_HID // 128

        def allreduce(cols, tag):
            p.on("tensor", lambda e: e.matmul(
                PB[7][0:2, 0:128], parb[:, 0:2], identF[:], start=True, stop=True),
                wait_exact=[(tag + "_pin", p.cnt(tag + "_pin"))],
                inc=[(tag + "_par", 1)])
            p.on("vector", lambda e: e.reduce_max(
                paro[0:2, 0:1], PB[7][0:2, 0:128], axis=AXX),
                wait_exact=[(tag + "_par", p.cnt(tag + "_par"))],
                inc=[(tag + "_par", 1)])
            p.on("gpsimd", lambda e: e.dma_start(
                out=cc_in.ap().rearrange("a b -> b a"), in_=paro[0:2, 0:1]),
                wait_exact=[(tag + "_par", p.cnt(tag + "_par"))],
                inc=[("d_cc", 16)])
            p.on("gpsimd", lambda e: e.collective_compute(
                "AllReduce", ALU.max, replica_groups=[list(range(8))],
                ins=[cc_in.ap()], outs=[cc_out.ap()]),
                wait_exact=[("d_cc", p.cnt("d_cc"))],
                inc=[("cc_done", 1)])
            p.on("gpsimd", lambda e: e.dma_start(
                out=scal[0:1, 0:2], in_=cc_out.ap()),
                wait_exact=[("cc_done", p.cnt("cc_done"))],
                inc=[("d_cc2", 16)])
            p.on("vector", None, wait_exact=[("d_cc2", p.cnt("d_cc2"))])

        p.on("vector", lambda e: e.memset(scbc[:, 3:4], 32768.5), inc=[("cbias", 1)])
        for c in range(ND):
            p.on("vector", lambda e, c=c: e.reduce_max(
                amx[:, c:c + 1], xn_fm[:, ts(c, S)], axis=AXX,
                apply_absolute_value=True))
        p.on("vector", lambda e: e.reduce_max(
            parb.ap().rearrange("p (a b) -> p a b", b=1)[:, 0:1, :],
            amx.ap().rearrange("p (a b) -> p a b", b=8)[:, 0:1, :], axis=AXX))
        p.on("vector", lambda e: e.tensor_copy(parb[:, 1:2], parb[:, 0:1]),
             inc=[("x_pin", 1)])
        allreduce(1, "x")
        p.on("vector", lambda e: e.tensor_scalar(
            scal[0:1, 2:3], scal[0:1, 0:1], 1.0 / 32767.0, 1e-12, ALU.mult, ALU.max))
        p.on("vector", lambda e: e.reciprocal(scal[0:1, 3:4], scal[0:1, 2:3]),
             inc=[("x_sc", 1)])
        p.wait("vector", ["x_sc"])
        p.on("tensor", lambda e: e.matmul(
            PB[7][:, 0:2], onesf[:], scal[0:1, 2:4], start=True, stop=True),
            wait_exact=[("x_sc", p.cnt("x_sc"))], inc=[("x_sc", 1)])
        p.on("vector", lambda e: e.tensor_copy(scbc[:, 0:2], PB[7][:, 0:2]),
             wait_exact=[("x_sc", p.cnt("x_sc"))], inc=[("x_bc", 1)])

        nxq = 0
        for tcq in range(4):
            for kcq in range(ND):
                j = kcq * 4 + tcq
                buf = i32t if nxq % 2 == 0 else i32u
                p.on("scalar", lambda e, j=j, buf=buf: e.activation(
                    out=buf[:], in_=xn_fm[:, j * 512:(j + 1) * 512], func=AF.Identity,
                    scale=scbc[:, 1:2], bias=scbc[:, 3:4]),
                    wait_exact=[("x_bc", p.cnt("x_bc")), ("l1q_ev", 32)] +
                               ([("xq_v", nxq - 1)] if nxq >= 2 else []),
                    inc=[("xq_s", 1)])
                p.on("vector", lambda e, j=j, buf=buf: e.scalar_tensor_tensor(
                    xn_fm[:, j * 512:(j + 1) * 512], buf[:], -32768.0,
                    scbc[:, 0:1].broadcast_to([128, 512]), ALU.add, ALU.mult),
                    wait_exact=[("xq_s", p.cnt("xq_s"))],
                    inc=[("xq_v", 1)])
                nxq += 1

        p.on("vector", lambda e: e.memset(amx[:, 0:2], 0.0), inc=[("ga_z", 1)])
        wg1_marks = []
        for hc in range(NH):
            p.on("sync", lambda e, hc=hc: e.dma_start(
                out=whg[hc % 2].ap().rearrange("p (c o) -> p c o", o=128),
                in_=scr["gate"].ap()[:, :, hc * 128:(hc + 1) * 128]),
                wait_exact=([("gt_mm", (hc - 1) * 4)] if hc >= 2 else []),
                inc=[("d_wg1", 16)])
            wg1_marks.append(p.cnt("d_wg1"))
        for hc in range(NH):
            for tc in range(4):
                pbg = PB[0 if tc % 2 == 0 else 1]
                wex = [("d_wg1", wg1_marks[hc])] if tc == 0 else []
                k = p.cnt("gt_ev")
                if k >= 2:
                    wex += [("gt_ev", k - 1)]
                for kc in range(ND):
                    p.on("tensor", lambda e, pbg=pbg, hc=hc, kc=kc, tc=tc: e.matmul(
                        pbg[:], whg[hc % 2][:, kc * 128:(kc + 1) * 128],
                        xn_fm[:, kc * S + tc * 512: kc * S + (tc + 1) * 512],
                        start=(kc == 0), stop=(kc == ND - 1)),
                        wait_exact=(wex if kc == 0 else []) +
                                   ([("xq_v", 8 * (tc + 1))] if (hc == 0 and kc == 0) else []),
                        inc=([("gt_mm", 1)] if kc == ND - 1 else None))
                p.on("scalar", lambda e, pbg=pbg, hc=hc, tc=tc: e.activation(
                    out=gate_sb[:, hc * S + tc * 512: hc * S + (tc + 1) * 512],
                    in_=pbg[:], func=AF.Identity),
                    wait_exact=[("gt_mm", p.cnt("gt_mm"))], inc=[("gt_ev", 1)])
            p.on("vector", lambda e, hc=hc: e.reduce_max(
                amx[:, 2:3], gate_sb[:, ts(hc, S)], axis=AXX,
                apply_absolute_value=True),
                wait=(["ga_z"] if hc == 0 else None),
                wait_exact=[("gt_ev", p.cnt("gt_ev"))])
            p.vf()
            p.on("vector", lambda e: e.tensor_max(
                amx[:, 0:1], amx[:, 0:1], amx[:, 2:3]), inc=[("ga_m", 1)])

        p.vf()
        p.on("vector", lambda e: e.tensor_copy(parb[:, 0:2],
             amx.ap().rearrange("p (a b) -> p a b", b=1)[:, 0:2, :]
             .broadcast_to([128, 2, 1]).rearrange("p a b -> p (a b)") if False
             else amx[:, 0:2]),
             wait_exact=[("ga_m", NH)], inc=[("gu_pin", 1)])
        allreduce(1, "gu")
        p.on("vector", lambda e: e.tensor_scalar(
            scal[0:1, 4:5], scal[0:1, 0:1], 1.0 / 32767.0, 1e-12, ALU.mult, ALU.max))
        p.on("vector", lambda e: e.reciprocal(scal[0:1, 5:6], scal[0:1, 4:5]),
             inc=[("gu_sc", 1)])
        p.wait("vector", ["gu_sc"])
        p.on("tensor", lambda e: e.matmul(
            PB[7][:, 0:2], onesf[:], scal[0:1, 4:6], start=True, stop=True),
            wait_exact=[("gu_sc", p.cnt("gu_sc"))], inc=[("gu_sc", 1)])
        p.on("vector", lambda e: e.tensor_copy(scbc[:, 4:6], PB[7][:, 0:2]),
             wait_exact=[("gu_sc", p.cnt("gu_sc"))], inc=[("gu_bc", 1)])

        f32t2 = [PB[2], PB[3]]
        p.on("vector", lambda e: e.memset(amx[:, 2:3], 0.0), inc=[("g2_z", 1)])
        NJ = NH * 4
        wu1_marks = []
        for hc in range(NH):
            p.on("sync", lambda e, hc=hc: e.dma_start(
                out=whg[hc % 2].ap().rearrange("p (c o) -> p c o", o=128),
                in_=scr["up"].ap()[:, :, hc * 128:(hc + 1) * 128]),
                wait_exact=([("up1_mm", (hc - 1) * 4)] if hc >= 2 else
                            [("gt_mm", NH * 4)]),
                inc=[("d_wu1", 16)])
            wu1_marks.append(p.cnt("d_wu1"))
        for s_ in range(NJ + 5):
            if s_ < NJ:
                hc, tc = s_ // 4, s_ % 4
                pbg = PB[0 if tc % 2 == 0 else 1]
                wex = [("d_wu1", wu1_marks[hc])] if tc == 0 else []
                if p.cnt("up1_v") >= 2:
                    wex += [("up1_v", p.cnt("up1_v") - 1)]
                for kc in range(ND):
                    p.on("tensor", lambda e, pbg=pbg, hc=hc, kc=kc, tc=tc: e.matmul(
                        pbg[:], whg[hc % 2][:, kc * 128:(kc + 1) * 128],
                        xn_fm[:, kc * S + tc * 512: kc * S + (tc + 1) * 512],
                        start=(kc == 0), stop=(kc == ND - 1)),
                        wait_exact=(wex if kc == 0 else []),
                        inc=([("up1_mm", 1)] if kc == ND - 1 else None))
                p.on("vector", lambda e, pbg=pbg, s_=s_: e.reduce_max(
                    amx[:, 6 + s_ % 2:7 + s_ % 2], pbg[:], axis=AXX,
                    apply_absolute_value=True),
                    wait_exact=[("up1_mm", p.cnt("up1_mm"))])
                if s_ >= 1:
                    p.on("vector", lambda e, s_=s_: e.tensor_max(
                        amx[:, 1:2], amx[:, 1:2],
                        amx[:, 6 + (s_ - 1) % 2:7 + (s_ - 1) % 2]),
                        inc=[("up1_v", 1)])
            if s_ < NJ:
                sl = slice(s_ * 512, (s_ + 1) * 512)
                p.on("scalar", lambda e, sl=sl: e.activation(
                    out=i32t[:], in_=gate_sb[:, sl], func=AF.Identity,
                    scale=scbc[:, 5:6], bias=scbc[:, 3:4]),
                    wait_exact=[("gu_bc", p.cnt("gu_bc")),
                                ("g2_dq", p.cnt("g2_dq"))],
                    inc=[("g2_s", 1)])
                p.on("vector", lambda e, sl=sl: e.scalar_tensor_tensor(
                    gate_sb[:, sl], i32t[:], -32768.0,
                    scbc[:, 4:5].broadcast_to([128, 512]), ALU.add, ALU.mult),
                    wait_exact=[("g2_s", p.cnt("g2_s"))], inc=[("g2_dq", 1)])
            if 0 <= s_ - 1 < NJ:
                t = s_ - 1
                p.on("vector", lambda e, t=t: e.tensor_scalar(
                    f32t2[t % 2][:, 0:512], sgt[:], 65536.0, 65535.49,
                    ALU.mult, ALU.min),
                    wait_exact=[("g2_s2", t + 1)], inc=[("g2_ts", 1)])
            if s_ < NJ:
                sl = slice(s_ * 512, (s_ + 1) * 512)
                p.on("scalar", lambda e, sl=sl: e.activation(
                    out=sgt[:], in_=gate_sb[:, sl], func=AF.Sigmoid),
                    wait_exact=[("g2_dq", p.cnt("g2_dq")),
                                ("g2_ts", p.cnt("g2_ts"))],
                    inc=[("g2_s2", 1)])
            if 0 <= s_ - 3 < NJ:
                m_ = s_ - 3
                slm = slice(m_ * 512, (m_ + 1) * 512)
                p.on("vector", lambda e, slm=slm, m_=m_: e.scalar_tensor_tensor(
                    gate_sb[:, slm], i32u[:], 1.0 / 65536.0,
                    gate_sb[:, slm], ALU.mult, ALU.mult),
                    wait_exact=[("g2_s2", m_ + 1)], inc=[("g2_p", 1)])
            if 0 <= s_ - 2 < NJ:
                a_ = s_ - 2
                p.on("vector", lambda e, a_=a_: e.tensor_scalar_add(
                    i32u[:], f32t2[a_ % 2][:, 0:512], 0.5))
            if 0 <= s_ - 4 < NJ:
                r_ = s_ - 4
                slr = slice(r_ * 512, (r_ + 1) * 512)
                p.on("vector", lambda e, slr=slr, r_=r_: e.reduce_max(
                    amx[:, 3 + 2 * (r_ % 2):4 + 2 * (r_ % 2)], gate_sb[:, slr],
                    axis=AXX, apply_absolute_value=True),
                    wait=(["g2_z"] if r_ == 0 else None))
            if 0 <= s_ - 5 < NJ:
                x_ = s_ - 5
                p.on("vector", lambda e, x_=x_: e.tensor_max(
                    amx[:, 2:3], amx[:, 2:3],
                    amx[:, 3 + 2 * (x_ % 2):4 + 2 * (x_ % 2)]), inc=[("g2_m", 1)])
        p.vf()
        p.on("vector", lambda e: e.tensor_max(
            amx[:, 1:2], amx[:, 1:2], amx[:, 6 + (NJ - 1) % 2:7 + (NJ - 1) % 2]),
            inc=[("up1_v", 1)])

        p.vf()
        p.on("vector", lambda e: e.tensor_copy(parb[:, 0:1], amx[:, 1:2]),
             wait_exact=[("up1_v", p.cnt("up1_v"))])
        p.on("vector", lambda e: e.tensor_copy(parb[:, 1:2], amx[:, 1:2]),
             inc=[("gu_pin", 1)])
        allreduce(1, "gu")
        p.on("vector", lambda e: e.tensor_scalar(
            scal[0:1, 6:7], scal[0:1, 0:1], 1.0 / 32767.0, 1e-12, ALU.mult, ALU.max))
        p.on("vector", lambda e: e.reciprocal(scal[0:1, 7:8], scal[0:1, 6:7]),
             inc=[("gu_sc", 1)])
        p.wait("vector", ["gu_sc"])
        p.on("tensor", lambda e: e.matmul(
            PB[7][:, 0:2], onesf[:], scal[0:1, 6:8], start=True, stop=True),
            wait_exact=[("gu_sc", p.cnt("gu_sc"))], inc=[("gu_sc", 1)])
        p.on("vector", lambda e: e.tensor_copy(scbc[:, 6:8], PB[7][:, 0:2]),
             wait_exact=[("gu_sc", p.cnt("gu_sc"))], inc=[("gu_bc", 1)])

        p.vf()
        p.on("vector", lambda e: e.tensor_copy(parb[:, 0:1], amx[:, 2:3]),
             wait_exact=[("g2_m", NH * 4)])
        p.on("vector", lambda e: e.tensor_copy(parb[:, 1:2], amx[:, 2:3]),
             inc=[("t2_pin", 1)])
        allreduce(1, "t2")
        p.on("vector", lambda e: e.tensor_scalar(
            scal[0:1, 4:5], scal[0:1, 0:1], 1.0 / 32767.0, 1e-12, ALU.mult, ALU.max))
        p.on("vector", lambda e: e.reciprocal(scal[0:1, 5:6], scal[0:1, 4:5]),
             inc=[("t2_sc", 1)])
        p.wait("vector", ["t2_sc"])
        p.on("tensor", lambda e: e.matmul(
            PB[7][:, 0:2], onesf[:], scal[0:1, 4:6], start=True, stop=True),
            wait_exact=[("t2_sc", p.cnt("t2_sc"))], inc=[("t2_sc", 1)])
        p.on("vector", lambda e: e.tensor_copy(scbc[:, 4:6], PB[7][:, 0:2]),
             wait_exact=[("t2_sc", p.cnt("t2_sc"))], inc=[("t2_bc", 1)])
        def _q2(j):
            sl = slice(j * 512, (j + 1) * 512)
            p.on("scalar", lambda e, sl=sl: e.activation(
                out=i32t[:], in_=gate_sb[:, sl], func=AF.Identity,
                scale=scbc[:, 5:6], bias=scbc[:, 3:4]),
                wait_exact=[("t2_bc", p.cnt("t2_bc")), ("q2_v", p.cnt("q2_v"))],
                inc=[("q2_s", 1)])
            p.on("vector", lambda e, sl=sl: e.scalar_tensor_tensor(
                gate_sb[:, sl], i32t[:], -32768.0,
                scbc[:, 4:5].broadcast_to([128, 512]), ALU.add, ALU.mult),
                wait_exact=[("q2_s", p.cnt("q2_s"))], inc=[("q2_v", 1)])
        _q2(0)

        p.on("vector", lambda e: e.memset(amx[:, 3:4], 0.0), inc=[("o_z", 1)])
        wu2_marks = []
        for hc in range(NH):
            p.on("sync", lambda e, hc=hc: e.dma_start(
                out=whg[hc % 2].ap().rearrange("p (c o) -> p c o", o=128),
                in_=scr["up"].ap()[:, :, hc * 128:(hc + 1) * 128]),
                wait_exact=([("o_mm", (hc - 1) * 4)] if hc >= 2 else
                            [("up1_mm", NH * 4)]),
                inc=[("d_wu2", 16)])
            wu2_marks.append(p.cnt("d_wu2"))
        for hc in range(NH):
            for tc in range(4):
                j = hc * 4 + tc
                if j + 1 < NH * 4:
                    _q2(j + 1)
                pbg = PB[0 if tc % 2 == 0 else 1]
                wex = [("d_wu2", wu2_marks[hc])] if tc == 0 else []
                if p.cnt("o_s") >= 2:
                    wex += [("o_s", p.cnt("o_s") - 1)]
                if hc == 0 and tc == 0:
                    wex += [("o_z", p.cnt("o_z"))]
                for kc in range(ND):
                    p.on("tensor", lambda e, pbg=pbg, hc=hc, kc=kc, tc=tc: e.matmul(
                        pbg[:], whg[hc % 2][:, kc * 128:(kc + 1) * 128],
                        xn_fm[:, kc * S + tc * 512: kc * S + (tc + 1) * 512],
                        start=(kc == 0), stop=(kc == ND - 1)),
                        wait_exact=(wex if kc == 0 else []),
                        inc=([("o_mm", 1)] if kc == ND - 1 else None))
                p.on("scalar", lambda e, pbg=pbg: e.activation(
                    out=i32o[:], in_=pbg[:], func=AF.Identity,
                    scale=scbc[:, 7:8], bias=scbc[:, 3:4]),
                    wait_exact=[("o_mm", p.cnt("o_mm")),
                                ("o_stt", p.cnt("o_stt"))],
                    inc=[("o_s", 1)])
                lo = hc * S + tc * 512
                p.on("vector", lambda e, lo=lo: e.scalar_tensor_tensor(
                    gate_sb[:, lo:lo + 512], i32o[:], -32768.0,
                    gate_sb[:, lo:lo + 512], ALU.add, ALU.mult),
                    wait_exact=[("o_s", p.cnt("o_s")), ("q2_v", j + 1)],
                    inc=[("o_stt", 1)])
                if j >= 1:
                    lp = lo - 512 if tc >= 1 else (hc - 1) * S + 3 * 512
                    p.on("vector", lambda e, lp=lp, j=j: e.reduce_max(
                        amx[:, 4 + (j - 1) % 2:5 + (j - 1) % 2],
                        gate_sb[:, lp:lp + 512], axis=AXX,
                        apply_absolute_value=True))
                if j >= 2:
                    p.on("vector", lambda e, j=j: e.tensor_max(
                        amx[:, 3:4], amx[:, 3:4],
                        amx[:, 4 + (j - 2) % 2:5 + (j - 2) % 2]), inc=[("o_v", 1)])
        p.vf()
        lp = (NH - 1) * S + 3 * 512
        p.on("vector", lambda e, lp=lp: e.reduce_max(
            amx[:, 4 + 95 % 2:5 + 95 % 2], gate_sb[:, lp:lp + 512], axis=AXX,
            apply_absolute_value=True))
        p.vf()
        p.on("vector", lambda e: e.tensor_max(
            amx[:, 3:4], amx[:, 3:4], amx[:, 4 + 94 % 2:5 + 94 % 2]),
            inc=[("o_v", 1)])
        p.vf()
        p.on("vector", lambda e: e.tensor_max(
            amx[:, 3:4], amx[:, 3:4], amx[:, 4 + 95 % 2:5 + 95 % 2]),
            inc=[("o_v", 1)])

        p.vf()
        p.on("vector", lambda e: e.tensor_copy(parb[:, 0:1], amx[:, 3:4]),
             wait_exact=[("o_v", p.cnt("o_v"))])
        p.on("vector", lambda e: e.tensor_copy(parb[:, 1:2], amx[:, 3:4]),
             inc=[("oo_pin", 1)])
        allreduce(1, "oo")
        p.on("vector", lambda e: e.tensor_mul(scal[0:1, 0:1], scal[0:1, 0:1],
                                              scal[0:1, 6:7]))
        p.on("vector", lambda e: e.tensor_scalar(
            scal[0:1, 4:5], scal[0:1, 0:1], 1.0 / 32767.0, 1e-12, ALU.mult, ALU.max))
        p.on("vector", lambda e: e.reciprocal(scal[0:1, 5:6], scal[0:1, 4:5]),
             inc=[("rcp", 1)])
        p.wait("vector", ["rcp"])
        p.on("vector", lambda e: e.tensor_mul(scal[0:1, 5:6], scal[0:1, 5:6],
                                              scal[0:1, 6:7]), inc=[("oq_sc", 1)])
        p.on("tensor", lambda e: e.matmul(
            PB[7][:, 0:2], onesf[:], scal[0:1, 4:6], start=True, stop=True),
            wait_exact=[("oq_sc", p.cnt("oq_sc"))], inc=[("oq_sc", 1)])
        p.on("vector", lambda e: e.tensor_copy(scbc[:, 4:6], PB[7][:, 0:2]),
             wait_exact=[("oq_sc", p.cnt("oq_sc"))], inc=[("oq_bc", 1)])
        def _oq(j):
            sl = slice(j * 512, (j + 1) * 512)
            buf = i32t if j % 2 == 0 else i32u
            p.on("scalar", lambda e, sl=sl, buf=buf: e.activation(
                out=buf[:], in_=gate_sb[:, sl], func=AF.Identity,
                scale=scbc[:, 5:6], bias=scbc[:, 3:4]),
                wait_exact=[("oq_bc", p.cnt("oq_bc"))] +
                           ([("oq_v", j - 1)] if j >= 2 else []),
                inc=[("oq_s", 1)])
            p.on("vector", lambda e, sl=sl, buf=buf: e.scalar_tensor_tensor(
                gate_sb[:, sl], buf[:], -32768.0,
                scbc[:, 4:5].broadcast_to([128, 512]), ALU.add, ALU.mult),
                wait_exact=[("oq_s", p.cnt("oq_s"))], inc=[("oq_v", 1)])
        for j in range(4):
            _oq(j)

        for ttg in range(4):
            for hc in range(NH):
                if ttg == 0:
                    for j in range(4 * (hc + 1), min(4 * (hc + 2), NH * 4)):
                        _oq(j)
                p.on("sync", lambda e, hc=hc: e.dma_start(
                    out=wdn[hc % 2].ap(),
                    in_=scr["down"].ap()[hc, :, :]),
                    wait_exact=([("dn_hc", max(0, p.cnt("dn_hc") - 1))]
                                if (ttg > 0 or hc >= 2) else
                                ([("up1_mm", NH * 4)] if ttg == 0 else [])),
                    inc=[("d_dn", 16)])
                for m4 in range(4):
                    for oc in range(2):
                        pbd = PB[m4 * 2 + oc]
                        wex = []
                        if hc == 0 and m4 == 0 and oc == 0:
                            wex = [("oq_v", 4 if ttg == 0 else NH * 4),
                                   ("dn_ev", p.cnt("dn_ev")),
                                   ("d_dn", p.cnt("d_dn"))]
                        elif hc > 0 and m4 == 0 and oc == 0:
                            wex = [("d_dn", p.cnt("d_dn"))]
                            if ttg == 0:
                                wex += [("oq_v", min((hc + 1) * 4, NH * 4))]
                        p.on("tensor", lambda e, pbd=pbd, hc=hc, m4=m4, oc=oc, ttg=ttg: e.matmul(
                            pbd[:], gate_sb[:, hc * S + (ttg * 4 + m4) * 128:
                                            hc * S + (ttg * 4 + m4) * 128 + 128],
                            wdn[hc % 2][:, oc * 512:(oc + 1) * 512],
                            start=(hc == 0), stop=(hc == NH - 1)),
                            wait_exact=wex,
                            inc=([("dn_mm", 1)] if hc == NH - 1 else None))
                        if m4 == 3 and oc == 1:
                            p.on("tensor", lambda e: e.nop(), inc=[("dn_hc", 1)])
            for m4 in range(4):
                tt = ttg * 4 + m4
                for oc in range(2):
                    pbd = PB[m4 * 2 + oc]
                    lo = tt * D + oc * 512
                    p.on("vector", lambda e, pbd=pbd, lo=lo: e.scalar_tensor_tensor(
                        x_res[:, lo:lo + 512], pbd[:], 1.0,
                        x_res[:, lo:lo + 512], ALU.mult, ALU.add),
                        wait_exact=[("dn_mm", p.cnt("dn_mm"))],
                        inc=[("dn_ev", 1)])

    import contextlib

    def emit_dump(key, src_ap):
        if not dbg:
            return
        p.on("sync", lambda e, key=key, src_ap=src_ap: e.dma_start(
            out=dbg_t[key].ap(), in_=src_ap), inc=[("d_out", 16)])

    with contextlib.ExitStack() as stack:
        _sqd = stack.enter_context(nc.sbuf_tensor("sqd_a", [128, D], BF16))
        _xtm = stack.enter_context(nc.sbuf_tensor("xtm_a", [128, 2 * D], BF16))
        emit_layernorm("a", _sqd, _xtm, first=True, spill=scr["xn"])
        p.barrier()
        if dbg:
            emit_dump("xn", xn_fm.ap())
            p.on("sync", lambda e: e.dma_start(
                out=dbg_t["stats"].ap()[:, 0:NT], in_=m_sb.ap()), inc=[("d_out", 16)])
            p.on("sync", lambda e: e.dma_start(
                out=dbg_t["stats"].ap()[:, NT:2 * NT], in_=r_sb.ap()), inc=[("d_out", 16)])
            for i_, tn in enumerate((dbg_s1, dbg_s2, dbg_s3, dbg_s4)):
                p.on("sync", lambda e, i_=i_, tn=tn: e.dma_start(
                    out=dbg_t["stats"].ap()[:, (2 + i_) * NT:(3 + i_) * NT],
                    in_=tn.ap()), inc=[("d_out", 16)])
    with contextlib.ExitStack() as stack:
        st = {}
        st["WT"] = {w: stack.enter_context(
            nc.sbuf_tensor("WT_" + w, [128, ND * D], BF16))
            for w in ("Wq", "Wk", "Wv", "Wo")}
        st["wstage"] = stack.enter_context(nc.sbuf_tensor("wstage", [128, 4 * D], BF16))
        st["q_pair"] = stack.enter_context(nc.sbuf_tensor("q_pair", [128, S], BF16))
        st["k_pair"] = stack.enter_context(nc.sbuf_tensor("k_pair", [128, S], BF16))
        st["v_aug"] = stack.enter_context(nc.sbuf_tensor("v_aug", [128, NT * 130], BF16))
        st["ctx_pair"] = stack.enter_context(nc.sbuf_tensor("ctx_pair", [128, S], BF16))
        st["e_sb"] = stack.enter_context(nc.sbuf_tensor("e_sb", [128, 8 * 512], BF16))
        st["ctxr"] = [stack.enter_context(nc.sbuf_tensor("ctxr%d" % i, [65, 512], F32))
                      for i in range(2)]
        st["r_row"] = [stack.enter_context(nc.sbuf_tensor("r_row%d" % i, [1, 512], BF16))
                       for i in range(2)]
        st["ones_row"] = stack.enter_context(nc.sbuf_tensor("ones_row", [1, 512], BF16))
        st["bv_row"] = stack.enter_context(nc.sbuf_tensor("bv_row", [1, D], BF16))
        st["bo_row"] = stack.enter_context(nc.sbuf_tensor("bo_row", [1, D], BF16))
        st["alibi"] = stack.enter_context(nc.sbuf_tensor("alibi", [128, NHEAD * NT], F32))
        st["slopes_bc"] = stack.enter_context(nc.sbuf_tensor("slopes_bc", [128, NHEAD], F32))
        emit_attn_l0(st)
        p.barrier()
    emit_dump("attn", x_res.ap())

    with contextlib.ExitStack() as stack:
        st = {}
        st["gw_stage"] = stack.enter_context(nc.sbuf_tensor("gw_stage", [8, D], BF16))
        st["gwT"] = stack.enter_context(nc.sbuf_tensor("gwT", [128, ND * 8], BF16))
        st["logit_sb"] = stack.enter_context(nc.sbuf_tensor("logit_sb", [128, 8], F32))
        st["comb"] = stack.enter_context(nc.sbuf_tensor("comb", [128, NT * 8], F32))
        st["g_t0"] = stack.enter_context(nc.sbuf_tensor("g_t0", [128, 8], F32))
        st["g_t1"] = stack.enter_context(nc.sbuf_tensor("g_t1", [128, 8], F32))
        st["g_c0"] = stack.enter_context(nc.sbuf_tensor("g_c0", [128, 1], F32))
        st["g_c1"] = stack.enter_context(nc.sbuf_tensor("g_c1", [128, 1], F32))
        st["w1g"] = [stack.enter_context(nc.sbuf_tensor("w1g%d" % i, [128, ND * 384], BF16))
                     for i in range(2)]
        st["w3g"] = [stack.enter_context(nc.sbuf_tensor("w3g%d" % i, [128, ND * 384], BF16))
                     for i in range(2)]
        st["w2g"] = stack.enter_context(nc.sbuf_tensor("w2g", [128, 6 * D], BF16))
        st["h_sb"] = stack.enter_context(nc.sbuf_tensor("h_sb0", [128, 6 * 768], BF16))
        st["hsil"] = [stack.enter_context(nc.sbuf_tensor("hsil%d" % i, [128, 512], BF16))
                      for i in range(2)]
        st["wq_stage"] = stack.enter_context(nc.sbuf_tensor("wq_stage", [128, 512], F32))
        st["dq_s"] = stack.enter_context(nc.sbuf_tensor("dq_s", [128, 16], F32))
        st["dq_si"] = stack.enter_context(nc.sbuf_tensor("dq_si", [128, 16], F32))
        st["dq_t"] = stack.enter_context(nc.sbuf_tensor("dq_t", [128, 512], F32))
        st["dq_i"] = stack.enter_context(nc.sbuf_tensor("dq_i", [128, 512], I32))
        st["dq_w"] = stack.enter_context(nc.sbuf_tensor("dq_w", [128, 512], BF16))
        st["tr_stage"] = stack.enter_context(nc.sbuf_tensor("tr_stage", [128, 4 * 128], BF16))
        st["tri_sb"] = stack.enter_context(nc.sbuf_tensor("tri_sb", [128, 128], BF16))
        st["stri_sb"] = stack.enter_context(nc.sbuf_tensor("stri_sb", [128, 128], BF16))
        st["iota_sb"] = stack.enter_context(nc.sbuf_tensor("iota_sb", [128, 768], F32))
        st["route_m"] = stack.enter_context(nc.sbuf_tensor("route_m", [128, 128], BF16))
        st["t_sb"] = stack.enter_context(nc.sbuf_tensor("t_sb", [128, 128], F32))
        st["pos"] = stack.enter_context(nc.sbuf_tensor("pos", [128, 128], F32))
        st["onesb"] = stack.enter_context(nc.sbuf_tensor("onesb", [128, 1], BF16))
        st["ccol_sb"] = stack.enter_context(nc.sbuf_tensor("ccol_sb", [128, 1], BF16))
        st["base_sb"] = stack.enter_context(nc.sbuf_tensor("base_sb", [1, 128], F32))
        st["pt_sb"] = [stack.enter_context(nc.sbuf_tensor("pt%d" % i, [128, 768], BF16))
                       for i in range(2)]
        st["ptT_sb"] = [stack.enter_context(nc.sbuf_tensor("ptT%d" % i, [128, 768], BF16))
                        for i in range(2)]
        st["xt_sb"] = [stack.enter_context(nc.sbuf_tensor("xt%d" % i, [128, 512], BF16))
                       for i in range(2)]
        st["gth_sb"] = [stack.enter_context(nc.sbuf_tensor("gth%d" % i, [128, 512], BF16))
                        for i in range(2)]
        st["gath_fm"] = stack.enter_context(nc.sbuf_tensor("gath_fm", [128, 8 * 768], BF16))
        st["y_sb"] = stack.enter_context(nc.sbuf_tensor("y_sb", [128, 6 * D], F32))
        dq_gen = emit_dequant(st)
        emit_moe(st, filler=dq_gen)
        for _ in dq_gen:
            pass
        p.barrier()
    emit_dump("moe", x_res.ap())
    if dbg:
        p.on("sync", lambda e: e.dma_start(
            out=dbg_t["scr"].ap(), in_=scr["up"].ap()), inc=[("d_out", 16)])

    with contextlib.ExitStack() as stack:
        _sqd = stack.enter_context(nc.sbuf_tensor("sqd_b", [128, D], BF16))
        _xtm = stack.enter_context(nc.sbuf_tensor("xtm_b", [128, 2 * D], BF16))
        emit_layernorm("b", _sqd, _xtm)
        p.barrier()

    with contextlib.ExitStack() as stack:
        st = {}
        st["wl1"] = stack.enter_context(nc.sbuf_tensor("wl1", [128, ND * 512], BF16))
        st["fmA"] = stack.enter_context(nc.sbuf_tensor("fmA", [128, ND * S], BF16))
        st["fmB"] = stack.enter_context(nc.sbuf_tensor("fmB", [128, ND * S], BF16))
        st["fres"] = stack.enter_context(nc.sbuf_tensor("fres", [128, ND * NF], BF16))
        st["fims"] = stack.enter_context(nc.sbuf_tensor("fims", [128, ND * NF], BF16))
        st["sre"] = stack.enter_context(nc.sbuf_tensor("sre", [128, 8], F32))
        st["sim"] = stack.enter_context(nc.sbuf_tensor("sim", [128, 8], F32))
        st["tA"] = stack.enter_context(nc.sbuf_tensor("tA", [128, 512], BF16))
        st["tB"] = stack.enter_context(nc.sbuf_tensor("tB", [128, 512], BF16))
        st["u1"] = stack.enter_context(nc.sbuf_tensor("u1", [128, 512], BF16))
        st["u2"] = stack.enter_context(nc.sbuf_tensor("u2", [128, 512], F32))
        st["u3"] = stack.enter_context(nc.sbuf_tensor("u3", [128, 512], BF16))
        st["u4"] = stack.enter_context(nc.sbuf_tensor("u4", [128, 512], F32))
        st["ab_a"] = stack.enter_context(nc.sbuf_tensor("ab_a", [128, 5 * 512], BF16))
        st["ab_b"] = stack.enter_context(nc.sbuf_tensor("ab_b", [128, 5 * 512], BF16))
        st["bl1_sb"] = stack.enter_context(nc.sbuf_tensor("bl1_sb", [128, 3 * ND], F32))
        emit_attn_l1(st)
        p.barrier()
    emit_dump("l1a", x_res.ap())

    with contextlib.ExitStack() as stack2:
        st2 = {}
        st2["whg"] = [stack2.enter_context(
            nc.sbuf_tensor("whg%d" % i, [128, ND * 128], BF16)) for i in range(2)]
        st2["wdn"] = [stack2.enter_context(
            nc.sbuf_tensor("wdn%d" % i, [128, D], BF16)) for i in range(2)]
        st2["i32t"] = stack2.enter_context(nc.sbuf_tensor("i32t", [128, 512], I32))
        st2["i32u"] = stack2.enter_context(nc.sbuf_tensor("i32u", [128, 512], I32))
        st2["sgt"] = stack2.enter_context(nc.sbuf_tensor("sgt", [128, 512], BF16))
        st2["amx"] = stack2.enter_context(nc.sbuf_tensor("amx", [128, 8], F32))
        st2["parb"] = stack2.enter_context(nc.sbuf_tensor("parb", [128, 2], F32))
        st2["paro"] = stack2.enter_context(nc.sbuf_tensor("paro", [128, 2], F32))
        st2["scal"] = stack2.enter_context(nc.sbuf_tensor("scal", [1, 8], F32))
        st2["scbc"] = stack2.enter_context(nc.sbuf_tensor("scbc", [128, 8], F32))
        st2["gate_sb"] = stack2.enter_context(
            nc.sbuf_tensor("gate_sb", [128, (L1_HID // 128) * S], BF16))
        emit_ffn_l1(st2)
        p.barrier()

    p.on("sync", lambda e: e.dma_start(
        out=out.ap().rearrange("(t p) d -> p t d", p=128),
        in_=x_res.ap().rearrange("p (t d) -> p t d", d=D)),
        inc=[("d_out", 16)])
    p.on("gpsimd", None, wait=["d_out"])

    p.finalize()
    return nc


_NC_CACHE = None


def _consts():
    n = np.arange(D)[:, None].astype(np.float64)
    f = np.arange(NF)[None, :].astype(np.float64)
    th = 2.0 * np.pi * n * f / D
    fre = np.cos(th)
    fim = -np.sin(th)
    c = np.full(NF, 2.0); c[0] = 1.0; c[NF - 1] = 1.0
    nn = np.arange(D)[None, :].astype(np.float64)
    thg = 2.0 * np.pi * f.T * nn / D
    gre = (c[:, None] * np.cos(thg)) / D
    gim = (-c[:, None] * np.sin(thg)) / D
    bf = ml_dtypes.bfloat16
    tri = np.triu(np.ones((128, 128))).astype(bf)
    kk = np.arange(128)
    stri = ((kk[:, None] % 8 == kk[None, :] % 8) &
            (kk[:, None] // 8 < kk[None, :] // 8)).astype(bf)
    iotaw = np.broadcast_to(np.arange(768, dtype=np.float32), (128, 768)).copy()
    return (fre.astype(bf), fim.astype(bf), gre.astype(bf), gim.astype(bf),
            tri, stri, iotaw)


def kernel(**inputs):
    global _NC_CACHE
    if _NC_CACHE is None:
        _NC_CACHE = build_nc()
    nc = _NC_CACHE
    fre, fim, gre, gim, tri, stri, iotaw = _consts()
    base = {
        "L0_Wq": inputs["L0_attn_Wq"], "L0_Wk": inputs["L0_attn_Wk"],
        "L0_Wv": inputs["L0_attn_Wv"], "L0_Wo": inputs["L0_attn_Wo"],
        "L0_bq": inputs["L0_attn_bq"], "L0_bk": inputs["L0_attn_bk"],
        "L0_bv": inputs["L0_attn_bv"], "L0_bo": inputs["L0_attn_bo"],
        "L0_slopes": inputs["L0_attn_slopes"],
        "gate_w": inputs["L0_ffn_gate_w"],
        "w1": inputs["L0_ffn_w1"], "w2": inputs["L0_ffn_w2"], "w3": inputs["L0_ffn_w3"],
        "sw1": inputs["L0_ffn_sw1"], "sw2": inputs["L0_ffn_sw2"], "sw3": inputs["L0_ffn_sw3"],
        "L1_Wq": inputs["L1_attn_Wq"], "L1_Wk": inputs["L1_attn_Wk"],
        "L1_Wv": inputs["L1_attn_Wv"],
        "L1_bq": inputs["L1_attn_bq"], "L1_bk": inputs["L1_attn_bk"],
        "L1_bv": inputs["L1_attn_bv"],
        "w_gate": inputs["L1_ffn_w_gate"], "w_up": inputs["L1_ffn_w_up"],
        "w_down": inputs["L1_ffn_w_down"],
        "fre": fre, "fim": fim, "gre": gre, "gim": gim,
        "tri": tri, "stri": stri, "iotaw": iotaw,
    }
    base = {k: np.ascontiguousarray(np.asarray(v)) for k, v in base.items()}
    x = np.asarray(inputs["L0_attn_x"])
    in_maps = []
    for c in range(8):
        m = dict(base)
        m["x"] = np.ascontiguousarray(x[c])
        in_maps.append(m)
    res = run_bass_kernel_spmd(nc, in_maps, core_ids=list(range(8)))
    return np.stack([res.results[c]["out"] for c in range(8)], axis=0)


if __name__ == "__main__":
    nc = build_nc()
    print("built OK, instructions:", nc.n_instructions())


# version 26
# speedup vs baseline: 1.1969x; 1.0414x over previous
import numpy as np
import ml_dtypes

import concourse.bass as bass
import concourse.mybir as mybir
from concourse.bass_utils import run_bass_kernel_spmd
from concourse.bass_isa import ReduceOp
import concourse.library_config as libcfg

F32 = mybir.dt.float32
BF16 = mybir.dt.bfloat16
I32 = mybir.dt.int32
AF = mybir.ActivationFunctionType
ALU = mybir.AluOpType
AXX = mybir.AxisListType.X

S = 2048
D = 1024
NT = S // 128
ND = D // 128
NHEAD = 16
MOE_E = 8
MOE_HID = 2816
SH_HID = 1536
L1_HID = 3072
NF = 513

ENGS = ("sync", "tensor", "vector", "scalar", "gpsimd")


class Prog:

    ALIAS = {}
    for _n in ("d_x", "d_sm", "d_dq", "d_dqo", "d_l1b", "d_l1f", "d_l1w",
               "d_l1g", "d_wg1", "d_wu1", "d_wu2", "d_dn", "d_out", "dbgd"):
        ALIAS[_n] = "dS"
    for _n in ("d_smg", "d_wstage", "d_gw", "d_wg", "d_cc", "d_cc2", "d_l1wg"):
        ALIAS[_n] = "dG"
    for _n in ("x_pin", "gu_pin", "t2_pin", "oo_pin"):
        ALIAS[_n] = "ar_pin"
    for _n in ("x_par", "gu_par", "t2_par", "oo_par"):
        ALIAS[_n] = "ar_par"
    for _n in ("x_sc", "gu_sc", "t2_sc", "oo_sc", "oq_sc"):
        ALIAS[_n] = "ar_sc"
    for _n in ("x_bc", "gu_bc", "t2_bc", "oq_bc"):
        ALIAS[_n] = "ar_bc"
    for _n in ("ga_z", "g2_z", "o_z", "l1s_z", "v_ones"):
        ALIAS[_n] = "zz"
    ALIAS["c_one"] = "c_id"
    ALIAS["gw_tr"] = "dq_tr"
    ALIAS["gw_ev"] = "dq_ev"
    ALIAS["g_tr"] = "dq_tr"
    ALIAS["s_tr"] = "dq_tr"
    ALIAS["s_te"] = "g_gf"
    ALIAS["sc_ev"] = "my_ev"
    ALIAS["d_xt"] = "dS"
    ALIAS["d_ct"] = "dS"
    ALIAS["g_pt"] = "rt"
    ALIAS["o_stt"] = "rt"
    ALIAS["g2_dq"] = "rt"
    ALIAS["g2_ts"] = "g_gf"
    ALIAS["s_pt"] = "rt"
    ALIAS["pos_d"] = "rt"
    ALIAS["g_ev"] = "my_ev"
    ALIAS["s_mm"] = "my_mm"

    def __init__(self, nc):
        self.nc = nc
        self.items = {e: [] for e in ENGS}
        self.sems = {}
        self.counts = {}

    def sem(self, name):
        name = self.ALIAS.get(name, name)
        if name not in self.sems:
            self.sems[name] = self.nc.alloc_semaphore(name)
            self.counts[name] = 0
        return self.sems[name]

    def cnt(self, name):
        self.sem(name)
        return self.counts[self.ALIAS.get(name, name)]

    def on(self, eng, fn, wait=None, wait_exact=None, inc=None):
        waits = []
        for nm in (wait or []):
            h = self.sem(nm)
            waits.append((h, self.counts[self.ALIAS.get(nm, nm)]))
        for nm, v in (wait_exact or []):
            waits.append((self.sem(nm), v))
        incs = []
        for nm, n in (inc or []):
            incs.append((self.sem(nm), n))
            self.counts[self.ALIAS.get(nm, nm)] += n
        self.items[eng].append((waits, fn, incs))

    def wait(self, eng, names):
        self.on(eng, None, wait=names)

    def vf(self):
        self.on("vector", lambda e: e.nop(), inc=[("vfen", 1)])
        self.on("vector", None, wait=["vfen"])

    def barrier(self, engs=ENGS):
        nm = "bar"
        for e in engs:
            self.on(e, lambda en: en.nop(), inc=[(nm, 1)])
        tgt = self.counts[nm]
        for e in engs:
            self.on(e, None, wait_exact=[(nm, tgt)])

    def finalize(self):
        nc = self.nc
        with nc.Block() as block:
            def runner(items):
                def run(engine):
                    for waits, fn, incs in items:
                        for h, v in waits:
                            if v > 0:
                                engine.wait_ge(h, v)
                        if fn is not None:
                            inst = fn(engine)
                            if incs:
                                inst.then_inc(*incs[0])
                                for h, n in incs[1:]:
                                    engine.nop().then_inc(h, n)
                return run
            block.sync(runner(self.items["sync"]))
            block.tensor(runner(self.items["tensor"]))
            block.vector(runner(self.items["vector"]))
            block.scalar(runner(self.items["scalar"]))
            block.gpsimd(runner(self.items["gpsimd"]))


def ts(i, n=128):
    return slice(i * n, (i + 1) * n)


def build_nc(dbg=False):
    nc = bass.Bass("TRN2", target_bir_lowering=False, debug=False,
                   detect_race_conditions=False)

    dram = {}
    def din(name, shape, dtype=F32):
        dram[name] = nc.dram_tensor(name, list(shape), dtype, kind="ExternalInput")
        return dram[name]

    din("x", (S, D))
    for w in ("Wq", "Wk", "Wv", "Wo"):
        din("L0_" + w, (D, D))
    for b in ("bq", "bk", "bv", "bo"):
        din("L0_" + b, (D,))
    din("L0_slopes", (NHEAD,))
    din("gate_w", (MOE_E, D))
    din("w1", (MOE_E, D, MOE_HID))
    din("w2", (MOE_E, MOE_HID, D))
    din("w3", (MOE_E, D, MOE_HID))
    din("sw1", (D, SH_HID))
    din("sw2", (SH_HID, D))
    din("sw3", (D, SH_HID))
    for w in ("Wq", "Wk", "Wv"):
        din("L1_" + w, (D, D))
    for b in ("bq", "bk", "bv"):
        din("L1_" + b, (D,))
    din("w_gate", (L1_HID, D))
    din("w_up", (L1_HID, D))
    din("w_down", (D, L1_HID))
    din("tri", (128, 128), BF16)
    din("stri", (128, 128), BF16)
    din("iotaw", (128, 768), F32)
    din("fre", (D, NF), BF16)
    din("fim", (D, NF), BF16)
    din("gre", (NF, D), BF16)
    din("gim", (NF, D), BF16)

    out = nc.dram_tensor("out", [S, D], F32, kind="ExternalOutput")
    dbg_t = {}
    if dbg:
        dbg_t["stats"] = nc.dram_tensor("dbg_stats", [128, 6 * NT], F32, kind="ExternalOutput")
        dbg_t["xn"] = nc.dram_tensor("dbg_xn", [128, ND * S], BF16, kind="ExternalOutput")
        dbg_t["attn"] = nc.dram_tensor("dbg_attn", [128, NT * D], F32, kind="ExternalOutput")
        dbg_t["moe"] = nc.dram_tensor("dbg_moe", [128, NT * D], F32, kind="ExternalOutput")
        dbg_t["l1a"] = nc.dram_tensor("dbg_l1a", [128, NT * D], F32, kind="ExternalOutput")
        dbg_t["scr"] = nc.dram_tensor("dbg_scr", [ND, 128, L1_HID], BF16, kind="ExternalOutput")
        dbg_t["wt"] = nc.dram_tensor("dbg_wt", [128, ND * D], BF16, kind="ExternalOutput")
        dbg_t["qp"] = nc.dram_tensor("dbg_qp", [128, S], BF16, kind="ExternalOutput")
        dbg_t["kp"] = nc.dram_tensor("dbg_kp", [128, S], BF16, kind="ExternalOutput")
        dbg_t["va"] = nc.dram_tensor("dbg_va", [128, NT * 130], BF16, kind="ExternalOutput")
        dbg_t["ctxp"] = nc.dram_tensor("dbg_ctxp", [128, S], BF16, kind="ExternalOutput")

    scr = {
        "up": nc.dram_tensor("scr_upT", [ND, 128, L1_HID], BF16, kind="Internal"),
        "gate": nc.dram_tensor("scr_gateT", [ND, 128, L1_HID], BF16, kind="Internal"),
        "down": nc.dram_tensor("scr_downT", [L1_HID // 128, 128, D], BF16, kind="Internal"),
        "xn": nc.dram_tensor("scr_xn", [NT, 128, D], BF16, kind="Internal"),
    }
    cc_in = nc.dram_tensor("cc_in", [1, 2], F32, kind="Internal")
    cc_out = nc.dram_tensor("cc_out", [1, 2], F32, kind="Internal")

    p = Prog(nc)

    A = nc.alloc_sbuf_tensor
    x_res = A("x_res", [128, NT * D], F32)
    xn_fm = A("xn_fm", [128, ND * S], BF16)
    ident = A("ident", [128, 128], BF16)
    bq_sb = A("bq_sb", [128, ND], F32)
    bk_sb = A("bk_sb", [128, ND], F32)
    slopes_sb = A("slopes_sb", [1, NHEAD], BF16)
    identF = A("identF", [128, 128], F32)
    onesf = A("onesf", [1, 128], F32)
    iot = A("iot", [128, NT], F32)
    iot_i = A("iot_i", [128, NT], I32)
    sums = A("sums", [128, NT], F32)
    sumsq = A("sumsq", [128, NT], F32)
    m_sb = A("m_sb", [128, NT], F32)
    r_sb = A("r_sb", [128, NT], F32)
    stat_a = A("stat_a", [128, NT], F32)
    stat_b = A("stat_b", [128, NT], F32)
    dbg_s1 = A("dbg_s1", [128, NT], F32)
    dbg_s2 = A("dbg_s2", [128, NT], F32)
    dbg_s3 = A("dbg_s3", [128, NT], F32)
    dbg_s4 = A("dbg_s4", [128, NT], F32)

    PB = [nc.alloc_psum_tensor("pb%d" % i, [128, 512], F32) for i in range(8)]

    p.on("sync", lambda e: e.dma_start(
        out=x_res.ap().rearrange("p (t d) -> p t d", d=D),
        in_=dram["x"].ap().rearrange("(t p) d -> p t d", p=128)),
        inc=[("d_x", 16)])
    for nm, dst, rr in (("L0_bq", bq_sb, True), ("L0_bk", bk_sb, True)):
        p.on("sync", lambda e, nm=nm, dst=dst: e.dma_start(
            out=dst.ap(), in_=dram[nm].ap().rearrange("(c p) -> p c", p=128),
            allow_slow_non_contiguous=True),
            inc=[("d_sm", 16)])
    p.on("gpsimd", lambda e: e.dma_start(
        out=slopes_sb.ap(), in_=dram["L0_slopes"].ap().rearrange("(o d) -> o d", o=1)),
        inc=[("d_smg", 16)])
    p.on("vector", lambda e: e.memset(ident[:], 1.0), inc=[("c_id", 1)])
    p.on("gpsimd", lambda e: e.affine_select(
        ident[:], ident[:], [[1, 128]], ALU.is_equal, 0.0, base=0,
        channel_multiplier=-1), wait=["c_id"], inc=[("c_id", 1)])
    p.on("vector", lambda e: e.memset(identF[:], 1.0), inc=[("c_id", 1)])
    p.on("gpsimd", lambda e: e.affine_select(
        identF[:], identF[:], [[1, 128]], ALU.is_equal, 0.0, base=0,
        channel_multiplier=-1), wait=["c_id"], inc=[("c_id", 1)])
    p.on("vector", lambda e: e.memset(onesf[:], 1.0), inc=[("c_id", 1)])
    p.on("gpsimd", lambda e: e.iota(iot_i[:], [[128, NT]], base=-2047,
                                    channel_multiplier=1),
         inc=[("c_id", 1)])
    p.on("vector", lambda e: e.tensor_copy(iot[:], iot_i[:]), wait=["c_id"],
         inc=[("c_id", 1)])

    def emit_layernorm(tg, sqdump, xn_tm, first=False, spill=None):
        xnm = []
        s_sum, s_sq = "ln_s", "ln_q"
        s_v, s_r = "ln_v", "ln_r"
        s_tm, s_tr = "ln_tm", "ln_tr"
        s_evS, s_evV = "ln_evS", "ln_evV"
        p.on("scalar", lambda e: e.memzero(sumsq[:]), inc=[(s_sq, 1)])
        for t in range(NT):
            p.on("vector", lambda e, t=t: e.reduce_sum(
                sums[:, t:t + 1], x_res[:, ts(t, D)], axis=AXX),
                wait=(["d_x"] if (first and t == 0) else None), inc=[(s_sum, 1)])
            p.on("scalar", lambda e, t=t: e.activation(
                out=sqdump[:], in_=x_res[:, ts(t, D)], func=AF.Square,
                accum_out=sumsq[:, t:t + 1]),
                wait=(["d_x"] if (first and t == 0) else None), inc=[(s_sq, 1)])
        p.wait("vector", [s_sum, s_sq])
        p.on("vector", lambda e: e.tensor_scalar_mul(m_sb[:], sums[:], 1.0 / D))
        p.on("vector", lambda e: e.tensor_scalar_mul(stat_a[:], sumsq[:], 1.0 / D))
        p.vf()
        p.on("vector", lambda e: e.tensor_mul(stat_b[:], m_sb[:], m_sb[:]))
        p.vf()
        p.on("vector", lambda e: e.tensor_sub(stat_a[:], stat_a[:], stat_b[:]))
        p.vf()
        p.on("vector", lambda e: e.tensor_scalar_add(stat_a[:], stat_a[:], 1e-5),
             inc=[(s_v, 1)])
        p.on("vector", lambda e: e.tensor_copy(dbg_s1[:], stat_a[:]))
        p.on("scalar", lambda e: e.sqrt(stat_b[:], stat_a[:]), wait=[s_v],
             inc=[(s_v, 1)])
        p.on("vector", lambda e: e.reciprocal(r_sb[:], stat_b[:]), wait=[s_v],
             inc=[("rcp", 1)])
        p.wait("vector", ["rcp"])
        p.on("vector", lambda e: e.tensor_mul(stat_b[:], r_sb[:], r_sb[:]))
        p.vf()
        p.on("vector", lambda e: e.tensor_mul(stat_b[:], stat_b[:], stat_a[:]))
        p.vf()
        p.on("vector", lambda e: e.tensor_scalar(
            stat_b[:], stat_b[:], -0.5, 1.5, ALU.mult, ALU.add))
        p.vf()
        p.on("vector", lambda e: e.tensor_mul(r_sb[:], r_sb[:], stat_b[:]),
             inc=[(s_r, 1)])
        for t in range(NT):
            buf = xn_tm[:, ts(t % 2, D)]
            we = [(s_tr, p.cnt(s_tr) - ND)] if t >= 2 else []
            if spill is not None and t >= 2:
                we = we + [("d_xn", xnm[t - 2])]
            p.on("vector", lambda e, t=t, buf=buf: e.tensor_scalar(
                buf, x_res[:, ts(t, D)], m_sb[:, t:t + 1], r_sb[:, t:t + 1],
                ALU.subtract, ALU.mult),
                wait=([s_r] if t == 0 else None), wait_exact=we, inc=[(s_tm, 1)])
            if spill is not None:
                p.on("sync", lambda e, t=t, buf=buf: e.dma_start(
                    out=spill.ap()[t], in_=buf),
                    wait_exact=[(s_tm, p.cnt(s_tm))], inc=[("d_xn", 16)])
                xnm.append(p.cnt("d_xn"))
            for dc in range(ND):
                k = t * ND + dc
                pb = PB[k % 2]
                sev = s_evS if k % 2 == 0 else s_evV
                we2 = [(sev, p.cnt(sev))] if k >= 2 else []
                p.on("tensor", lambda e, buf=buf, dc=dc, pb=pb: e.matmul(
                    pb[:, 0:128], buf[:, ts(dc, 128)], ident[:],
                    start=True, stop=True),
                    wait_exact=[(s_tm, p.cnt(s_tm))] + we2, inc=[(s_tr, 1)])
                dst_lo = dc * S + t * 128
                if k % 2 == 0:
                    p.on("scalar", lambda e, pb=pb, dst_lo=dst_lo: e.activation(
                        out=xn_fm[:, dst_lo:dst_lo + 128], in_=pb[:, 0:128],
                        func=AF.Identity),
                        wait_exact=[(s_tr, p.cnt(s_tr))], inc=[(s_evS, 1)])
                else:
                    p.on("vector", lambda e, pb=pb, dst_lo=dst_lo: e.tensor_copy(
                        xn_fm[:, dst_lo:dst_lo + 128], pb[:, 0:128]),
                        wait_exact=[(s_tr, p.cnt(s_tr))], inc=[(s_evV, 1)])

    def emit_attn_l0(st):
        WT = st["WT"]; wstage = st["wstage"]
        q_pair = st["q_pair"]; k_pair = st["k_pair"]
        v_aug = st["v_aug"]; ctx_pair = st["ctx_pair"]
        e_sb = st["e_sb"]
        ones_row = st["ones_row"]; bv_row = st["bv_row"]; bo_row = st["bo_row"]
        alibi = st["alibi"]; slopes_bc = st["slopes_bc"]

        for nm, dst in (("L0_bv", bv_row), ("L0_bo", bo_row)):
            p.on("gpsimd", lambda e, nm=nm, dst=dst: e.dma_start(
                out=dst.ap(), in_=dram[nm].ap().rearrange("(o d) -> o d", o=1)),
                inc=[("d_smg", 16)])
        p.on("vector", lambda e: e.memset(ones_row[:], 1.0), inc=[("c_one", 1)])
        p.on("tensor", lambda e: e.matmul(
            PB[4][:, 0:NHEAD], ones_row[0:1, 0:128], slopes_sb[:],
            start=True, stop=True),
            wait=["d_smg", "c_one", "c_id"], inc=[("c_alibi", 1)])
        p.on("vector", lambda e: e.tensor_copy(slopes_bc[:], PB[4][:, 0:NHEAD]),
             wait_exact=[("c_alibi", p.cnt("c_alibi"))], inc=[("c_alibi", 1)])
        p.vf()
        for h in range(NHEAD):
            p.on("vector", lambda e, h=h: e.tensor_mul(
                alibi[:, h * NT:(h + 1) * NT], iot[:],
                slopes_bc[:, h:h + 1].broadcast_to([128, NT])),
                inc=[("c_alibi", 1)])

        for wi, w in enumerate(("Wq", "Wk", "Wv", "Wo")):
            wst_marks = []
            for half in range(2):
                p.on("gpsimd", lambda e, w=w, half=half: e.dma_start(
                    out=wstage.ap().rearrange("p (c i) -> p c i", i=D),
                    in_=dram["L0_" + w].ap().rearrange("(c p) i -> p c i", p=128)
                        [:, half * 4:(half + 1) * 4, :]),
                    wait_exact=[("wt_tr", wi * 64 + half * 32)],
                    inc=[("d_wstage", 16)])
                wst_marks.append(p.cnt("d_wstage"))
            for oc in range(ND):
                for ic in range(ND):
                    k = wi * 64 + oc * ND + ic
                    pb = PB[2 + (k % 2)]
                    sev = "wt_evS" if k % 2 == 0 else "wt_evV"
                    we = [(sev, p.cnt(sev))] if k >= 2 else []
                    p.on("tensor", lambda e, oc=oc, ic=ic, pb=pb: e.matmul(
                        pb[:, 0:128],
                        wstage[:, (oc % 4) * D + ic * 128: (oc % 4) * D + (ic + 1) * 128],
                        ident[:], start=True, stop=True),
                        wait_exact=[("d_wstage", wst_marks[oc // 4])] + we,
                        inc=[("wt_tr", 1)])
                    dst = WT[w][:, ic * D + oc * 128: ic * D + (oc + 1) * 128]
                    if k % 2 == 0:
                        p.on("scalar", lambda e, pb=pb, dst=dst: e.activation(
                            out=dst, in_=pb[:, 0:128], func=AF.Identity),
                            wait_exact=[("wt_tr", p.cnt("wt_tr"))],
                            inc=[("wt_evS", 1), ("wt_done", 1)])
                    else:
                        p.on("vector", lambda e, pb=pb, dst=dst: e.tensor_copy(
                            dst, pb[:, 0:128]),
                            wait_exact=[("wt_tr", p.cnt("wt_tr"))],
                            inc=[("wt_evV", 1), ("wt_done", 1)])

        for pr in range(NHEAD // 2):
            for name, dstt, bias, pbq in (("Wq", q_pair, bq_sb, PB[0]),
                                          ("Wk", k_pair, bk_sb, PB[1])):
                for tc in range(4):
                    for kc in range(ND):
                        wex = []
                        if tc == 0 and kc == 0:
                            wex = [("wt_done", (1 if name == "Wq" else 2) * 64),
                                   ("ex_0", p.cnt("ex_0")), ("ex_1", p.cnt("ex_1")),
                                   ("wo_ev", p.cnt("wo_ev"))]
                            if dbg:
                                wex += [("dbgd", p.cnt("dbgd"))]
                        elif kc == 0:
                            wex = [("qk_ev", p.cnt("qk_ev"))]
                        p.on("tensor", lambda e, pbq=pbq, name=name, kc=kc, pr=pr, tc=tc: e.matmul(
                            pbq[:], WT[name][:, kc * D + pr * 128: kc * D + (pr + 1) * 128],
                            xn_fm[:, kc * S + tc * 512: kc * S + (tc + 1) * 512],
                            start=(kc == 0), stop=(kc == ND - 1)),
                            wait_exact=wex,
                            inc=([("qk_mm", 1)] if kc == ND - 1 else None))
                    p.on("scalar", lambda e, pbq=pbq, dstt=dstt, bias=bias, pr=pr, tc=tc: e.activation(
                        out=dstt[:, tc * 512:(tc + 1) * 512], in_=pbq[:],
                        func=AF.Identity, bias=bias[:, pr:pr + 1]),
                        wait=(["d_sm"] if (pr == 0 and tc == 0) else None),
                        wait_exact=[("qk_mm", p.cnt("qk_mm"))], inc=[("qk_ev", 1)])
            p.on("vector", lambda e: e.memset(
                v_aug.ap().rearrange("p (t c) -> p t c", c=130)[:, :, 64:65], 1.0),
                wait_exact=[("cx_0", p.cnt("cx_0")), ("cx_1", p.cnt("cx_1"))],
                inc=[("v_ones", 1)])
            p.on("vector", lambda e: e.memset(
                v_aug.ap().rearrange("p (t c) -> p t c", c=130)[:, :, 129:130], 1.0),
                inc=[("v_ones", 1)])
            for tt in range(NT):
                pb = PB[6 + tt % 2]
                for kc in range(ND):
                    wex = []
                    if tt == 0 and kc == 0:
                        wex = [("wt_done", 3 * 64),
                               ("cx_0", p.cnt("cx_0")), ("cx_1", p.cnt("cx_1"))]
                    elif kc == 0 and tt >= 2:
                        wex = [("v_ev", p.cnt("v_ev") - 1)]
                    p.on("tensor", lambda e, pb=pb, kc=kc, tt=tt, pr=pr: e.matmul(
                        pb[:, 0:128],
                        xn_fm[:, kc * S + tt * 128: kc * S + (tt + 1) * 128],
                        WT["Wv"][:, kc * D + pr * 128: kc * D + (pr + 1) * 128],
                        start=(kc == 0), stop=False), wait_exact=wex)
                p.on("tensor", lambda e, pb=pb, pr=pr: e.matmul(
                    pb[:, 0:128], ones_row[0:1, 0:128],
                    bv_row[0:1, pr * 128:(pr + 1) * 128], start=False, stop=True),
                    wait=((["d_smg", "c_one"]) if pr == 0 and tt == 0 else None),
                    inc=[("v_mm", 1)])
                p.on("vector", lambda e, pb=pb, tt=tt: e.tensor_copy(
                    v_aug.ap().rearrange("p (t c) -> p t c", c=130)
                         [:, tt:tt + 1, :].rearrange("p t (h c) -> p (t h) c", c=65)
                         [:, :, 0:64],
                    pb[:, 0:128].rearrange("p (h c) -> p h c", c=64)),
                    wait_exact=[("v_mm", p.cnt("v_mm"))], inc=[("v_ev", 1)])
            pending_evac = []
            for qc in range(4):
                HHL = []
                for hh in range(2):
                    HHL.append(dict(
                        hh=hh, hr=slice(hh * 64, hh * 64 + 64), h=pr * 2 + hh,
                        pb_s=(PB[0 + hh], PB[2 + hh]), pb_ctx=PB[4 + hh],
                        pb_bc=PB[6 + hh], ctxr=st["ctxr"][hh],
                        r_row=st["r_row"][hh], exm={}))

                def _pv2(L, kc, exc, qc=qc):
                    hh = L["hh"]
                    ex_s, pvr, cp_s = "ex_%d" % hh, "pvr_%d" % hh, "cp_%d" % hh
                    pb_ctx = L["pb_ctx"]
                    eb = e_sb[:, (hh * 4 + kc % 4) * 512:(hh * 4 + kc % 4 + 1) * 512]
                    wex2 = [(ex_s, exc)]
                    if kc == 0:
                        wex2 += [(cp_s, p.cnt(cp_s))]
                        if qc <= 1:
                            wex2 += [("v_ev", p.cnt("v_ev"))]
                    p.on("tensor", lambda e, pb_ctx=pb_ctx, eb=eb, kc=kc, hh=hh: e.matmul(
                        pb_ctx[0:65, :],
                        v_aug[:, kc * 130 + hh * 65: kc * 130 + hh * 65 + 65],
                        eb, start=(kc == 0), stop=(kc == NT - 1)),
                        wait_exact=wex2, inc=[(pvr, 1)])

                for kc in range(NT):
                    for L in HHL:
                        hh = L["hh"]
                        sc_s, ex_s = "sc_%d" % hh, "ex_%d" % hh
                        pbs = L["pb_s"][kc % 2]
                        wex = [(ex_s, max(0, p.cnt(ex_s) - 1))]
                        if kc == 0 and qc == 0:
                            wex += [("qk_ev", p.cnt("qk_ev")),
                                    ("v_ev", p.cnt("v_ev")),
                                    ("v_ones", p.cnt("v_ones")),
                                    ("c_alibi", 18)]
                        p.on("tensor", lambda e, pbs=pbs, hr=L["hr"], kc=kc, qc=qc: e.matmul(
                            pbs[:], k_pair[hr, ts(kc, 128)], q_pair[hr, ts(qc, 512)],
                            start=True, stop=True),
                            wait_exact=wex, inc=[(sc_s, 1)])
                        eb = e_sb[:, (hh * 4 + kc % 4) * 512:(hh * 4 + kc % 4 + 1) * 512]
                        p.on("scalar", lambda e, pbs=pbs, eb=eb, h=L["h"], kc=kc: e.activation(
                            out=eb, in_=pbs[:], func=AF.Exp, scale=0.125,
                            bias=alibi[:, h * NT + kc: h * NT + kc + 1]),
                            wait_exact=[(sc_s, p.cnt(sc_s))], inc=[(ex_s, 1)])
                        L["exm"][kc] = p.cnt(ex_s)
                    if kc >= 1:
                        for L in HHL:
                            _pv2(L, kc - 1, L["exm"][kc - 1])
                for L in HHL:
                    _pv2(L, NT - 1, L["exm"][NT - 1])

                for L in HHL:
                    hh = L["hh"]
                    pb_ctx = L["pb_ctx"]; pb_bc = L["pb_bc"]
                    ctxr = L["ctxr"]; r_row = L["r_row"]
                    pvr, cx_s = "pvr_%d" % hh, "cx_%d" % hh
                    cp_s = "cp_%d" % hh
                    p.on("vector", lambda e, pb_ctx=pb_ctx, ctxr=ctxr: e.tensor_copy(
                        ctxr[:], pb_ctx[0:65, :]),
                        wait_exact=[(pvr, p.cnt(pvr)),
                                    (cx_s, p.cnt(cx_s))], inc=[(cp_s, 1)])
                    p.vf()
                    p.on("vector", lambda e, ctxr=ctxr: e.reciprocal(
                        ctxr[64:65, :], ctxr[64:65, :]), inc=[("rcp", 1)])
                    p.wait("vector", ["rcp"])
                    p.on("vector", lambda e, ctxr=ctxr, r_row=r_row: e.tensor_copy(
                        r_row[:], ctxr[64:65, :]), inc=[("rc_%d" % hh, 1)])

                    def _evac(hh=hh, qc=qc, pb_bc=pb_bc, ctxr=ctxr, r_row=r_row,
                              rc_t=p.cnt("rc_%d" % hh)):
                        cx_s2 = "cx_%d" % hh
                        p.on("tensor", lambda e: e.matmul(
                            pb_bc[0:64, :], ones_row[0:1, 0:64], r_row[:],
                            start=True, stop=True),
                            wait_exact=[("rc_%d" % hh, rc_t),
                                        (cx_s2, p.cnt(cx_s2))],
                            inc=[("rb_%d" % hh, 1)])
                        p.on("vector", lambda e: e.tensor_mul(
                            ctx_pair[slice(hh * 64, hh * 64 + 64), ts(qc, 512)],
                            ctxr[0:64, :], pb_bc[0:64, :]),
                            wait_exact=[("rb_%d" % hh, p.cnt("rb_%d" % hh))],
                            inc=[(cx_s2, 1)])
                    pending_evac.append(_evac)
                    if len(pending_evac) > 1:
                        pending_evac.pop(0)()
            while pending_evac:
                pending_evac.pop(0)()
            if dbg and pr == 0:
                p.on("sync", lambda e: e.dma_start(
                    out=dbg_t["wt"].ap(), in_=WT["Wq"].ap()),
                    wait_exact=[("wt_done", 64)], inc=[("dbgd", 16)])
                p.on("sync", lambda e: e.dma_start(
                    out=dbg_t["qp"].ap(), in_=q_pair.ap()),
                    wait_exact=[("qk_ev", p.cnt("qk_ev"))], inc=[("dbgd", 16)])
                p.on("sync", lambda e: e.dma_start(
                    out=dbg_t["kp"].ap(), in_=k_pair.ap()), inc=[("dbgd", 16)])
                p.on("sync", lambda e: e.dma_start(
                    out=dbg_t["va"].ap(), in_=v_aug.ap()),
                    wait_exact=[("v_ev", p.cnt("v_ev"))], inc=[("dbgd", 16)])
                p.on("sync", lambda e: e.dma_start(
                    out=dbg_t["ctxp"].ap(), in_=ctx_pair.ap()),
                    wait_exact=[("cx_0", p.cnt("cx_0")), ("cx_1", p.cnt("cx_1"))],
                    inc=[("dbgd", 16)])
            for tt in range(NT):
                for oc in range(2):
                    j = tt * 2 + oc
                    pbo = PB[j % 2]
                    wex = []
                    if j == 0:
                        wex = [("cx_0", p.cnt("cx_0")), ("cx_1", p.cnt("cx_1")),
                               ("wt_done", 4 * 64),
                               ("ex_0", p.cnt("ex_0")), ("ex_1", p.cnt("ex_1"))]
                    elif j >= 2:
                        wex = [("wo_ev", p.cnt("wo_ev") - 1)]
                    if pr == 0:
                        p.on("tensor", lambda e, pbo=pbo, oc=oc: e.matmul(
                            pbo[:], ones_row[0:1, 0:128],
                            bo_row[0:1, oc * 512:(oc + 1) * 512],
                            start=True, stop=False),
                            wait=(["d_smg"] if j == 0 else None), wait_exact=wex)
                        wex = []
                    p.on("tensor", lambda e, pbo=pbo, tt=tt, oc=oc, pr=pr: e.matmul(
                        pbo[:], ctx_pair[:, ts(tt, 128)],
                        WT["Wo"][:, pr * D + oc * 512: pr * D + (oc + 1) * 512],
                        start=(pr != 0), stop=True),
                        wait_exact=wex, inc=[("wo_mm", 1)])
                    p.on("vector", lambda e, pbo=pbo, tt=tt, oc=oc: e.scalar_tensor_tensor(
                        x_res[:, tt * D + oc * 512: tt * D + (oc + 1) * 512],
                        pbo[:], 1.0,
                        x_res[:, tt * D + oc * 512: tt * D + (oc + 1) * 512],
                        ALU.mult, ALU.add),
                        wait_exact=[("wo_mm", p.cnt("wo_mm"))], inc=[("wo_ev", 1)])

    def emit_moe(st, filler=None):
        gw_stage = st["gw_stage"]; gwT = st["gwT"]
        logit_sb = st["logit_sb"]; comb = st["comb"]
        g_t0 = st["g_t0"]; g_t1 = st["g_t1"]; g_c0 = st["g_c0"]; g_c1 = st["g_c1"]
        w1g = st["w1g"]; w3g = st["w3g"]; w2g = st["w2g"]
        h_sb = st["h_sb"]
        hsil = st["hsil"]
        tri_sb = st["tri_sb"]; stri_sb = st["stri_sb"]; iota_sb = st["iota_sb"]
        route_m = st["route_m"]; t_sb = st["t_sb"]; pos = st["pos"]
        onesb = st["onesb"]; ccol_sb = st["ccol_sb"]; base_sb = st["base_sb"]
        pt_sb = st["pt_sb"]; ptT_sb = st["ptT_sb"]
        xt_sb = st["xt_sb"]; gth_sb = st["gth_sb"]
        gath_fm = st["gath_fm"]; y_sb = st["y_sb"]
        yb = gath_fm

        p.on("gpsimd", lambda e: e.dma_start(
            out=gw_stage[0:MOE_E, :], in_=dram["gate_w"].ap()),
            inc=[("d_gw", 16)])
        gw_mark = p.cnt("d_gw")
        for kc in range(ND):
            p.on("tensor", lambda e, kc=kc: e.matmul(
                PB[0][:, 0:8], gw_stage[0:MOE_E, ts(kc, 128)], ident[0:8, 0:8],
                start=True, stop=True),
                wait_exact=([("d_gw", gw_mark)] if kc == 0 else
                            [("gw_ev", p.cnt("gw_ev"))]),
                inc=[("gw_tr", 1)])
            p.on("vector", lambda e, kc=kc: e.tensor_copy(
                gwT[:, kc * 8:(kc + 1) * 8], PB[0][:, 0:8]),
                wait_exact=[("gw_tr", p.cnt("gw_tr"))], inc=[("gw_ev", 1)])
        for tt in range(NT):
            for kc in range(ND):
                p.on("tensor", lambda e, tt=tt, kc=kc: e.matmul(
                    PB[1][:, 0:8], xn_fm[:, kc * S + tt * 128: kc * S + (tt + 1) * 128],
                    gwT[:, kc * 8:(kc + 1) * 8],
                    start=(kc == 0), stop=(kc == ND - 1)),
                    wait_exact=([("gw_ev", p.cnt("gw_ev"))] if (tt == 0 and kc == 0) else
                                ([("gl_ev", p.cnt("gl_ev"))] if kc == 0 else [])),
                    inc=([("gl_mm", 1)] if kc == ND - 1 else None))
            p.on("vector", lambda e: e.tensor_copy(logit_sb[:], PB[1][:, 0:8]),
                 wait_exact=[("gl_mm", p.cnt("gl_mm"))], inc=[("gl_ev", 1)])
            p.vf()
            p.on("vector", lambda e: e.reduce_max(g_c0[:], logit_sb[:], axis=AXX))
            p.vf()
            p.on("vector", lambda e: e.tensor_scalar_mul(g_c0[:], g_c0[:], -1.0),
                 inc=[("g_nm", 1)])
            p.on("scalar", lambda e: e.activation(
                out=g_t0[:], in_=logit_sb[:], func=AF.Exp, bias=g_c0[:]),
                wait_exact=[("g_nm", p.cnt("g_nm"))], inc=[("g_ex", 1)])
            p.on("vector", lambda e: e.reduce_sum(g_c0[:], g_t0[:], axis=AXX),
                 wait_exact=[("g_ex", p.cnt("g_ex"))])
            p.vf()
            p.on("vector", lambda e: e.reciprocal(g_c1[:], g_c0[:]),
                 inc=[("rcp", 1)])
            p.wait("vector", ["rcp"])
            p.on("vector", lambda e: e.tensor_scalar_mul(g_t0[:], g_t0[:], g_c1[:]))
            p.vf()
            p.on("vector", lambda e: e.reduce_max(g_c0[:], g_t0[:], axis=AXX))
            p.vf()
            p.on("vector", lambda e: e.tensor_single_scalar(
                g_t1[:], g_t0[:], g_c0[:], ALU.is_lt))
            p.vf()
            p.on("vector", lambda e: e.tensor_mul(g_t1[:], g_t0[:], g_t1[:]))
            p.vf()
            p.on("vector", lambda e: e.reduce_max(g_c1[:], g_t1[:], axis=AXX))
            p.vf()
            p.on("vector", lambda e: e.tensor_single_scalar(
                g_t1[:], g_t0[:], g_c1[:], ALU.is_ge))
            p.vf()
            p.on("vector", lambda e, tt=tt: e.tensor_mul(
                comb[:, tt * 8:(tt + 1) * 8], g_t0[:], g_t1[:]),
                inc=[("comb_d", 1)])

        for nm, dst in (("tri", tri_sb), ("stri", stri_sb), ("iotaw", iota_sb)):
            p.on("sync", lambda e, nm=nm, dst=dst: e.dma_start(
                out=dst.ap(), in_=dram[nm].ap()), inc=[("d_ct", 16)])
        ct_mark = p.cnt("d_ct")
        p.on("vector", lambda e: e.memset(onesb[:], 1.0), inc=[("rt", 1)])
        p.on("vector", lambda e: e.tensor_scalar(
            route_m[:], comb[:], 0.0, 0.0, ALU.is_gt, ALU.bypass),
            wait_exact=[("comb_d", 16), ("d_ct", ct_mark)], inc=[("rt", 1)])
        p.on("tensor", lambda e: e.matmul(PB[0][:, 0:128], tri_sb[:], route_m[:],
                                          start=True, stop=True),
             wait_exact=[("rt", p.cnt("rt")), ("dq_ev", p.cnt("dq_ev"))],
             inc=[("g_tr", 1)])
        p.on("tensor", lambda e: e.matmul(PB[1][:, 0:1], route_m[:], onesb[:],
                                          start=True, stop=True),
             wait_exact=[("gl_ev", p.cnt("gl_ev"))], inc=[("g_tr", 1)])
        p.on("vector", lambda e: e.tensor_sub(t_sb[:], PB[0][:, 0:128], route_m[:]),
             wait_exact=[("g_tr", p.cnt("g_tr"))], inc=[("rt", 1)])
        p.on("vector", lambda e: e.tensor_copy(ccol_sb[:], PB[1][:, 0:1]),
             inc=[("rt", 1)])
        p.on("tensor", lambda e: e.matmul(PB[2][0:1, 0:128], ccol_sb[:], stri_sb[:],
                                          start=True, stop=True),
             wait_exact=[("rt", p.cnt("rt"))], inc=[("g_tr", 1)])
        p.on("vector", lambda e: e.tensor_copy(base_sb[0:1, :], PB[2][0:1, 0:128]),
             wait_exact=[("g_tr", p.cnt("g_tr"))], inc=[("rt", 1)])
        p.on("tensor", lambda e: e.matmul(
            PB[3][:, 0:128], onesf[0:1, 0:128], base_sb[0:1, :],
            start=True, stop=True),
             wait_exact=[("rt", p.cnt("rt"))], inc=[("g_tr", 1)])
        p.on("vector", lambda e: e.tensor_add(t_sb[:], t_sb[:], PB[3][:, 0:128]),
             wait_exact=[("g_tr", p.cnt("g_tr"))])
        p.vf()
        p.on("vector", lambda e: e.scalar_tensor_tensor(
            pos[:], t_sb[:], 1.0, route_m[:], ALU.add, ALU.mult))
        p.vf()
        p.on("vector", lambda e: e.tensor_scalar_add(pos[:], pos[:], -1.0),
             inc=[("pos_d", 1)])
        pmark = p.cnt("pos_d")
        p.vf()

        CAP = 768
        state = {"drained": 0, "gi": 0, "wy": 0, "hseq": 0}
        fh = [(0, 0), (0, 0)]
        NGRP = 8 * 8 + 4

        def drain():
            tgt = (144 * (state["gi"]) + NGRP - 1) // NGRP
            while filler is not None and state["drained"] < tgt:
                if next(filler, None) is None and state["drained"] < 144:
                    pass
                state["drained"] += 1

        for e_i in range(MOE_E):
            for dc in range(2):
                for tt in range(NT):
                    s = e_i * 32 + dc * NT + tt
                    ptb = pt_sb[tt % 2]
                    xt = xt_sb[tt % 2]
                    wex = [("gstep", s - 1)] if s >= 2 else [("d_xn", 16 * NT)]
                    p.on("sync", lambda e, xt=xt, tt=tt, dc=dc: e.dma_start(
                        out=xt[:], in_=scr["xn"].ap()[tt][:, dc * 512:(dc + 1) * 512]),
                        wait_exact=wex, inc=[("d_xt", 16)])
                    xtmark = p.cnt("d_xt")
                    for sc in range(6):
                        wex = []
                        if sc == 0:
                            if s >= 2:
                                wex = [("gstep", s - 1)]
                            if dc == 0 and tt == 0:
                                wex += [("pos_d", pmark), ("s_tr", p.cnt("s_tr"))]
                        p.on("vector", lambda e, ptb=ptb, sc=sc, tt=tt, e_i=e_i:
                             e.tensor_single_scalar(
                                 ptb[:, ts(sc, 128)], iota_sb[:, ts(sc, 128)],
                                 pos[:, tt * 8 + e_i: tt * 8 + e_i + 1], ALU.is_equal),
                             wait_exact=wex, inc=([("g_pt", 1)] if sc == 5 else None))
                    for sc in range(6):
                        wex = []
                        if sc == 0:
                            wex = [("g_pt", p.cnt("g_pt")), ("d_xt", xtmark)]
                            if tt == 0 and dc == 0:
                                wex += [("my_ev", p.cnt("my_ev")),
                                        ("mh_ev", p.cnt("mh_ev")),
                                        ("mh_sil", p.cnt("mh_sil"))]
                            if tt == 0 and dc == 1:
                                wex += [("g_ev", p.cnt("g_ev"))]
                        p.on("tensor", lambda e, ptb=ptb, sc=sc, tt=tt, dc=dc, xt=xt:
                             e.matmul(PB[sc][:, 0:512], ptb[:, ts(sc, 128)], xt[:],
                                      start=(tt == 0), stop=(tt == NT - 1)),
                             wait_exact=wex,
                             inc=([("gstep", 1)] if sc == 5 else None))
                evmark = {}
                for sc in range(7):
                    if sc < 6:
                        g = gth_sb[sc % 2]
                        jl = dc * 6 + sc
                        wex = [("gstep", p.cnt("gstep"))]
                        if jl >= 2:
                            wex += [("g_gf", p.cnt("g_gf"))]
                        p.on("vector", lambda e, g=g, sc=sc: e.tensor_copy(
                            g[:], PB[sc][:, 0:512]), wait_exact=wex, inc=[("g_ev", 1)])
                        evmark[sc] = p.cnt("g_ev")
                    if sc >= 1:
                        sp = sc - 1
                        g = gth_sb[sp % 2]
                        jl = dc * 6 + sp
                        for ds in range(4):
                            k4 = jl * 4 + ds
                            pbt = PB[6 + k4 % 2]
                            wex = [("g_ev", evmark[sp])]
                            if jl == 0 and ds == 0:
                                wex += [("dq_ev", p.cnt("dq_ev")),
                                        ("mh_ev", p.cnt("mh_ev")),
                                        ("mh_sil", p.cnt("mh_sil")),
                                        ("g_gf", p.cnt("g_gf"))]
                            elif k4 >= 2:
                                wex += [("g_gf", p.cnt("g_gf") - 1)]
                            p.on("tensor", lambda e, pbt=pbt, g=g, ds=ds: e.matmul(
                                pbt[:, 0:128], g[:, ts(ds, 128)], ident[:],
                                start=True, stop=True),
                                wait_exact=wex, inc=[("g_tr", 1)])
                            kc = dc * 4 + ds
                            p.on("scalar", lambda e, pbt=pbt, kc=kc, sp=sp: e.activation(
                                out=gath_fm[:, kc * CAP + sp * 128: kc * CAP + sp * 128 + 128],
                                in_=pbt[:, 0:128], func=AF.Identity),
                                wait_exact=[("g_tr", p.cnt("g_tr"))], inc=[("g_gf", 1)])

            gplan = [3, 3, 3, 3, 3, 3, 3, 1]
            g0 = 0
            hb = h_sb
            for g, ghc in enumerate(gplan):
                q = state["gi"] % 2
                in1 = dram["w1"].ap()[e_i].rearrange(
                    "(kc p) h -> p kc h", p=128)[:, :, g0 * 128:(g0 + ghc) * 128]
                in3 = dram["w3"].ap()[e_i].rearrange(
                    "(kc p) h -> p kc h", p=128)[:, :, g0 * 128:(g0 + ghc) * 128]
                wex = [("mh1_mm", fh[0][0]), ("mh3_mm", fh[0][1])]
                p.on("gpsimd", lambda e, q=q, ghc=ghc, in1=in1: e.dma_start(
                    out=w1g[q].ap().rearrange("p (kc h) -> p kc h", kc=ND)[:, :, 0:ghc * 128],
                    in_=in1), wait_exact=wex, inc=[("d_wg", 16)])
                p.on("gpsimd", lambda e, q=q, ghc=ghc, in3=in3: e.dma_start(
                    out=w3g[q].ap().rearrange("p (kc h) -> p kc h", kc=ND)[:, :, 0:ghc * 128],
                    in_=in3), inc=[("d_wg", 16)])
                if g % 2 == 0:
                    yhc = ghc + (gplan[g + 1] if g + 1 < len(gplan) else 0)
                    in2 = dram["w2"].ap()[e_i].rearrange(
                        "(hc p) o -> p hc o", p=128)[:, g0:g0 + yhc, :]
                    p.on("gpsimd", lambda e, yhc=yhc, in2=in2: e.dma_start(
                        out=w2g.ap().rearrange("p (hc o) -> p hc o", o=D)[:, 0:yhc, :],
                        in_=in2), wait_exact=[("my_mm", state["wy"])],
                        inc=[("d_wg", 16)])
                dma_tgt = p.cnt("d_wg")
                for tg in range(2):
                    for hcl in range(ghc):
                        hcp = (g % 2) * 3 + hcl
                        hsq = state["hseq"]
                        state["hseq"] += 1
                        pb1 = PB[4 + 2 * (hsq % 2)]
                        pb3 = PB[5 + 2 * (hsq % 2)]
                        for kc in range(ND):
                            wex = [("d_wg", dma_tgt)] if (tg == 0 and hcl == 0 and kc == 0) else []
                            if kc == 0:
                                wex += [("mh_sil", max(0, p.cnt("mh_sil") - 1))]
                                if hsq % 2 == 1:
                                    wex += [("dq_ev", p.cnt("dq_ev")),
                                            ("g_gf", p.cnt("g_gf"))]
                            p.on("tensor", lambda e, pb1=pb1, q=q, kc=kc, hcl=hcl, tg=tg:
                                 e.matmul(pb1[:, 0:384],
                                          w1g[q][:, kc * 384 + hcl * 128: kc * 384 + (hcl + 1) * 128],
                                          gath_fm[:, kc * CAP + tg * 384: kc * CAP + (tg + 1) * 384],
                                          start=(kc == 0), stop=(kc == ND - 1)),
                                 wait_exact=wex,
                                 inc=([("mh1_mm", 1)] if kc == ND - 1 else None))
                        for kc in range(ND):
                            wex = []
                            if kc == 0:
                                wex = [("mh_ev", max(0, p.cnt("mh_ev") - 1))]
                                if hsq % 2 == 1:
                                    wex += [("dq_ev", p.cnt("dq_ev")),
                                            ("g_gf", p.cnt("g_gf"))]
                            p.on("tensor", lambda e, pb3=pb3, q=q, kc=kc, hcl=hcl, tg=tg:
                                 e.matmul(pb3[:, 0:384],
                                          w3g[q][:, kc * 384 + hcl * 128: kc * 384 + (hcl + 1) * 128],
                                          gath_fm[:, kc * CAP + tg * 384: kc * CAP + (tg + 1) * 384],
                                          start=(kc == 0), stop=(kc == ND - 1)),
                                 wait_exact=wex,
                                 inc=([("mh3_mm", 1)] if kc == ND - 1 else None))
                        hs = hsil[hsq % 2]
                        p.on("scalar", lambda e, pb1=pb1, hs=hs: e.activation(
                            out=hs[:, 0:384], in_=pb1[:, 0:384], func=AF.Silu),
                            wait_exact=[("mh1_mm", p.cnt("mh1_mm")),
                                        ("mh_ev", max(0, p.cnt("mh_ev") - 1))],
                            inc=[("mh_sil", 1)])
                        wex = [("mh3_mm", p.cnt("mh3_mm")),
                               ("mh_sil", p.cnt("mh_sil"))]
                        if g % 2 == 0 and tg == 0 and hcl == 0:
                            wex.append(("my_mm", p.cnt("my_mm")))
                        p.on("vector", lambda e, pb3=pb3, hs=hs, hcp=hcp, tg=tg: e.tensor_mul(
                            hb[:, hcp * CAP + tg * 384: hcp * CAP + (tg + 1) * 384],
                            hs[:, 0:384], pb3[:, 0:384]),
                            wait_exact=wex, inc=[("mh_ev", 1)])
                fh = [fh[1], (p.cnt("mh1_mm"), p.cnt("mh3_mm"))]
                g0 += ghc
                state["gi"] += 1
                drain()
                if g % 2 == 1:
                    yp = g // 2
                    yhc = gplan[g - 1] + gplan[g]
                    ysnap = p.cnt("my_ev")
                    ymark = {}
                    for oc in range(2):
                        for m in range(6):
                            k = oc * 6 + m
                            pby = PB[k % 4]
                            for hcp in range(yhc):
                                wex = []
                                if hcp == 0:
                                    if k == 0:
                                        wex = [("mh_ev", p.cnt("mh_ev")),
                                               ("my_ev", ysnap),
                                               ("g_ev", p.cnt("g_ev"))]
                                    elif k >= 4:
                                        wex = [("my_ev", ymark[k - 4])]
                                    else:
                                        wex = [("my_ev", ysnap)]
                                p.on("tensor", lambda e, pby=pby, hcp=hcp, m=m, oc=oc, yhc=yhc:
                                     e.matmul(pby[:, 0:512],
                                              hb[:, hcp * CAP + m * 128: hcp * CAP + (m + 1) * 128],
                                              w2g[:, hcp * D + oc * 512: hcp * D + oc * 512 + 512],
                                              start=(hcp == 0), stop=(hcp == yhc - 1)),
                                     wait_exact=wex,
                                     inc=([("my_mm", 1)] if hcp == yhc - 1 else None))
                            lo = m * D + oc * 512
                            if yp == 0:
                                p.on("vector", lambda e, pby=pby, lo=lo: e.tensor_copy(
                                    y_sb[:, lo:lo + 512], pby[:, 0:512]),
                                    wait_exact=[("my_mm", p.cnt("my_mm"))],
                                    inc=[("my_ev", 1)])
                            elif yp < 3:
                                p.on("vector", lambda e, pby=pby, lo=lo: e.tensor_add(
                                    y_sb[:, lo:lo + 512], y_sb[:, lo:lo + 512],
                                    pby[:, 0:512]),
                                    wait_exact=[("my_mm", p.cnt("my_mm"))],
                                    inc=[("my_ev", 1)])
                            else:
                                p.on("vector", lambda e, pby=pby, lo=lo: e.tensor_add(
                                    yb[:, lo:lo + 512], y_sb[:, lo:lo + 512],
                                    pby[:, 0:512]),
                                    wait_exact=[("my_mm", p.cnt("my_mm"))],
                                    inc=[("my_ev", 1)])
                            ymark[k] = p.cnt("my_ev")
                    state["wy"] = p.cnt("my_mm")

            ssnap = p.cnt("my_ev")
            scmark = {}
            smmark = {}
            strmark = {}
            temark = {}
            for tt in range(NT + 1):
                if tt < NT:
                    ptb = ptT_sb[tt % 2]
                    srcb = pt_sb[tt % 2]
                    for sc in range(6):
                        wex = []
                        if sc == 0:
                            if tt >= 2:
                                wex = [("s_tr", strmark[tt - 2])]
                            else:
                                wex = [("gstep", p.cnt("gstep"))]
                        p.on("vector", lambda e, srcb=srcb, sc=sc, tt=tt, e_i=e_i:
                             e.tensor_single_scalar(
                                 srcb[:, ts(sc, 128)], iota_sb[:, ts(sc, 128)],
                                 pos[:, tt * 8 + e_i: tt * 8 + e_i + 1], ALU.is_equal),
                             wait_exact=wex, inc=([("s_pt", 1)] if sc == 5 else None))
                    for sc in range(6):
                        k6 = tt * 6 + sc
                        pbt = PB[6 + k6 % 2]
                        wex = [("s_pt", p.cnt("s_pt"))] if sc == 0 else []
                        if tt == 0 and sc == 0:
                            wex += [("dq_ev", p.cnt("dq_ev")),
                                    ("mh_ev", p.cnt("mh_ev")),
                                    ("mh_sil", p.cnt("mh_sil")),
                                    ("g_gf", p.cnt("g_gf"))]
                        elif k6 >= 2:
                            wex += [("s_te", p.cnt("s_te") - 1)]
                        p.on("tensor", lambda e, pbt=pbt, srcb=srcb, sc=sc: e.matmul(
                            pbt[:, 0:128], srcb[:, ts(sc, 128)], ident[:],
                            start=True, stop=True),
                            wait_exact=wex, inc=[("s_tr", 1)])
                        wex = [("s_tr", p.cnt("s_tr"))]
                        if sc == 0 and tt >= 2:
                            wex += [("s_mm", smmark[tt - 2])]
                        p.on("scalar", lambda e, pbt=pbt, ptb=ptb, sc=sc: e.activation(
                            out=ptb[:, ts(sc, 128)], in_=pbt[:, 0:128], func=AF.Identity),
                            wait_exact=wex, inc=[("s_te", 1)])
                    strmark[tt] = p.cnt("s_tr")
                    temark[tt] = p.cnt("s_te")
                if tt >= 1:
                    tp = tt - 1
                    ptb = ptT_sb[tp % 2]
                    for dc in range(2):
                        k2 = tp * 2 + dc
                        pbs = PB[k2 % 4]
                        for sc in range(6):
                            wex = []
                            if sc == 0:
                                wex = [("s_te", temark[tp])]
                                if k2 < 4:
                                    wex += [("my_ev", ssnap)]
                                else:
                                    wex += [("my_ev", scmark[k2 - 4])]
                            p.on("tensor", lambda e, pbs=pbs, ptb=ptb, sc=sc, dc=dc:
                                 e.matmul(pbs[:, 0:512], ptb[:, ts(sc, 128)],
                                          yb[:, sc * D + dc * 512: sc * D + dc * 512 + 512],
                                          start=(sc == 0), stop=(sc == 5)),
                                 wait_exact=wex,
                                 inc=([("s_mm", 1)] if sc == 5 else None))
                        p.on("vector", lambda e, pbs=pbs, tp=tp, dc=dc, e_i=e_i:
                             e.scalar_tensor_tensor(
                                 x_res[:, tp * D + dc * 512: tp * D + dc * 512 + 512],
                                 pbs[:, 0:512],
                                 comb[:, tp * 8 + e_i: tp * 8 + e_i + 1],
                                 x_res[:, tp * D + dc * 512: tp * D + dc * 512 + 512],
                                 ALU.mult, ALU.add),
                             wait_exact=[("s_mm", p.cnt("s_mm"))], inc=[("sc_ev", 1)])
                        scmark[k2] = p.cnt("my_ev")
                    smmark[tp] = p.cnt("s_mm")

        g0 = 0
        for gl, ghc in enumerate([3, 3, 3, 3]):
            q = state["gi"] % 2
            in1 = dram["sw1"].ap().rearrange(
                "(kc p) h -> p kc h", p=128)[:, :, g0 * 128:(g0 + ghc) * 128]
            in3 = dram["sw3"].ap().rearrange(
                "(kc p) h -> p kc h", p=128)[:, :, g0 * 128:(g0 + ghc) * 128]
            in2 = dram["sw2"].ap().rearrange(
                "(hc p) o -> p hc o", p=128)[:, g0:g0 + ghc, :]
            wex = [("mh1_mm", fh[0][0]), ("mh3_mm", fh[0][1])]
            p.on("gpsimd", lambda e, q=q, ghc=ghc, in1=in1: e.dma_start(
                out=w1g[q].ap().rearrange("p (kc h) -> p kc h", kc=ND)[:, :, 0:ghc * 128],
                in_=in1), wait_exact=wex, inc=[("d_wg", 16)])
            p.on("gpsimd", lambda e, q=q, ghc=ghc, in3=in3: e.dma_start(
                out=w3g[q].ap().rearrange("p (kc h) -> p kc h", kc=ND)[:, :, 0:ghc * 128],
                in_=in3), inc=[("d_wg", 16)])
            p.on("gpsimd", lambda e, ghc=ghc, in2=in2: e.dma_start(
                out=w2g.ap().rearrange("p (hc o) -> p hc o", o=D)[:, 0:ghc, :],
                in_=in2), wait_exact=[("my_mm", state["wy"])], inc=[("d_wg", 16)])
            dma_tgt = p.cnt("d_wg")
            for tg in range(4):
                for hcl in range(ghc):
                    hsq = state["hseq"]
                    state["hseq"] += 1
                    pb1 = PB[4 + 2 * (hsq % 2)]
                    pb3 = PB[5 + 2 * (hsq % 2)]
                    for kc in range(ND):
                        wex = [("d_wg", dma_tgt)] if (tg == 0 and hcl == 0 and kc == 0) else []
                        if kc == 0:
                            wex += [("mh_sil", max(0, p.cnt("mh_sil") - 1))]
                            if hsq % 2 == 1:
                                wex += [("dq_ev", p.cnt("dq_ev")),
                                        ("g_gf", p.cnt("g_gf"))]
                        p.on("tensor", lambda e, pb1=pb1, q=q, kc=kc, hcl=hcl, tg=tg:
                             e.matmul(pb1[:, 0:512],
                                      w1g[q][:, kc * 384 + hcl * 128: kc * 384 + (hcl + 1) * 128],
                                      xn_fm[:, kc * S + tg * 512: kc * S + (tg + 1) * 512],
                                      start=(kc == 0), stop=(kc == ND - 1)),
                             wait_exact=wex,
                             inc=([("mh1_mm", 1)] if kc == ND - 1 else None))
                    for kc in range(ND):
                        wex = []
                        if kc == 0:
                            wex = [("mh_ev", max(0, p.cnt("mh_ev") - 1))]
                            if hsq % 2 == 1:
                                wex += [("dq_ev", p.cnt("dq_ev")),
                                        ("g_gf", p.cnt("g_gf"))]
                        p.on("tensor", lambda e, pb3=pb3, q=q, kc=kc, hcl=hcl, tg=tg:
                             e.matmul(pb3[:, 0:512],
                                      w3g[q][:, kc * 384 + hcl * 128: kc * 384 + (hcl + 1) * 128],
                                      xn_fm[:, kc * S + tg * 512: kc * S + (tg + 1) * 512],
                                      start=(kc == 0), stop=(kc == ND - 1)),
                             wait_exact=wex,
                             inc=([("mh3_mm", 1)] if kc == ND - 1 else None))
                    hs = hsil[hsq % 2]
                    p.on("scalar", lambda e, pb1=pb1, hs=hs: e.activation(
                        out=hs[:], in_=pb1[:, 0:512], func=AF.Silu),
                        wait_exact=[("mh1_mm", p.cnt("mh1_mm")),
                                    ("mh_ev", max(0, p.cnt("mh_ev") - 1))],
                        inc=[("mh_sil", 1)])
                    wex = [("mh3_mm", p.cnt("mh3_mm")),
                           ("mh_sil", p.cnt("mh_sil"))]
                    if tg >= 1 and hcl == 0:
                        wex.append(("my_mm", p.cnt("my_mm")))
                    if gl == 0 and tg == 0 and hcl == 0:
                        wex.append(("my_mm", p.cnt("my_mm")))
                    p.on("vector", lambda e, pb3=pb3, hs=hs, hcl=hcl: e.tensor_mul(
                        hb[:, hcl * 512:(hcl + 1) * 512], hs[:], pb3[:, 0:512]),
                        wait_exact=wex, inc=[("mh_ev", 1)])
                for oc in range(2):
                    for hcl in range(ghc):
                        for m4 in range(4):
                            pby = PB[m4]
                            wex = []
                            if hcl == 0 and m4 == 0:
                                wex = [("mh_ev", p.cnt("mh_ev")),
                                       ("my_ev", p.cnt("my_ev"))]
                            p.on("tensor", lambda e, pby=pby, hcl=hcl, m4=m4, oc=oc, ghc=ghc:
                                 e.matmul(pby[:, 0:512],
                                          hb[:, hcl * 512 + m4 * 128: hcl * 512 + (m4 + 1) * 128],
                                          w2g[:, hcl * D + oc * 512: hcl * D + (oc + 1) * 512],
                                          start=(hcl == 0), stop=(hcl == ghc - 1)),
                                 wait_exact=wex,
                                 inc=([("my_mm", 1)] if hcl == ghc - 1 else None))
                    for m4 in range(4):
                        tt4 = tg * 4 + m4
                        lo = tt4 * D + oc * 512
                        p.on("vector", lambda e, m4=m4, lo=lo: e.scalar_tensor_tensor(
                            x_res[:, lo:lo + 512], PB[m4][:, 0:512], 1.0,
                            x_res[:, lo:lo + 512], ALU.mult, ALU.add),
                            wait_exact=[("my_mm", p.cnt("my_mm"))],
                            inc=[("my_ev", 1)])
            fh = [fh[1], (p.cnt("mh1_mm"), p.cnt("mh3_mm"))]
            state["wy"] = p.cnt("my_mm")
            g0 += ghc
            state["gi"] += 1
            drain()

    def emit_dequant(st):
        wq_stage = st["wq_stage"]
        dq_s = st["dq_s"]
        dq_si = st["dq_si"]
        dq_t = st["dq_t"]
        dq_i = st["dq_i"]
        dq_w = st["dq_w"]
        tr_stage = st["tr_stage"]

        for wname, wsrc, O, I in (("up", "w_up", L1_HID, D),
                                  ("gate", "w_gate", L1_HID, D),
                                  ("down", "w_down", D, L1_HID)):
            noc = O // 128
            nh = max(1, I // 512)
            IH = I // nh
            nb = IH // 32
            nic = IH // 128
            for oc in range(noc):
                for ih in range(nh):
                    p.on("sync", lambda e, wsrc=wsrc, oc=oc, ih=ih, IH=IH: e.dma_start(
                        out=wq_stage[:, 0:IH],
                        in_=dram[wsrc].ap().rearrange("(c p) i -> p c i", p=128)
                            [:, oc, ih * IH:(ih + 1) * IH]),
                        wait_exact=[("dq_done", max(0, p.cnt("dq_done") - 1))],
                        inc=[("d_dq", 16)])
                    w3d = wq_stage.ap().rearrange("p (b k) -> p b k", k=32)[:, 0:nb, :]
                    p.on("vector", lambda e, w3d=w3d, nb=nb: e.reduce_max(
                        dq_s.ap().rearrange("p (b k) -> p b k", k=1)[:, 0:nb, :], w3d,
                        axis=AXX, apply_absolute_value=True),
                        wait_exact=[("d_dq", p.cnt("d_dq"))])
                    p.vf()
                    p.on("vector", lambda e, nb=nb: e.tensor_scalar(
                        dq_s[:, 0:nb], dq_s[:, 0:nb], 1.0 / 7.0, 1e-12, ALU.mult, ALU.max))
                    p.vf()
                    p.on("vector", lambda e, nb=nb: e.reciprocal(dq_si[:, 0:nb], dq_s[:, 0:nb]),
                         inc=[("rcp", 1)])
                    p.wait("vector", ["rcp"])
                    p.on("vector", lambda e, w3d=w3d, nb=nb: e.tensor_mul(
                        dq_t.ap().rearrange("p (b k) -> p b k", k=32)[:, 0:nb, :], w3d,
                        dq_si.ap().rearrange("p (b k) -> p b k", k=1)[:, 0:nb, :]
                        .broadcast_to([128, nb, 32])))
                    p.vf()
                    p.on("vector", lambda e, IH=IH: e.tensor_scalar_add(
                        dq_i[:, 0:IH], dq_t[:, 0:IH], 8.5))
                    p.vf()
                    p.on("vector", lambda e, nb=nb: e.scalar_tensor_tensor(
                        dq_w.ap().rearrange("p (b k) -> p b k", k=32)[:, 0:nb, :],
                        dq_i.ap().rearrange("p (b k) -> p b k", k=32)[:, 0:nb, :],
                        -8.0,
                        dq_s.ap().rearrange("p (b k) -> p b k", k=1)[:, 0:nb, :]
                        .broadcast_to([128, nb, 32]),
                        ALU.add, ALU.mult), inc=[("dq_w", 1)])
                    for ic in range(nic):
                        k = p.cnt("dq_tr")
                        pb = PB[6 + k % 2]
                        we = [("dq_ev", max(0, p.cnt("dq_ev") - 1))]
                        if ic == 0:
                            we += [("mh_ev", p.cnt("mh_ev")),
                                   ("mh_sil", p.cnt("mh_sil")),
                                   ("g_gf", p.cnt("g_gf"))]
                        p.on("tensor", lambda e, pb=pb, ic=ic: e.matmul(
                            pb[:, 0:128], dq_w[:, ts(ic, 128)], ident[:],
                            start=True, stop=True),
                            wait_exact=[("dq_w", p.cnt("dq_w"))] + we,
                            inc=[("dq_tr", 1)])
                        p.on("scalar", lambda e, pb=pb, ic=ic: e.activation(
                            out=tr_stage[:, ts(ic, 128)], in_=pb[:, 0:128],
                            func=AF.Identity),
                            wait_exact=[("dq_tr", p.cnt("dq_tr"))], inc=[("dq_ev", 1)])
                    p.on("sync", lambda e, wname=wname, oc=oc, ih=ih, nic=nic: e.dma_start(
                        out=scr[wname].ap()[ih * nic:(ih + 1) * nic, :,
                                            oc * 128:(oc + 1) * 128]
                            .rearrange("c p o -> p c o"),
                        in_=tr_stage.ap().rearrange("p (c o) -> p c o", o=128)[:, 0:nic, :]),
                        wait_exact=[("dq_ev", p.cnt("dq_ev"))], inc=[("d_dqo", 16)])
                    p.on("sync", None, wait_exact=[("d_dqo", p.cnt("d_dqo"))])
                    p.on("sync", lambda e: e.nop(), inc=[("dq_done", 1)])
                    yield

    def emit_attn_l1(st):
        wl1 = st["wl1"]
        fmA = st["fmA"]
        fmB = st["fmB"]
        fres = st["fres"]
        fims = st["fims"]
        sre = st["sre"]
        sim = st["sim"]
        tA = st["tA"]; tB = st["tB"]; u1 = st["u1"]; u2 = st["u2"]
        u3 = st["u3"]; u4 = st["u4"]
        ab_a = st["ab_a"]
        ab_b = st["ab_b"]
        bl1_sb = st["bl1_sb"]

        FW = [128, 128, 128, 128, 1]

        for wi, b in enumerate(("bq", "bk", "bv")):
            p.on("sync", lambda e, wi=wi, b=b: e.dma_start(
                out=bl1_sb[:, wi * ND:(wi + 1) * ND],
                in_=dram["L1_" + b].ap().rearrange("(c p) -> p c", p=128),
                allow_slow_non_contiguous=True), inc=[("d_l1b", 16)])
        p.on("sync", lambda e: e.dma_start(
            out=fres.ap().rearrange("p (c f) -> p c f", f=NF),
            in_=dram["fre"].ap().rearrange("(c p) f -> p c f", p=128)),
            inc=[("d_l1f", 16)])
        p.on("sync", lambda e: e.dma_start(
            out=fims.ap().rearrange("p (c f) -> p c f", f=NF),
            in_=dram["fim"].ap().rearrange("(c p) f -> p c f", p=128)),
            inc=[("d_l1f", 16)])
        f_mark = p.cnt("d_l1f")
        p.on("vector", lambda e: e.memset(sre[:], 0.0), inc=[("l1s_z", 1)])
        p.on("vector", lambda e: e.memset(sim[:], 0.0), inc=[("l1s_z", 1)])

        def project(widx, wname, dst, bias_col, semtag):
            w_marks = []
            for half in range(2):
                p.on("gpsimd", lambda e, wname=wname, half=half: e.dma_start(
                    out=wl1.ap().rearrange("p (c o) -> p c o", o=512),
                    in_=dram[wname].ap().rearrange("(c p) o -> p c o", p=128)
                        [:, :, half * 512:(half + 1) * 512]),
                    wait_exact=([("l1w_free", p.cnt("l1w_free"))] if half == 0
                                else [(semtag + "_mm", p.cnt(semtag + "_mm") + 16)]),
                    inc=[("d_l1wg", 16)])
                w_marks.append(p.cnt("d_l1wg"))
            for oc in range(ND):
                for tc in range(4):
                    pbp = PB[6 + (oc * 4 + tc) % 2]
                    wex = [("d_l1wg", w_marks[oc // 4])] if tc == 0 else []
                    k = p.cnt(semtag + "_mm")
                    if k >= 2:
                        wex += [(semtag + "_ev", k - 1)]
                    for kc in range(ND):
                        p.on("tensor", lambda e, pbp=pbp, kc=kc, oc=oc, tc=tc: e.matmul(
                            pbp[:], wl1[:, kc * 512 + (oc % 4) * 128: kc * 512 + (oc % 4 + 1) * 128],
                            xn_fm[:, kc * S + tc * 512: kc * S + (tc + 1) * 512],
                            start=(kc == 0), stop=(kc == ND - 1)),
                            wait_exact=(wex if kc == 0 else []),
                            inc=([(semtag + "_mm", 1)] if kc == ND - 1 else None))
                    p.on("scalar", lambda e, pbp=pbp, dst=dst, oc=oc, tc=tc, bias_col=bias_col: e.activation(
                        out=dst[:, oc * S + tc * 512: oc * S + (tc + 1) * 512],
                        in_=pbp[:], func=AF.Identity,
                        bias=bl1_sb[:, bias_col + oc: bias_col + oc + 1]),
                        wait=(["d_l1b"] if oc == 0 and tc == 0 else None),
                        wait_exact=[(semtag + "_mm", p.cnt(semtag + "_mm"))],
                        inc=[(semtag + "_ev", 1)])
            p.on("sync", lambda e: e.nop(),
                 wait_exact=[(semtag + "_ev", p.cnt(semtag + "_ev"))],
                 inc=[("l1w_free", 1)])

        project(0, "L1_Wk", fmA, 1 * ND, "l1k")
        project(1, "L1_Wv", fmB, 2 * ND, "l1v")

        f0 = 0
        fcnt = 0
        for fc in range(5):
            fw = FW[fc]
            for tc in range(4):
                par = 4 * (fcnt % 2)
                fcnt += 1
                wex0 = [("l1k_ev", 32), ("l1v_ev", 32), ("l1s_z", p.cnt("l1s_z")),
                        ("d_l1f", f_mark)] if (fc == 0 and tc == 0) else \
                       [("l1h_v", max(0, p.cnt("l1h_v") - 1))]
                for ci, (pb, fsb, xin) in enumerate((
                        (PB[par + 0], fres, fmA), (PB[par + 1], fims, fmA),
                        (PB[par + 2], fres, fmB), (PB[par + 3], fims, fmB))):
                    for kc in range(ND):
                        p.on("tensor", lambda e, pb=pb, fsb=fsb, xin=xin, kc=kc, fc=fc, fw=fw, f0=f0, tc=tc: e.matmul(
                            pb[0:fw, :], fsb[:, kc * NF + f0: kc * NF + f0 + fw],
                            xin[:, kc * S + tc * 512: kc * S + (tc + 1) * 512],
                            start=(kc == 0), stop=(kc == ND - 1)),
                            wait_exact=(wex0 if (ci == 0 and kc == 0) else []),
                            inc=([("l1h_mm", 1)] if (ci == 3 and kc == ND - 1) else None))
                mmt = p.cnt("l1h_mm")
                fwsl = slice(0, fw)
                p.on("scalar", lambda e, fwsl=fwsl, par=par: e.activation(
                    out=tA[fwsl, :], in_=PB[par + 0][fwsl, :], func=AF.Identity),
                    wait_exact=[("l1h_mm", mmt), ("l1h_v", p.cnt("l1h_v"))],
                    inc=[("l1h_s", 1)])
                p.on("scalar", lambda e, fwsl=fwsl, par=par: e.activation(
                    out=tB[fwsl, :], in_=PB[par + 1][fwsl, :], func=AF.Identity),
                    inc=[("l1h_s", 1)])
                p.on("vector", lambda e, fwsl=fwsl, par=par: e.tensor_mul(
                    u1[fwsl, :], tA[fwsl, :], PB[par + 2][fwsl, :]),
                    wait_exact=[("l1h_s", p.cnt("l1h_s"))])
                p.on("vector", lambda e, fwsl=fwsl, par=par: e.tensor_mul(
                    u2[fwsl, :], tB[fwsl, :], PB[par + 3][fwsl, :]))
                p.on("vector", lambda e, fwsl=fwsl, par=par: e.tensor_mul(
                    u3[fwsl, :], tA[fwsl, :], PB[par + 3][fwsl, :]))
                p.on("vector", lambda e, fwsl=fwsl, par=par: e.tensor_mul(
                    u4[fwsl, :], tB[fwsl, :], PB[par + 2][fwsl, :]))
                p.on("vector", lambda e, fwsl=fwsl: e.tensor_sub(
                    u1[fwsl, :], u1[fwsl, :], u2[fwsl, :]))
                p.on("vector", lambda e, fwsl=fwsl: e.tensor_add(
                    u3[fwsl, :], u3[fwsl, :], u4[fwsl, :]))
                p.on("vector", lambda e, fwsl=fwsl: e.reduce_sum(
                    u2[fwsl, 0:1], u1[fwsl, :], axis=AXX))
                p.on("vector", lambda e, fwsl=fwsl: e.reduce_sum(
                    u4[fwsl, 0:1], u3[fwsl, :], axis=AXX))
                p.on("vector", lambda e, fwsl=fwsl, fc=fc: e.tensor_add(
                    sre[fwsl, fc:fc + 1], sre[fwsl, fc:fc + 1], u2[fwsl, 0:1]))
                p.on("vector", lambda e, fwsl=fwsl, fc=fc: e.tensor_add(
                    sim[fwsl, fc:fc + 1], sim[fwsl, fc:fc + 1], u4[fwsl, 0:1]),
                    inc=[("l1h_v", 1)])
            f0 += fw

        for gi, gnm in ((0, "gre"), (1, "gim")):
            p.on("sync", lambda e, gi=gi, gnm=gnm: e.dma_start(
                out=fmB.ap().rearrange("p (c o) -> p c o", o=D)[:, gi * 5:gi * 5 + 4, :],
                in_=dram[gnm].ap()[0:512, :].rearrange("(c p) o -> p c o", p=128)),
                wait_exact=([("l1h_v", 20)] if gi == 0 else []),
                inc=[("d_l1g", 16)])
            p.on("sync", lambda e, gi=gi, gnm=gnm: e.dma_start(
                out=fmB[0:1, (gi * 5 + 4) * D:(gi * 5 + 5) * D],
                in_=dram[gnm].ap()[512:513, :]),
                inc=[("d_l1g", 16)])
        g_mark = p.cnt("d_l1g")
        p.on("sync", lambda e: e.nop(), wait_exact=[("l1h_v", 20)],
             inc=[("l1w_free", 1)])
        project(2, "L1_Wq", fmA, 0 * ND, "l1q")

        qstep = 0
        for tc in range(4):
            f0 = 0
            for fc in range(5):
                fw = FW[fc]
                fwsl = slice(0, fw)
                wex0 = [("l1q_ev", 32)] if (tc == 0 and fc == 0) else \
                       [("l1a_v", max(0, p.cnt("l1a_v") - 1))]
                qpar = 2 * (qstep % 2)
                qstep += 1
                for ci, (pb, fsb) in enumerate(((PB[qpar + 0], fres),
                                                (PB[qpar + 1], fims))):
                    for kc in range(ND):
                        p.on("tensor", lambda e, pb=pb, fsb=fsb, kc=kc, fw=fw, f0=f0, tc=tc: e.matmul(
                            pb[0:fw, :], fsb[:, kc * NF + f0: kc * NF + f0 + fw],
                            fmA[:, kc * S + tc * 512: kc * S + (tc + 1) * 512],
                            start=(kc == 0), stop=(kc == ND - 1)),
                            wait_exact=(wex0 if (ci == 0 and kc == 0) else []),
                            inc=([("l1a_mm", 1)] if (ci == 1 and kc == ND - 1) else None))
                mmt = p.cnt("l1a_mm")
                p.on("vector", lambda e, fwsl=fwsl, fc=fc, qpar=qpar: e.tensor_scalar_mul(
                    u1[fwsl, :], PB[qpar + 1][fwsl, :], sim[fwsl, fc:fc + 1]),
                    wait_exact=[("l1a_mm", mmt)])
                p.on("vector", lambda e, fwsl=fwsl, fc=fc, qpar=qpar: e.tensor_scalar_mul(
                    u2[fwsl, :], PB[qpar + 1][fwsl, :], sre[fwsl, fc:fc + 1]))
                p.on("vector", lambda e, fwsl=fwsl, fc=fc, qpar=qpar: e.scalar_tensor_tensor(
                    ab_a[fwsl, fc * 512:(fc + 1) * 512], PB[qpar + 0][fwsl, :],
                    sre[fwsl, fc:fc + 1], u1[fwsl, :], ALU.mult, ALU.add))
                p.on("vector", lambda e, fwsl=fwsl, fc=fc, qpar=qpar: e.scalar_tensor_tensor(
                    ab_b[fwsl, fc * 512:(fc + 1) * 512], PB[qpar + 0][fwsl, :],
                    sim[fwsl, fc:fc + 1], u2[fwsl, :], ALU.mult, ALU.subtract),
                    inc=[("l1a_v", 1)])
                f0 += fw
            for m4 in range(4):
                tt = tc * 4 + m4
                for oc in range(2):
                    j = p.cnt("l1o_mm")
                    pbo = PB[4 + (tt * 2 + oc) % 2]
                    first = True
                    f0 = 0
                    for fc in range(5):
                        fw = FW[fc]
                        for gi in range(2):
                            ab = ab_a if gi == 0 else ab_b
                            goff = gi * 5 * D
                            wex = []
                            if first:
                                wex = [("l1a_v", p.cnt("l1a_v")),
                                       ("l1o_ev", max(0, p.cnt("l1o_ev") - 1))]
                                if tc == 0 and tt == 0 and oc == 0:
                                    wex += [("d_l1g", g_mark)]
                            p.on("tensor", lambda e, pbo=pbo, ab=ab, fc=fc, fw=fw, m4=m4, goff=goff, oc=oc, first=first, gi=gi: e.matmul(
                                pbo[:], ab[0:fw, fc * 512 + m4 * 128: fc * 512 + (m4 + 1) * 128],
                                fmB[:, goff + fc * D + oc * 512: goff + fc * D + (oc + 1) * 512]
                                [0:fw, :] if False else
                                fmB[0:fw, goff + fc * D + oc * 512: goff + fc * D + (oc + 1) * 512],
                                start=first, stop=(fc == 4 and gi == 1)),
                                wait_exact=wex,
                                inc=([("l1o_mm", 1)] if (fc == 4 and gi == 1) else None))
                            first = False
                    p.on("vector", lambda e, pbo=pbo, tt=tt, oc=oc: e.scalar_tensor_tensor(
                        x_res[:, tt * D + oc * 512: tt * D + (oc + 1) * 512],
                        pbo[:], 1.0,
                        x_res[:, tt * D + oc * 512: tt * D + (oc + 1) * 512],
                        ALU.mult, ALU.add),
                        wait_exact=[("l1o_mm", p.cnt("l1o_mm"))],
                        inc=[("l1o_ev", 1)])

    def emit_ffn_l1(st):
        gate_sb = st["gate_sb"]
        whg = st["whg"]
        wdn = st["wdn"]
        i32t = st["i32t"]
        sgt = st["sgt"]
        i32u = st["i32u"]
        i32o = i32u
        amx = st["amx"]
        parb = st["parb"]
        paro = st["paro"]
        scal = st["scal"]
        scbc = st["scbc"]
        NH = L1_HID // 128

        def allreduce(cols, tag):
            p.on("tensor", lambda e: e.matmul(
                PB[7][0:2, 0:128], parb[:, 0:2], identF[:], start=True, stop=True),
                wait_exact=[(tag + "_pin", p.cnt(tag + "_pin"))],
                inc=[(tag + "_par", 1)])
            p.on("vector", lambda e: e.reduce_max(
                paro[0:2, 0:1], PB[7][0:2, 0:128], axis=AXX),
                wait_exact=[(tag + "_par", p.cnt(tag + "_par"))],
                inc=[(tag + "_par", 1)])
            p.on("gpsimd", lambda e: e.dma_start(
                out=cc_in.ap().rearrange("a b -> b a"), in_=paro[0:2, 0:1]),
                wait_exact=[(tag + "_par", p.cnt(tag + "_par"))],
                inc=[("d_cc", 16)])
            p.on("gpsimd", lambda e: e.collective_compute(
                "AllReduce", ALU.max, replica_groups=[list(range(8))],
                ins=[cc_in.ap()], outs=[cc_out.ap()]),
                wait_exact=[("d_cc", p.cnt("d_cc"))],
                inc=[("cc_done", 1)])
            p.on("gpsimd", lambda e: e.dma_start(
                out=scal[0:1, 0:2], in_=cc_out.ap()),
                wait_exact=[("cc_done", p.cnt("cc_done"))],
                inc=[("d_cc2", 16)])
            p.on("vector", None, wait_exact=[("d_cc2", p.cnt("d_cc2"))])

        p.on("vector", lambda e: e.memset(scbc[:, 3:4], 32768.5), inc=[("cbias", 1)])
        for c in range(ND):
            p.on("vector", lambda e, c=c: e.reduce_max(
                amx[:, c:c + 1], xn_fm[:, ts(c, S)], axis=AXX,
                apply_absolute_value=True))
        p.on("vector", lambda e: e.reduce_max(
            parb.ap().rearrange("p (a b) -> p a b", b=1)[:, 0:1, :],
            amx.ap().rearrange("p (a b) -> p a b", b=8)[:, 0:1, :], axis=AXX))
        p.on("vector", lambda e: e.tensor_copy(parb[:, 1:2], parb[:, 0:1]),
             inc=[("x_pin", 1)])
        allreduce(1, "x")
        p.on("vector", lambda e: e.tensor_scalar(
            scal[0:1, 2:3], scal[0:1, 0:1], 1.0 / 32767.0, 1e-12, ALU.mult, ALU.max))
        p.on("vector", lambda e: e.reciprocal(scal[0:1, 3:4], scal[0:1, 2:3]),
             inc=[("x_sc", 1)])
        p.wait("vector", ["x_sc"])
        p.on("tensor", lambda e: e.matmul(
            PB[7][:, 0:2], onesf[:], scal[0:1, 2:4], start=True, stop=True),
            wait_exact=[("x_sc", p.cnt("x_sc"))], inc=[("x_sc", 1)])
        p.on("vector", lambda e: e.tensor_copy(scbc[:, 0:2], PB[7][:, 0:2]),
             wait_exact=[("x_sc", p.cnt("x_sc"))], inc=[("x_bc", 1)])

        nxq = 0
        for tcq in range(4):
            for kcq in range(ND):
                j = kcq * 4 + tcq
                buf = i32t if nxq % 2 == 0 else i32u
                p.on("scalar", lambda e, j=j, buf=buf: e.activation(
                    out=buf[:], in_=xn_fm[:, j * 512:(j + 1) * 512], func=AF.Identity,
                    scale=scbc[:, 1:2], bias=scbc[:, 3:4]),
                    wait_exact=[("x_bc", p.cnt("x_bc")), ("l1q_ev", 32)] +
                               ([("xq_v", nxq - 1)] if nxq >= 2 else []),
                    inc=[("xq_s", 1)])
                p.on("vector", lambda e, j=j, buf=buf: e.scalar_tensor_tensor(
                    xn_fm[:, j * 512:(j + 1) * 512], buf[:], -32768.0,
                    scbc[:, 0:1].broadcast_to([128, 512]), ALU.add, ALU.mult),
                    wait_exact=[("xq_s", p.cnt("xq_s"))],
                    inc=[("xq_v", 1)])
                nxq += 1

        p.on("vector", lambda e: e.memset(amx[:, 0:2], 0.0), inc=[("ga_z", 1)])
        wg1_marks = []
        for hc in range(NH):
            p.on("sync", lambda e, hc=hc: e.dma_start(
                out=whg[hc % 2].ap().rearrange("p (c o) -> p c o", o=128),
                in_=scr["gate"].ap()[:, :, hc * 128:(hc + 1) * 128]),
                wait_exact=([("gt_mm", (hc - 1) * 4)] if hc >= 2 else []),
                inc=[("d_wg1", 16)])
            wg1_marks.append(p.cnt("d_wg1"))
        for hc in range(NH):
            for tc in range(4):
                pbg = PB[0 if tc % 2 == 0 else 1]
                wex = [("d_wg1", wg1_marks[hc])] if tc == 0 else []
                k = p.cnt("gt_ev")
                if k >= 2:
                    wex += [("gt_ev", k - 1)]
                for kc in range(ND):
                    p.on("tensor", lambda e, pbg=pbg, hc=hc, kc=kc, tc=tc: e.matmul(
                        pbg[:], whg[hc % 2][:, kc * 128:(kc + 1) * 128],
                        xn_fm[:, kc * S + tc * 512: kc * S + (tc + 1) * 512],
                        start=(kc == 0), stop=(kc == ND - 1)),
                        wait_exact=(wex if kc == 0 else []) +
                                   ([("xq_v", 8 * (tc + 1))] if (hc == 0 and kc == 0) else []),
                        inc=([("gt_mm", 1)] if kc == ND - 1 else None))
                p.on("scalar", lambda e, pbg=pbg, hc=hc, tc=tc: e.activation(
                    out=gate_sb[:, hc * S + tc * 512: hc * S + (tc + 1) * 512],
                    in_=pbg[:], func=AF.Identity),
                    wait_exact=[("gt_mm", p.cnt("gt_mm"))], inc=[("gt_ev", 1)])
            p.on("vector", lambda e, hc=hc: e.reduce_max(
                amx[:, 2:3], gate_sb[:, ts(hc, S)], axis=AXX,
                apply_absolute_value=True),
                wait=(["ga_z"] if hc == 0 else None),
                wait_exact=[("gt_ev", p.cnt("gt_ev"))])
            p.vf()
            p.on("vector", lambda e: e.tensor_max(
                amx[:, 0:1], amx[:, 0:1], amx[:, 2:3]), inc=[("ga_m", 1)])

        p.vf()
        p.on("vector", lambda e: e.tensor_copy(parb[:, 0:2],
             amx.ap().rearrange("p (a b) -> p a b", b=1)[:, 0:2, :]
             .broadcast_to([128, 2, 1]).rearrange("p a b -> p (a b)") if False
             else amx[:, 0:2]),
             wait_exact=[("ga_m", NH)], inc=[("gu_pin", 1)])
        allreduce(1, "gu")
        p.on("vector", lambda e: e.tensor_scalar(
            scal[0:1, 4:5], scal[0:1, 0:1], 1.0 / 32767.0, 1e-12, ALU.mult, ALU.max))
        p.on("vector", lambda e: e.reciprocal(scal[0:1, 5:6], scal[0:1, 4:5]),
             inc=[("gu_sc", 1)])
        p.wait("vector", ["gu_sc"])
        p.on("tensor", lambda e: e.matmul(
            PB[7][:, 0:2], onesf[:], scal[0:1, 4:6], start=True, stop=True),
            wait_exact=[("gu_sc", p.cnt("gu_sc"))], inc=[("gu_sc", 1)])
        p.on("vector", lambda e: e.tensor_copy(scbc[:, 4:6], PB[7][:, 0:2]),
             wait_exact=[("gu_sc", p.cnt("gu_sc"))], inc=[("gu_bc", 1)])

        f32t2 = [PB[2], PB[3]]
        p.on("vector", lambda e: e.memset(amx[:, 2:3], 0.0), inc=[("g2_z", 1)])
        NJ = NH * 4
        wu1_marks = []
        for hc in range(NH):
            p.on("sync", lambda e, hc=hc: e.dma_start(
                out=whg[hc % 2].ap().rearrange("p (c o) -> p c o", o=128),
                in_=scr["up"].ap()[:, :, hc * 128:(hc + 1) * 128]),
                wait_exact=([("up1_mm", (hc - 1) * 4)] if hc >= 2 else
                            [("gt_mm", NH * 4)]),
                inc=[("d_wu1", 16)])
            wu1_marks.append(p.cnt("d_wu1"))
        for s_ in range(NJ + 5):
            if s_ < NJ:
                hc, tc = s_ // 4, s_ % 4
                pbg = PB[0 if tc % 2 == 0 else 1]
                wex = [("d_wu1", wu1_marks[hc])] if tc == 0 else []
                if p.cnt("up1_v") >= 2:
                    wex += [("up1_v", p.cnt("up1_v") - 1)]
                for kc in range(ND):
                    p.on("tensor", lambda e, pbg=pbg, hc=hc, kc=kc, tc=tc: e.matmul(
                        pbg[:], whg[hc % 2][:, kc * 128:(kc + 1) * 128],
                        xn_fm[:, kc * S + tc * 512: kc * S + (tc + 1) * 512],
                        start=(kc == 0), stop=(kc == ND - 1)),
                        wait_exact=(wex if kc == 0 else []),
                        inc=([("up1_mm", 1)] if kc == ND - 1 else None))
                p.on("vector", lambda e, pbg=pbg, s_=s_: e.reduce_max(
                    amx[:, 6 + s_ % 2:7 + s_ % 2], pbg[:], axis=AXX,
                    apply_absolute_value=True),
                    wait_exact=[("up1_mm", p.cnt("up1_mm"))])
                if s_ >= 1:
                    p.on("vector", lambda e, s_=s_: e.tensor_max(
                        amx[:, 1:2], amx[:, 1:2],
                        amx[:, 6 + (s_ - 1) % 2:7 + (s_ - 1) % 2]),
                        inc=[("up1_v", 1)])
            if s_ < NJ:
                sl = slice(s_ * 512, (s_ + 1) * 512)
                p.on("scalar", lambda e, sl=sl: e.activation(
                    out=i32t[:], in_=gate_sb[:, sl], func=AF.Identity,
                    scale=scbc[:, 5:6], bias=scbc[:, 3:4]),
                    wait_exact=[("gu_bc", p.cnt("gu_bc")),
                                ("g2_dq", p.cnt("g2_dq"))],
                    inc=[("g2_s", 1)])
                p.on("vector", lambda e, sl=sl: e.scalar_tensor_tensor(
                    gate_sb[:, sl], i32t[:], -32768.0,
                    scbc[:, 4:5].broadcast_to([128, 512]), ALU.add, ALU.mult),
                    wait_exact=[("g2_s", p.cnt("g2_s"))], inc=[("g2_dq", 1)])
            if 0 <= s_ - 1 < NJ:
                t = s_ - 1
                p.on("vector", lambda e, t=t: e.tensor_scalar(
                    f32t2[t % 2][:, 0:512], sgt[:], 65536.0, 65535.49,
                    ALU.mult, ALU.min),
                    wait_exact=[("g2_s2", t + 1)], inc=[("g2_ts", 1)])
            if s_ < NJ:
                sl = slice(s_ * 512, (s_ + 1) * 512)
                p.on("scalar", lambda e, sl=sl: e.activation(
                    out=sgt[:], in_=gate_sb[:, sl], func=AF.Sigmoid),
                    wait_exact=[("g2_dq", p.cnt("g2_dq")),
                                ("g2_ts", p.cnt("g2_ts"))],
                    inc=[("g2_s2", 1)])
            if 0 <= s_ - 3 < NJ:
                m_ = s_ - 3
                slm = slice(m_ * 512, (m_ + 1) * 512)
                p.on("vector", lambda e, slm=slm, m_=m_: e.scalar_tensor_tensor(
                    gate_sb[:, slm], i32u[:], 1.0 / 65536.0,
                    gate_sb[:, slm], ALU.mult, ALU.mult),
                    wait_exact=[("g2_s2", m_ + 1)], inc=[("g2_p", 1)])
            if 0 <= s_ - 2 < NJ:
                a_ = s_ - 2
                p.on("vector", lambda e, a_=a_: e.tensor_scalar_add(
                    i32u[:], f32t2[a_ % 2][:, 0:512], 0.5))
            if 0 <= s_ - 4 < NJ:
                r_ = s_ - 4
                slr = slice(r_ * 512, (r_ + 1) * 512)
                p.on("vector", lambda e, slr=slr, r_=r_: e.reduce_max(
                    amx[:, 3 + 2 * (r_ % 2):4 + 2 * (r_ % 2)], gate_sb[:, slr],
                    axis=AXX, apply_absolute_value=True),
                    wait=(["g2_z"] if r_ == 0 else None))
            if 0 <= s_ - 5 < NJ:
                x_ = s_ - 5
                p.on("vector", lambda e, x_=x_: e.tensor_max(
                    amx[:, 2:3], amx[:, 2:3],
                    amx[:, 3 + 2 * (x_ % 2):4 + 2 * (x_ % 2)]), inc=[("g2_m", 1)])
        p.vf()
        p.on("vector", lambda e: e.tensor_max(
            amx[:, 1:2], amx[:, 1:2], amx[:, 6 + (NJ - 1) % 2:7 + (NJ - 1) % 2]),
            inc=[("up1_v", 1)])

        p.vf()
        p.on("vector", lambda e: e.tensor_copy(parb[:, 0:1], amx[:, 1:2]),
             wait_exact=[("up1_v", p.cnt("up1_v"))])
        p.on("vector", lambda e: e.tensor_copy(parb[:, 1:2], amx[:, 1:2]),
             inc=[("gu_pin", 1)])
        allreduce(1, "gu")
        p.on("vector", lambda e: e.tensor_scalar(
            scal[0:1, 6:7], scal[0:1, 0:1], 1.0 / 32767.0, 1e-12, ALU.mult, ALU.max))
        p.on("vector", lambda e: e.reciprocal(scal[0:1, 7:8], scal[0:1, 6:7]),
             inc=[("gu_sc", 1)])
        p.wait("vector", ["gu_sc"])
        p.on("tensor", lambda e: e.matmul(
            PB[7][:, 0:2], onesf[:], scal[0:1, 6:8], start=True, stop=True),
            wait_exact=[("gu_sc", p.cnt("gu_sc"))], inc=[("gu_sc", 1)])
        p.on("vector", lambda e: e.tensor_copy(scbc[:, 6:8], PB[7][:, 0:2]),
             wait_exact=[("gu_sc", p.cnt("gu_sc"))], inc=[("gu_bc", 1)])

        p.vf()
        p.on("vector", lambda e: e.tensor_copy(parb[:, 0:1], amx[:, 2:3]),
             wait_exact=[("g2_m", NH * 4)])
        p.on("vector", lambda e: e.tensor_copy(parb[:, 1:2], amx[:, 2:3]),
             inc=[("t2_pin", 1)])
        allreduce(1, "t2")
        p.on("vector", lambda e: e.tensor_scalar(
            scal[0:1, 4:5], scal[0:1, 0:1], 1.0 / 32767.0, 1e-12, ALU.mult, ALU.max))
        p.on("vector", lambda e: e.reciprocal(scal[0:1, 5:6], scal[0:1, 4:5]),
             inc=[("t2_sc", 1)])
        p.wait("vector", ["t2_sc"])
        p.on("tensor", lambda e: e.matmul(
            PB[7][:, 0:2], onesf[:], scal[0:1, 4:6], start=True, stop=True),
            wait_exact=[("t2_sc", p.cnt("t2_sc"))], inc=[("t2_sc", 1)])
        p.on("vector", lambda e: e.tensor_copy(scbc[:, 4:6], PB[7][:, 0:2]),
             wait_exact=[("t2_sc", p.cnt("t2_sc"))], inc=[("t2_bc", 1)])
        def _q2(j):
            sl = slice(j * 512, (j + 1) * 512)
            p.on("scalar", lambda e, sl=sl: e.activation(
                out=i32t[:], in_=gate_sb[:, sl], func=AF.Identity,
                scale=scbc[:, 5:6], bias=scbc[:, 3:4]),
                wait_exact=[("t2_bc", p.cnt("t2_bc")), ("q2_v", p.cnt("q2_v"))],
                inc=[("q2_s", 1)])
            p.on("vector", lambda e, sl=sl: e.scalar_tensor_tensor(
                gate_sb[:, sl], i32t[:], -32768.0,
                scbc[:, 4:5].broadcast_to([128, 512]), ALU.add, ALU.mult),
                wait_exact=[("q2_s", p.cnt("q2_s"))], inc=[("q2_v", 1)])
        _q2(0)

        p.on("vector", lambda e: e.memset(amx[:, 3:4], 0.0), inc=[("o_z", 1)])
        wu2_marks = []
        for hc in range(NH):
            p.on("sync", lambda e, hc=hc: e.dma_start(
                out=whg[hc % 2].ap().rearrange("p (c o) -> p c o", o=128),
                in_=scr["up"].ap()[:, :, hc * 128:(hc + 1) * 128]),
                wait_exact=([("o_mm", (hc - 1) * 4)] if hc >= 2 else
                            [("up1_mm", NH * 4)]),
                inc=[("d_wu2", 16)])
            wu2_marks.append(p.cnt("d_wu2"))
        for hc in range(NH):
            for tc in range(4):
                j = hc * 4 + tc
                if j + 1 < NH * 4:
                    _q2(j + 1)
                pbg = PB[0 if tc % 2 == 0 else 1]
                wex = [("d_wu2", wu2_marks[hc])] if tc == 0 else []
                if p.cnt("o_s") >= 2:
                    wex += [("o_s", p.cnt("o_s") - 1)]
                if hc == 0 and tc == 0:
                    wex += [("o_z", p.cnt("o_z"))]
                for kc in range(ND):
                    p.on("tensor", lambda e, pbg=pbg, hc=hc, kc=kc, tc=tc: e.matmul(
                        pbg[:], whg[hc % 2][:, kc * 128:(kc + 1) * 128],
                        xn_fm[:, kc * S + tc * 512: kc * S + (tc + 1) * 512],
                        start=(kc == 0), stop=(kc == ND - 1)),
                        wait_exact=(wex if kc == 0 else []),
                        inc=([("o_mm", 1)] if kc == ND - 1 else None))
                p.on("scalar", lambda e, pbg=pbg: e.activation(
                    out=i32o[:], in_=pbg[:], func=AF.Identity,
                    scale=scbc[:, 7:8], bias=scbc[:, 3:4]),
                    wait_exact=[("o_mm", p.cnt("o_mm")),
                                ("o_stt", p.cnt("o_stt"))],
                    inc=[("o_s", 1)])
                lo = hc * S + tc * 512
                p.on("vector", lambda e, lo=lo: e.scalar_tensor_tensor(
                    gate_sb[:, lo:lo + 512], i32o[:], -32768.0,
                    gate_sb[:, lo:lo + 512], ALU.add, ALU.mult),
                    wait_exact=[("o_s", p.cnt("o_s")), ("q2_v", j + 1)],
                    inc=[("o_stt", 1)])
                if j >= 1:
                    lp = lo - 512 if tc >= 1 else (hc - 1) * S + 3 * 512
                    p.on("vector", lambda e, lp=lp, j=j: e.reduce_max(
                        amx[:, 4 + (j - 1) % 2:5 + (j - 1) % 2],
                        gate_sb[:, lp:lp + 512], axis=AXX,
                        apply_absolute_value=True))
                if j >= 2:
                    p.on("vector", lambda e, j=j: e.tensor_max(
                        amx[:, 3:4], amx[:, 3:4],
                        amx[:, 4 + (j - 2) % 2:5 + (j - 2) % 2]), inc=[("o_v", 1)])
        p.vf()
        lp = (NH - 1) * S + 3 * 512
        p.on("vector", lambda e, lp=lp: e.reduce_max(
            amx[:, 4 + 95 % 2:5 + 95 % 2], gate_sb[:, lp:lp + 512], axis=AXX,
            apply_absolute_value=True))
        p.vf()
        p.on("vector", lambda e: e.tensor_max(
            amx[:, 3:4], amx[:, 3:4], amx[:, 4 + 94 % 2:5 + 94 % 2]),
            inc=[("o_v", 1)])
        p.vf()
        p.on("vector", lambda e: e.tensor_max(
            amx[:, 3:4], amx[:, 3:4], amx[:, 4 + 95 % 2:5 + 95 % 2]),
            inc=[("o_v", 1)])

        p.vf()
        p.on("vector", lambda e: e.tensor_copy(parb[:, 0:1], amx[:, 3:4]),
             wait_exact=[("o_v", p.cnt("o_v"))])
        p.on("vector", lambda e: e.tensor_copy(parb[:, 1:2], amx[:, 3:4]),
             inc=[("oo_pin", 1)])
        allreduce(1, "oo")
        p.on("vector", lambda e: e.tensor_mul(scal[0:1, 0:1], scal[0:1, 0:1],
                                              scal[0:1, 6:7]))
        p.on("vector", lambda e: e.tensor_scalar(
            scal[0:1, 4:5], scal[0:1, 0:1], 1.0 / 32767.0, 1e-12, ALU.mult, ALU.max))
        p.on("vector", lambda e: e.reciprocal(scal[0:1, 5:6], scal[0:1, 4:5]),
             inc=[("rcp", 1)])
        p.wait("vector", ["rcp"])
        p.on("vector", lambda e: e.tensor_mul(scal[0:1, 5:6], scal[0:1, 5:6],
                                              scal[0:1, 6:7]), inc=[("oq_sc", 1)])
        p.on("tensor", lambda e: e.matmul(
            PB[7][:, 0:2], onesf[:], scal[0:1, 4:6], start=True, stop=True),
            wait_exact=[("oq_sc", p.cnt("oq_sc"))], inc=[("oq_sc", 1)])
        p.on("vector", lambda e: e.tensor_copy(scbc[:, 4:6], PB[7][:, 0:2]),
             wait_exact=[("oq_sc", p.cnt("oq_sc"))], inc=[("oq_bc", 1)])
        def _oq(j):
            sl = slice(j * 512, (j + 1) * 512)
            buf = i32t if j % 2 == 0 else i32u
            p.on("scalar", lambda e, sl=sl, buf=buf: e.activation(
                out=buf[:], in_=gate_sb[:, sl], func=AF.Identity,
                scale=scbc[:, 5:6], bias=scbc[:, 3:4]),
                wait_exact=[("oq_bc", p.cnt("oq_bc"))] +
                           ([("oq_v", j - 1)] if j >= 2 else []),
                inc=[("oq_s", 1)])
            p.on("vector", lambda e, sl=sl, buf=buf: e.scalar_tensor_tensor(
                gate_sb[:, sl], buf[:], -32768.0,
                scbc[:, 4:5].broadcast_to([128, 512]), ALU.add, ALU.mult),
                wait_exact=[("oq_s", p.cnt("oq_s"))], inc=[("oq_v", 1)])
        for j in range(4):
            _oq(j)

        for ttg in range(4):
            for hc in range(NH):
                if ttg == 0:
                    for j in range(4 * (hc + 1), min(4 * (hc + 2), NH * 4)):
                        _oq(j)
                p.on("sync", lambda e, hc=hc: e.dma_start(
                    out=wdn[hc % 2].ap(),
                    in_=scr["down"].ap()[hc, :, :]),
                    wait_exact=([("dn_hc", max(0, p.cnt("dn_hc") - 1))]
                                if (ttg > 0 or hc >= 2) else
                                ([("up1_mm", NH * 4)] if ttg == 0 else [])),
                    inc=[("d_dn", 16)])
                for m4 in range(4):
                    for oc in range(2):
                        pbd = PB[m4 * 2 + oc]
                        wex = []
                        if hc == 0 and m4 == 0 and oc == 0:
                            wex = [("oq_v", 4 if ttg == 0 else NH * 4),
                                   ("dn_ev", p.cnt("dn_ev")),
                                   ("d_dn", p.cnt("d_dn"))]
                        elif hc > 0 and m4 == 0 and oc == 0:
                            wex = [("d_dn", p.cnt("d_dn"))]
                            if ttg == 0:
                                wex += [("oq_v", min((hc + 1) * 4, NH * 4))]
                        p.on("tensor", lambda e, pbd=pbd, hc=hc, m4=m4, oc=oc, ttg=ttg: e.matmul(
                            pbd[:], gate_sb[:, hc * S + (ttg * 4 + m4) * 128:
                                            hc * S + (ttg * 4 + m4) * 128 + 128],
                            wdn[hc % 2][:, oc * 512:(oc + 1) * 512],
                            start=(hc == 0), stop=(hc == NH - 1)),
                            wait_exact=wex,
                            inc=([("dn_mm", 1)] if hc == NH - 1 else None))
                        if m4 == 3 and oc == 1:
                            p.on("tensor", lambda e: e.nop(), inc=[("dn_hc", 1)])
            for m4 in range(4):
                tt = ttg * 4 + m4
                for oc in range(2):
                    pbd = PB[m4 * 2 + oc]
                    lo = tt * D + oc * 512
                    p.on("vector", lambda e, pbd=pbd, lo=lo: e.scalar_tensor_tensor(
                        x_res[:, lo:lo + 512], pbd[:], 1.0,
                        x_res[:, lo:lo + 512], ALU.mult, ALU.add),
                        wait_exact=[("dn_mm", p.cnt("dn_mm"))],
                        inc=[("dn_ev", 1)])

    import contextlib

    def emit_dump(key, src_ap):
        if not dbg:
            return
        p.on("sync", lambda e, key=key, src_ap=src_ap: e.dma_start(
            out=dbg_t[key].ap(), in_=src_ap), inc=[("d_out", 16)])

    with contextlib.ExitStack() as stack:
        _sqd = stack.enter_context(nc.sbuf_tensor("sqd_a", [128, D], BF16))
        _xtm = stack.enter_context(nc.sbuf_tensor("xtm_a", [128, 2 * D], BF16))
        emit_layernorm("a", _sqd, _xtm, first=True, spill=scr["xn"])
        p.barrier()
        if dbg:
            emit_dump("xn", xn_fm.ap())
            p.on("sync", lambda e: e.dma_start(
                out=dbg_t["stats"].ap()[:, 0:NT], in_=m_sb.ap()), inc=[("d_out", 16)])
            p.on("sync", lambda e: e.dma_start(
                out=dbg_t["stats"].ap()[:, NT:2 * NT], in_=r_sb.ap()), inc=[("d_out", 16)])
            for i_, tn in enumerate((dbg_s1, dbg_s2, dbg_s3, dbg_s4)):
                p.on("sync", lambda e, i_=i_, tn=tn: e.dma_start(
                    out=dbg_t["stats"].ap()[:, (2 + i_) * NT:(3 + i_) * NT],
                    in_=tn.ap()), inc=[("d_out", 16)])
    with contextlib.ExitStack() as stack:
        st = {}
        st["WT"] = {w: stack.enter_context(
            nc.sbuf_tensor("WT_" + w, [128, ND * D], BF16))
            for w in ("Wq", "Wk", "Wv", "Wo")}
        st["wstage"] = stack.enter_context(nc.sbuf_tensor("wstage", [128, 4 * D], BF16))
        st["q_pair"] = stack.enter_context(nc.sbuf_tensor("q_pair", [128, S], BF16))
        st["k_pair"] = stack.enter_context(nc.sbuf_tensor("k_pair", [128, S], BF16))
        st["v_aug"] = stack.enter_context(nc.sbuf_tensor("v_aug", [128, NT * 130], BF16))
        st["ctx_pair"] = stack.enter_context(nc.sbuf_tensor("ctx_pair", [128, S], BF16))
        st["e_sb"] = stack.enter_context(nc.sbuf_tensor("e_sb", [128, 8 * 512], BF16))
        st["ctxr"] = [stack.enter_context(nc.sbuf_tensor("ctxr%d" % i, [65, 512], F32))
                      for i in range(2)]
        st["r_row"] = [stack.enter_context(nc.sbuf_tensor("r_row%d" % i, [1, 512], BF16))
                       for i in range(2)]
        st["ones_row"] = stack.enter_context(nc.sbuf_tensor("ones_row", [1, 512], BF16))
        st["bv_row"] = stack.enter_context(nc.sbuf_tensor("bv_row", [1, D], BF16))
        st["bo_row"] = stack.enter_context(nc.sbuf_tensor("bo_row", [1, D], BF16))
        st["alibi"] = stack.enter_context(nc.sbuf_tensor("alibi", [128, NHEAD * NT], F32))
        st["slopes_bc"] = stack.enter_context(nc.sbuf_tensor("slopes_bc", [128, NHEAD], F32))
        emit_attn_l0(st)
        p.barrier()
    emit_dump("attn", x_res.ap())

    with contextlib.ExitStack() as stack:
        st = {}
        st["gw_stage"] = stack.enter_context(nc.sbuf_tensor("gw_stage", [8, D], BF16))
        st["gwT"] = stack.enter_context(nc.sbuf_tensor("gwT", [128, ND * 8], BF16))
        st["logit_sb"] = stack.enter_context(nc.sbuf_tensor("logit_sb", [128, 8], F32))
        st["comb"] = stack.enter_context(nc.sbuf_tensor("comb", [128, NT * 8], F32))
        st["g_t0"] = stack.enter_context(nc.sbuf_tensor("g_t0", [128, 8], F32))
        st["g_t1"] = stack.enter_context(nc.sbuf_tensor("g_t1", [128, 8], F32))
        st["g_c0"] = stack.enter_context(nc.sbuf_tensor("g_c0", [128, 1], F32))
        st["g_c1"] = stack.enter_context(nc.sbuf_tensor("g_c1", [128, 1], F32))
        st["w1g"] = [stack.enter_context(nc.sbuf_tensor("w1g%d" % i, [128, ND * 384], BF16))
                     for i in range(2)]
        st["w3g"] = [stack.enter_context(nc.sbuf_tensor("w3g%d" % i, [128, ND * 384], BF16))
                     for i in range(2)]
        st["w2g"] = stack.enter_context(nc.sbuf_tensor("w2g", [128, 6 * D], BF16))
        st["h_sb"] = stack.enter_context(nc.sbuf_tensor("h_sb0", [128, 6 * 768], BF16))
        st["hsil"] = [stack.enter_context(nc.sbuf_tensor("hsil%d" % i, [128, 512], BF16))
                      for i in range(2)]
        st["wq_stage"] = stack.enter_context(nc.sbuf_tensor("wq_stage", [128, 512], F32))
        st["dq_s"] = stack.enter_context(nc.sbuf_tensor("dq_s", [128, 16], F32))
        st["dq_si"] = stack.enter_context(nc.sbuf_tensor("dq_si", [128, 16], F32))
        st["dq_t"] = stack.enter_context(nc.sbuf_tensor("dq_t", [128, 512], F32))
        st["dq_i"] = stack.enter_context(nc.sbuf_tensor("dq_i", [128, 512], I32))
        st["dq_w"] = stack.enter_context(nc.sbuf_tensor("dq_w", [128, 512], BF16))
        st["tr_stage"] = stack.enter_context(nc.sbuf_tensor("tr_stage", [128, 4 * 128], BF16))
        st["tri_sb"] = stack.enter_context(nc.sbuf_tensor("tri_sb", [128, 128], BF16))
        st["stri_sb"] = stack.enter_context(nc.sbuf_tensor("stri_sb", [128, 128], BF16))
        st["iota_sb"] = stack.enter_context(nc.sbuf_tensor("iota_sb", [128, 768], F32))
        st["route_m"] = stack.enter_context(nc.sbuf_tensor("route_m", [128, 128], BF16))
        st["t_sb"] = stack.enter_context(nc.sbuf_tensor("t_sb", [128, 128], F32))
        st["pos"] = stack.enter_context(nc.sbuf_tensor("pos", [128, 128], F32))
        st["onesb"] = stack.enter_context(nc.sbuf_tensor("onesb", [128, 1], BF16))
        st["ccol_sb"] = stack.enter_context(nc.sbuf_tensor("ccol_sb", [128, 1], BF16))
        st["base_sb"] = stack.enter_context(nc.sbuf_tensor("base_sb", [1, 128], F32))
        st["pt_sb"] = [stack.enter_context(nc.sbuf_tensor("pt%d" % i, [128, 768], BF16))
                       for i in range(2)]
        st["ptT_sb"] = [stack.enter_context(nc.sbuf_tensor("ptT%d" % i, [128, 768], BF16))
                        for i in range(2)]
        st["xt_sb"] = [stack.enter_context(nc.sbuf_tensor("xt%d" % i, [128, 512], BF16))
                       for i in range(2)]
        st["gth_sb"] = [stack.enter_context(nc.sbuf_tensor("gth%d" % i, [128, 512], BF16))
                        for i in range(2)]
        st["gath_fm"] = stack.enter_context(nc.sbuf_tensor("gath_fm", [128, 8 * 768], BF16))
        st["y_sb"] = stack.enter_context(nc.sbuf_tensor("y_sb", [128, 6 * D], F32))
        dq_gen = emit_dequant(st)
        emit_moe(st, filler=dq_gen)
        for _ in dq_gen:
            pass
        p.barrier()
    emit_dump("moe", x_res.ap())
    if dbg:
        p.on("sync", lambda e: e.dma_start(
            out=dbg_t["scr"].ap(), in_=scr["up"].ap()), inc=[("d_out", 16)])

    with contextlib.ExitStack() as stack:
        _sqd = stack.enter_context(nc.sbuf_tensor("sqd_b", [128, D], BF16))
        _xtm = stack.enter_context(nc.sbuf_tensor("xtm_b", [128, 2 * D], BF16))
        emit_layernorm("b", _sqd, _xtm)
        p.barrier()

    with contextlib.ExitStack() as stack:
        st = {}
        st["wl1"] = stack.enter_context(nc.sbuf_tensor("wl1", [128, ND * 512], BF16))
        st["fmA"] = stack.enter_context(nc.sbuf_tensor("fmA", [128, ND * S], BF16))
        st["fmB"] = stack.enter_context(nc.sbuf_tensor("fmB", [128, ND * S], BF16))
        st["fres"] = stack.enter_context(nc.sbuf_tensor("fres", [128, ND * NF], BF16))
        st["fims"] = stack.enter_context(nc.sbuf_tensor("fims", [128, ND * NF], BF16))
        st["sre"] = stack.enter_context(nc.sbuf_tensor("sre", [128, 8], F32))
        st["sim"] = stack.enter_context(nc.sbuf_tensor("sim", [128, 8], F32))
        st["tA"] = stack.enter_context(nc.sbuf_tensor("tA", [128, 512], BF16))
        st["tB"] = stack.enter_context(nc.sbuf_tensor("tB", [128, 512], BF16))
        st["u1"] = stack.enter_context(nc.sbuf_tensor("u1", [128, 512], BF16))
        st["u2"] = stack.enter_context(nc.sbuf_tensor("u2", [128, 512], F32))
        st["u3"] = stack.enter_context(nc.sbuf_tensor("u3", [128, 512], BF16))
        st["u4"] = stack.enter_context(nc.sbuf_tensor("u4", [128, 512], F32))
        st["ab_a"] = stack.enter_context(nc.sbuf_tensor("ab_a", [128, 5 * 512], BF16))
        st["ab_b"] = stack.enter_context(nc.sbuf_tensor("ab_b", [128, 5 * 512], BF16))
        st["bl1_sb"] = stack.enter_context(nc.sbuf_tensor("bl1_sb", [128, 3 * ND], F32))
        emit_attn_l1(st)
        p.barrier()
    emit_dump("l1a", x_res.ap())

    with contextlib.ExitStack() as stack2:
        st2 = {}
        st2["whg"] = [stack2.enter_context(
            nc.sbuf_tensor("whg%d" % i, [128, ND * 128], BF16)) for i in range(2)]
        st2["wdn"] = [stack2.enter_context(
            nc.sbuf_tensor("wdn%d" % i, [128, D], BF16)) for i in range(2)]
        st2["i32t"] = stack2.enter_context(nc.sbuf_tensor("i32t", [128, 512], I32))
        st2["i32u"] = stack2.enter_context(nc.sbuf_tensor("i32u", [128, 512], I32))
        st2["sgt"] = stack2.enter_context(nc.sbuf_tensor("sgt", [128, 512], BF16))
        st2["amx"] = stack2.enter_context(nc.sbuf_tensor("amx", [128, 8], F32))
        st2["parb"] = stack2.enter_context(nc.sbuf_tensor("parb", [128, 2], F32))
        st2["paro"] = stack2.enter_context(nc.sbuf_tensor("paro", [128, 2], F32))
        st2["scal"] = stack2.enter_context(nc.sbuf_tensor("scal", [1, 8], F32))
        st2["scbc"] = stack2.enter_context(nc.sbuf_tensor("scbc", [128, 8], F32))
        st2["gate_sb"] = stack2.enter_context(
            nc.sbuf_tensor("gate_sb", [128, (L1_HID // 128) * S], BF16))
        emit_ffn_l1(st2)
        p.barrier()

    p.on("sync", lambda e: e.dma_start(
        out=out.ap().rearrange("(t p) d -> p t d", p=128),
        in_=x_res.ap().rearrange("p (t d) -> p t d", d=D)),
        inc=[("d_out", 16)])
    p.on("gpsimd", None, wait=["d_out"])

    p.finalize()
    return nc


_NC_CACHE = None


def _consts():
    n = np.arange(D)[:, None].astype(np.float64)
    f = np.arange(NF)[None, :].astype(np.float64)
    th = 2.0 * np.pi * n * f / D
    fre = np.cos(th)
    fim = -np.sin(th)
    c = np.full(NF, 2.0); c[0] = 1.0; c[NF - 1] = 1.0
    nn = np.arange(D)[None, :].astype(np.float64)
    thg = 2.0 * np.pi * f.T * nn / D
    gre = (c[:, None] * np.cos(thg)) / D
    gim = (-c[:, None] * np.sin(thg)) / D
    bf = ml_dtypes.bfloat16
    tri = np.triu(np.ones((128, 128))).astype(bf)
    kk = np.arange(128)
    stri = ((kk[:, None] % 8 == kk[None, :] % 8) &
            (kk[:, None] // 8 < kk[None, :] // 8)).astype(bf)
    iotaw = np.broadcast_to(np.arange(768, dtype=np.float32), (128, 768)).copy()
    return (fre.astype(bf), fim.astype(bf), gre.astype(bf), gim.astype(bf),
            tri, stri, iotaw)


def kernel(**inputs):
    global _NC_CACHE
    if _NC_CACHE is None:
        _NC_CACHE = build_nc()
    nc = _NC_CACHE
    fre, fim, gre, gim, tri, stri, iotaw = _consts()
    base = {
        "L0_Wq": inputs["L0_attn_Wq"], "L0_Wk": inputs["L0_attn_Wk"],
        "L0_Wv": inputs["L0_attn_Wv"], "L0_Wo": inputs["L0_attn_Wo"],
        "L0_bq": inputs["L0_attn_bq"], "L0_bk": inputs["L0_attn_bk"],
        "L0_bv": inputs["L0_attn_bv"], "L0_bo": inputs["L0_attn_bo"],
        "L0_slopes": inputs["L0_attn_slopes"],
        "gate_w": inputs["L0_ffn_gate_w"],
        "w1": inputs["L0_ffn_w1"], "w2": inputs["L0_ffn_w2"], "w3": inputs["L0_ffn_w3"],
        "sw1": inputs["L0_ffn_sw1"], "sw2": inputs["L0_ffn_sw2"], "sw3": inputs["L0_ffn_sw3"],
        "L1_Wq": inputs["L1_attn_Wq"], "L1_Wk": inputs["L1_attn_Wk"],
        "L1_Wv": inputs["L1_attn_Wv"],
        "L1_bq": inputs["L1_attn_bq"], "L1_bk": inputs["L1_attn_bk"],
        "L1_bv": inputs["L1_attn_bv"],
        "w_gate": inputs["L1_ffn_w_gate"], "w_up": inputs["L1_ffn_w_up"],
        "w_down": inputs["L1_ffn_w_down"],
        "fre": fre, "fim": fim, "gre": gre, "gim": gim,
        "tri": tri, "stri": stri, "iotaw": iotaw,
    }
    base = {k: np.ascontiguousarray(np.asarray(v)) for k, v in base.items()}
    x = np.asarray(inputs["L0_attn_x"])
    in_maps = []
    for c in range(8):
        m = dict(base)
        m["x"] = np.ascontiguousarray(x[c])
        in_maps.append(m)
    res = run_bass_kernel_spmd(nc, in_maps, core_ids=list(range(8)))
    return np.stack([res.results[c]["out"] for c in range(8)], axis=0)


if __name__ == "__main__":
    nc = build_nc()
    print("built OK, instructions:", nc.n_instructions())


# version 27 (final)
# speedup vs baseline: 1.2204x; 1.0196x over previous
import numpy as np
import ml_dtypes

import concourse.bass as bass
import concourse.mybir as mybir
from concourse.bass_utils import run_bass_kernel_spmd
from concourse.bass_isa import ReduceOp
import concourse.library_config as libcfg

F32 = mybir.dt.float32
BF16 = mybir.dt.bfloat16
I32 = mybir.dt.int32
AF = mybir.ActivationFunctionType
ALU = mybir.AluOpType
AXX = mybir.AxisListType.X

S = 2048
D = 1024
NT = S // 128
ND = D // 128
NHEAD = 16
MOE_E = 8
MOE_HID = 2816
SH_HID = 1536
L1_HID = 3072
NF = 513

ENGS = ("sync", "tensor", "vector", "scalar", "gpsimd")


class Prog:

    ALIAS = {}
    for _n in ("d_x", "d_sm", "d_dq", "d_dqo", "d_l1b", "d_l1f", "d_l1w",
               "d_l1g", "d_wg1", "d_wu1", "d_wu2", "d_dn", "d_out", "dbgd"):
        ALIAS[_n] = "dS"
    for _n in ("d_smg", "d_wstage", "d_gw", "d_wg", "d_cc", "d_cc2", "d_l1wg"):
        ALIAS[_n] = "dG"
    for _n in ("x_pin", "gu_pin", "t2_pin", "oo_pin"):
        ALIAS[_n] = "ar_pin"
    for _n in ("x_par", "gu_par", "t2_par", "oo_par"):
        ALIAS[_n] = "ar_par"
    for _n in ("x_sc", "gu_sc", "t2_sc", "oo_sc", "oq_sc"):
        ALIAS[_n] = "ar_sc"
    for _n in ("x_bc", "gu_bc", "t2_bc", "oq_bc"):
        ALIAS[_n] = "ar_bc"
    for _n in ("ga_z", "g2_z", "o_z", "l1s_z", "v_ones"):
        ALIAS[_n] = "zz"
    ALIAS["c_one"] = "c_id"
    ALIAS["gw_tr"] = "dq_tr"
    ALIAS["gw_ev"] = "dq_ev"
    ALIAS["g_tr"] = "dq_tr"
    ALIAS["s_tr"] = "dq_tr"
    ALIAS["s_te"] = "g_gf"
    ALIAS["sc_ev"] = "my_ev"
    ALIAS["d_xt"] = "dS"
    ALIAS["d_ct"] = "dS"
    ALIAS["g_pt"] = "rt"
    ALIAS["o_stt"] = "rt"
    ALIAS["g2_dq"] = "rt"
    ALIAS["g2_ts"] = "g_gf"
    ALIAS["s_pt"] = "rt"
    ALIAS["pos_d"] = "rt"
    ALIAS["g_ev"] = "my_ev"
    ALIAS["s_mm"] = "my_mm"

    def __init__(self, nc):
        self.nc = nc
        self.items = {e: [] for e in ENGS}
        self.sems = {}
        self.counts = {}

    def sem(self, name):
        name = self.ALIAS.get(name, name)
        if name not in self.sems:
            self.sems[name] = self.nc.alloc_semaphore(name)
            self.counts[name] = 0
        return self.sems[name]

    def cnt(self, name):
        self.sem(name)
        return self.counts[self.ALIAS.get(name, name)]

    def on(self, eng, fn, wait=None, wait_exact=None, inc=None):
        waits = []
        for nm in (wait or []):
            h = self.sem(nm)
            waits.append((h, self.counts[self.ALIAS.get(nm, nm)]))
        for nm, v in (wait_exact or []):
            waits.append((self.sem(nm), v))
        incs = []
        for nm, n in (inc or []):
            incs.append((self.sem(nm), n))
            self.counts[self.ALIAS.get(nm, nm)] += n
        self.items[eng].append((waits, fn, incs))

    def wait(self, eng, names):
        self.on(eng, None, wait=names)

    def vf(self):
        self.on("vector", lambda e: e.nop(), inc=[("vfen", 1)])
        self.on("vector", None, wait=["vfen"])

    def barrier(self, engs=ENGS):
        nm = "bar"
        for e in engs:
            self.on(e, lambda en: en.nop(), inc=[(nm, 1)])
        tgt = self.counts[nm]
        for e in engs:
            self.on(e, None, wait_exact=[(nm, tgt)])

    def finalize(self):
        nc = self.nc
        with nc.Block() as block:
            def runner(items):
                def run(engine):
                    for waits, fn, incs in items:
                        for h, v in waits:
                            if v > 0:
                                engine.wait_ge(h, v)
                        if fn is not None:
                            inst = fn(engine)
                            if incs:
                                inst.then_inc(*incs[0])
                                for h, n in incs[1:]:
                                    engine.nop().then_inc(h, n)
                return run
            block.sync(runner(self.items["sync"]))
            block.tensor(runner(self.items["tensor"]))
            block.vector(runner(self.items["vector"]))
            block.scalar(runner(self.items["scalar"]))
            block.gpsimd(runner(self.items["gpsimd"]))


def ts(i, n=128):
    return slice(i * n, (i + 1) * n)


def build_nc(dbg=False):
    nc = bass.Bass("TRN2", target_bir_lowering=False, debug=False,
                   detect_race_conditions=False)

    dram = {}
    def din(name, shape, dtype=F32):
        dram[name] = nc.dram_tensor(name, list(shape), dtype, kind="ExternalInput")
        return dram[name]

    din("x", (S, D))
    for w in ("Wq", "Wk", "Wv", "Wo"):
        din("L0_" + w, (D, D))
    for b in ("bq", "bk", "bv", "bo"):
        din("L0_" + b, (D,))
    din("L0_slopes", (NHEAD,))
    din("gate_w", (MOE_E, D))
    din("w1", (MOE_E, D, MOE_HID))
    din("w2", (MOE_E, MOE_HID, D))
    din("w3", (MOE_E, D, MOE_HID))
    din("sw1", (D, SH_HID))
    din("sw2", (SH_HID, D))
    din("sw3", (D, SH_HID))
    for w in ("Wq", "Wk", "Wv"):
        din("L1_" + w, (D, D))
    for b in ("bq", "bk", "bv"):
        din("L1_" + b, (D,))
    din("w_gate", (L1_HID, D))
    din("w_up", (L1_HID, D))
    din("w_down", (D, L1_HID))
    din("tri", (128, 128), BF16)
    din("stri", (128, 128), BF16)
    din("iotaw", (128, 768), F32)
    din("fre", (D, NF), BF16)
    din("fim", (D, NF), BF16)
    din("gre", (NF, D), BF16)
    din("gim", (NF, D), BF16)

    out = nc.dram_tensor("out", [S, D], F32, kind="ExternalOutput")
    dbg_t = {}
    if dbg:
        dbg_t["stats"] = nc.dram_tensor("dbg_stats", [128, 6 * NT], F32, kind="ExternalOutput")
        dbg_t["xn"] = nc.dram_tensor("dbg_xn", [128, ND * S], BF16, kind="ExternalOutput")
        dbg_t["attn"] = nc.dram_tensor("dbg_attn", [128, NT * D], F32, kind="ExternalOutput")
        dbg_t["moe"] = nc.dram_tensor("dbg_moe", [128, NT * D], F32, kind="ExternalOutput")
        dbg_t["l1a"] = nc.dram_tensor("dbg_l1a", [128, NT * D], F32, kind="ExternalOutput")
        dbg_t["scr"] = nc.dram_tensor("dbg_scr", [ND, 128, L1_HID], BF16, kind="ExternalOutput")
        dbg_t["wt"] = nc.dram_tensor("dbg_wt", [128, ND * D], BF16, kind="ExternalOutput")
        dbg_t["qp"] = nc.dram_tensor("dbg_qp", [128, S], BF16, kind="ExternalOutput")
        dbg_t["kp"] = nc.dram_tensor("dbg_kp", [128, S], BF16, kind="ExternalOutput")
        dbg_t["va"] = nc.dram_tensor("dbg_va", [128, NT * 130], BF16, kind="ExternalOutput")
        dbg_t["ctxp"] = nc.dram_tensor("dbg_ctxp", [128, S], BF16, kind="ExternalOutput")

    scr = {
        "up": nc.dram_tensor("scr_upT", [ND, 128, L1_HID], BF16, kind="Internal"),
        "gate": nc.dram_tensor("scr_gateT", [ND, 128, L1_HID], BF16, kind="Internal"),
        "down": nc.dram_tensor("scr_downT", [L1_HID // 128, 128, D], BF16, kind="Internal"),
        "xn": nc.dram_tensor("scr_xn", [NT, 128, D], BF16, kind="Internal"),
    }
    cc_in = nc.dram_tensor("cc_in", [1, 2], F32, kind="Internal")
    cc_out = nc.dram_tensor("cc_out", [1, 2], F32, kind="Internal")

    p = Prog(nc)

    A = nc.alloc_sbuf_tensor
    x_res = A("x_res", [128, NT * D], F32)
    xn_fm = A("xn_fm", [128, ND * S], BF16)
    ident = A("ident", [128, 128], BF16)
    bq_sb = A("bq_sb", [128, ND], F32)
    bk_sb = A("bk_sb", [128, ND], F32)
    slopes_sb = A("slopes_sb", [1, NHEAD], BF16)
    identF = A("identF", [128, 128], F32)
    onesf = A("onesf", [1, 128], F32)
    iot = A("iot", [128, NT], F32)
    iot_i = A("iot_i", [128, NT], I32)
    sums = A("sums", [128, NT], F32)
    sumsq = A("sumsq", [128, NT], F32)
    m_sb = A("m_sb", [128, NT], F32)
    r_sb = A("r_sb", [128, NT], F32)
    stat_a = A("stat_a", [128, NT], F32)
    stat_b = A("stat_b", [128, NT], F32)
    dbg_s1 = A("dbg_s1", [128, NT], F32)
    dbg_s2 = A("dbg_s2", [128, NT], F32)
    dbg_s3 = A("dbg_s3", [128, NT], F32)
    dbg_s4 = A("dbg_s4", [128, NT], F32)

    PB = [nc.alloc_psum_tensor("pb%d" % i, [128, 512], F32) for i in range(8)]

    p.on("sync", lambda e: e.dma_start(
        out=x_res.ap().rearrange("p (t d) -> p t d", d=D),
        in_=dram["x"].ap().rearrange("(t p) d -> p t d", p=128)),
        inc=[("d_x", 16)])
    for nm, dst, rr in (("L0_bq", bq_sb, True), ("L0_bk", bk_sb, True)):
        p.on("sync", lambda e, nm=nm, dst=dst: e.dma_start(
            out=dst.ap(), in_=dram[nm].ap().rearrange("(c p) -> p c", p=128),
            allow_slow_non_contiguous=True),
            inc=[("d_sm", 16)])
    p.on("gpsimd", lambda e: e.dma_start(
        out=slopes_sb.ap(), in_=dram["L0_slopes"].ap().rearrange("(o d) -> o d", o=1)),
        inc=[("d_smg", 16)])
    p.on("vector", lambda e: e.memset(ident[:], 1.0), inc=[("c_id", 1)])
    p.on("gpsimd", lambda e: e.affine_select(
        ident[:], ident[:], [[1, 128]], ALU.is_equal, 0.0, base=0,
        channel_multiplier=-1), wait=["c_id"], inc=[("c_id", 1)])
    p.on("vector", lambda e: e.memset(identF[:], 1.0), inc=[("c_id", 1)])
    p.on("gpsimd", lambda e: e.affine_select(
        identF[:], identF[:], [[1, 128]], ALU.is_equal, 0.0, base=0,
        channel_multiplier=-1), wait=["c_id"], inc=[("c_id", 1)])
    p.on("vector", lambda e: e.memset(onesf[:], 1.0), inc=[("c_id", 1)])
    p.on("gpsimd", lambda e: e.iota(iot_i[:], [[128, NT]], base=-2047,
                                    channel_multiplier=1),
         inc=[("c_id", 1)])
    p.on("vector", lambda e: e.tensor_copy(iot[:], iot_i[:]), wait=["c_id"],
         inc=[("c_id", 1)])

    def emit_layernorm(tg, sqdump, xn_tm, first=False, spill=None):
        xnm = []
        s_sum, s_sq = "ln_s", "ln_q"
        s_v, s_r = "ln_v", "ln_r"
        s_tm, s_tr = "ln_tm", "ln_tr"
        s_evS, s_evV = "ln_evS", "ln_evV"
        p.on("scalar", lambda e: e.memzero(sumsq[:]), inc=[(s_sq, 1)])
        for t in range(NT):
            p.on("vector", lambda e, t=t: e.reduce_sum(
                sums[:, t:t + 1], x_res[:, ts(t, D)], axis=AXX),
                wait=(["d_x"] if (first and t == 0) else None), inc=[(s_sum, 1)])
            p.on("scalar", lambda e, t=t: e.activation(
                out=sqdump[:], in_=x_res[:, ts(t, D)], func=AF.Square,
                accum_out=sumsq[:, t:t + 1]),
                wait=(["d_x"] if (first and t == 0) else None), inc=[(s_sq, 1)])
        p.wait("vector", [s_sum, s_sq])
        p.on("vector", lambda e: e.tensor_scalar_mul(m_sb[:], sums[:], 1.0 / D))
        p.on("vector", lambda e: e.tensor_scalar_mul(stat_a[:], sumsq[:], 1.0 / D))
        p.vf()
        p.on("vector", lambda e: e.tensor_mul(stat_b[:], m_sb[:], m_sb[:]))
        p.vf()
        p.on("vector", lambda e: e.tensor_sub(stat_a[:], stat_a[:], stat_b[:]))
        p.vf()
        p.on("vector", lambda e: e.tensor_scalar_add(stat_a[:], stat_a[:], 1e-5),
             inc=[(s_v, 1)])
        p.on("vector", lambda e: e.tensor_copy(dbg_s1[:], stat_a[:]))
        p.on("scalar", lambda e: e.sqrt(stat_b[:], stat_a[:]), wait=[s_v],
             inc=[(s_v, 1)])
        p.on("vector", lambda e: e.reciprocal(r_sb[:], stat_b[:]), wait=[s_v],
             inc=[("rcp", 1)])
        p.wait("vector", ["rcp"])
        p.on("vector", lambda e: e.tensor_mul(stat_b[:], r_sb[:], r_sb[:]))
        p.vf()
        p.on("vector", lambda e: e.tensor_mul(stat_b[:], stat_b[:], stat_a[:]))
        p.vf()
        p.on("vector", lambda e: e.tensor_scalar(
            stat_b[:], stat_b[:], -0.5, 1.5, ALU.mult, ALU.add))
        p.vf()
        p.on("vector", lambda e: e.tensor_mul(r_sb[:], r_sb[:], stat_b[:]),
             inc=[(s_r, 1)])
        for t in range(NT):
            buf = xn_tm[:, ts(t % 2, D)]
            we = [(s_tr, p.cnt(s_tr) - ND)] if t >= 2 else []
            if spill is not None and t >= 2:
                we = we + [("d_xn", xnm[t - 2])]
            p.on("vector", lambda e, t=t, buf=buf: e.tensor_scalar(
                buf, x_res[:, ts(t, D)], m_sb[:, t:t + 1], r_sb[:, t:t + 1],
                ALU.subtract, ALU.mult),
                wait=([s_r] if t == 0 else None), wait_exact=we, inc=[(s_tm, 1)])
            if spill is not None:
                p.on("sync", lambda e, t=t, buf=buf: e.dma_start(
                    out=spill.ap()[t], in_=buf),
                    wait_exact=[(s_tm, p.cnt(s_tm))], inc=[("d_xn", 16)])
                xnm.append(p.cnt("d_xn"))
            for dc in range(ND):
                k = t * ND + dc
                pb = PB[k % 2]
                sev = s_evS if k % 2 == 0 else s_evV
                we2 = [(sev, p.cnt(sev))] if k >= 2 else []
                p.on("tensor", lambda e, buf=buf, dc=dc, pb=pb: e.matmul(
                    pb[:, 0:128], buf[:, ts(dc, 128)], ident[:],
                    start=True, stop=True),
                    wait_exact=[(s_tm, p.cnt(s_tm))] + we2, inc=[(s_tr, 1)])
                dst_lo = dc * S + t * 128
                if k % 2 == 0:
                    p.on("scalar", lambda e, pb=pb, dst_lo=dst_lo: e.activation(
                        out=xn_fm[:, dst_lo:dst_lo + 128], in_=pb[:, 0:128],
                        func=AF.Identity),
                        wait_exact=[(s_tr, p.cnt(s_tr))], inc=[(s_evS, 1)])
                else:
                    p.on("vector", lambda e, pb=pb, dst_lo=dst_lo: e.tensor_copy(
                        xn_fm[:, dst_lo:dst_lo + 128], pb[:, 0:128]),
                        wait_exact=[(s_tr, p.cnt(s_tr))], inc=[(s_evV, 1)])

    def emit_attn_l0(st):
        WT = st["WT"]; wstage = st["wstage"]
        q_pair = st["q_pair"]; k_pair = st["k_pair"]
        v_aug = st["v_aug"]; ctx_pair = st["ctx_pair"]
        e_sb = st["e_sb"]
        ones_row = st["ones_row"]; bv_row = st["bv_row"]; bo_row = st["bo_row"]
        alibi = st["alibi"]; slopes_bc = st["slopes_bc"]

        for nm, dst in (("L0_bv", bv_row), ("L0_bo", bo_row)):
            p.on("gpsimd", lambda e, nm=nm, dst=dst: e.dma_start(
                out=dst.ap(), in_=dram[nm].ap().rearrange("(o d) -> o d", o=1)),
                inc=[("d_smg", 16)])
        p.on("vector", lambda e: e.memset(ones_row[:], 1.0), inc=[("c_one", 1)])
        p.on("tensor", lambda e: e.matmul(
            PB[4][:, 0:NHEAD], ones_row[0:1, 0:128], slopes_sb[:],
            start=True, stop=True),
            wait=["d_smg", "c_one", "c_id"], inc=[("c_alibi", 1)])
        p.on("vector", lambda e: e.tensor_copy(slopes_bc[:], PB[4][:, 0:NHEAD]),
             wait_exact=[("c_alibi", p.cnt("c_alibi"))], inc=[("c_alibi", 1)])
        p.vf()
        for h in range(NHEAD):
            p.on("vector", lambda e, h=h: e.tensor_mul(
                alibi[:, h * NT:(h + 1) * NT], iot[:],
                slopes_bc[:, h:h + 1].broadcast_to([128, NT])),
                inc=[("c_alibi", 1)])

        for wi, w in enumerate(("Wq", "Wk", "Wv", "Wo")):
            wst_marks = []
            for half in range(2):
                p.on("gpsimd", lambda e, w=w, half=half: e.dma_start(
                    out=wstage.ap().rearrange("p (c i) -> p c i", i=D),
                    in_=dram["L0_" + w].ap().rearrange("(c p) i -> p c i", p=128)
                        [:, half * 4:(half + 1) * 4, :]),
                    wait_exact=[("wt_tr", wi * 64 + half * 32)],
                    inc=[("d_wstage", 16)])
                wst_marks.append(p.cnt("d_wstage"))
            for oc in range(ND):
                for ic in range(ND):
                    k = wi * 64 + oc * ND + ic
                    pb = PB[2 + (k % 2)]
                    sev = "wt_evS" if k % 2 == 0 else "wt_evV"
                    we = [(sev, p.cnt(sev))] if k >= 2 else []
                    p.on("tensor", lambda e, oc=oc, ic=ic, pb=pb: e.matmul(
                        pb[:, 0:128],
                        wstage[:, (oc % 4) * D + ic * 128: (oc % 4) * D + (ic + 1) * 128],
                        ident[:], start=True, stop=True),
                        wait_exact=[("d_wstage", wst_marks[oc // 4])] + we,
                        inc=[("wt_tr", 1)])
                    dst = WT[w][:, ic * D + oc * 128: ic * D + (oc + 1) * 128]
                    if k % 2 == 0:
                        p.on("scalar", lambda e, pb=pb, dst=dst: e.activation(
                            out=dst, in_=pb[:, 0:128], func=AF.Identity),
                            wait_exact=[("wt_tr", p.cnt("wt_tr"))],
                            inc=[("wt_evS", 1), ("wt_done", 1)])
                    else:
                        p.on("vector", lambda e, pb=pb, dst=dst: e.tensor_copy(
                            dst, pb[:, 0:128]),
                            wait_exact=[("wt_tr", p.cnt("wt_tr"))],
                            inc=[("wt_evV", 1), ("wt_done", 1)])

        for pr in range(NHEAD // 2):
            for name, dstt, bias, pbq in (("Wq", q_pair, bq_sb, PB[0]),
                                          ("Wk", k_pair, bk_sb, PB[1])):
                for tc in range(4):
                    for kc in range(ND):
                        wex = []
                        if tc == 0 and kc == 0:
                            wex = [("wt_done", (1 if name == "Wq" else 2) * 64),
                                   ("ex_0", p.cnt("ex_0")), ("ex_1", p.cnt("ex_1")),
                                   ("wo_ev", p.cnt("wo_ev"))]
                            if dbg:
                                wex += [("dbgd", p.cnt("dbgd"))]
                        elif kc == 0:
                            wex = [("qk_ev", p.cnt("qk_ev"))]
                        p.on("tensor", lambda e, pbq=pbq, name=name, kc=kc, pr=pr, tc=tc: e.matmul(
                            pbq[:], WT[name][:, kc * D + pr * 128: kc * D + (pr + 1) * 128],
                            xn_fm[:, kc * S + tc * 512: kc * S + (tc + 1) * 512],
                            start=(kc == 0), stop=(kc == ND - 1)),
                            wait_exact=wex,
                            inc=([("qk_mm", 1)] if kc == ND - 1 else None))
                    p.on("scalar", lambda e, pbq=pbq, dstt=dstt, bias=bias, pr=pr, tc=tc: e.activation(
                        out=dstt[:, tc * 512:(tc + 1) * 512], in_=pbq[:],
                        func=AF.Identity, bias=bias[:, pr:pr + 1]),
                        wait=(["d_sm"] if (pr == 0 and tc == 0) else None),
                        wait_exact=[("qk_mm", p.cnt("qk_mm"))], inc=[("qk_ev", 1)])
            p.on("vector", lambda e: e.memset(
                v_aug.ap().rearrange("p (t c) -> p t c", c=130)[:, :, 64:65], 1.0),
                wait_exact=[("cx_0", p.cnt("cx_0")), ("cx_1", p.cnt("cx_1"))],
                inc=[("v_ones", 1)])
            p.on("vector", lambda e: e.memset(
                v_aug.ap().rearrange("p (t c) -> p t c", c=130)[:, :, 129:130], 1.0),
                inc=[("v_ones", 1)])
            for tt in range(NT):
                pb = PB[6 + tt % 2]
                for kc in range(ND):
                    wex = []
                    if tt == 0 and kc == 0:
                        wex = [("wt_done", 3 * 64),
                               ("cx_0", p.cnt("cx_0")), ("cx_1", p.cnt("cx_1"))]
                    elif kc == 0 and tt >= 2:
                        wex = [("v_ev", p.cnt("v_ev") - 1)]
                    p.on("tensor", lambda e, pb=pb, kc=kc, tt=tt, pr=pr: e.matmul(
                        pb[:, 0:128],
                        xn_fm[:, kc * S + tt * 128: kc * S + (tt + 1) * 128],
                        WT["Wv"][:, kc * D + pr * 128: kc * D + (pr + 1) * 128],
                        start=(kc == 0), stop=False), wait_exact=wex)
                p.on("tensor", lambda e, pb=pb, pr=pr: e.matmul(
                    pb[:, 0:128], ones_row[0:1, 0:128],
                    bv_row[0:1, pr * 128:(pr + 1) * 128], start=False, stop=True),
                    wait=((["d_smg", "c_one"]) if pr == 0 and tt == 0 else None),
                    inc=[("v_mm", 1)])
                p.on("vector", lambda e, pb=pb, tt=tt: e.tensor_copy(
                    v_aug.ap().rearrange("p (t c) -> p t c", c=130)
                         [:, tt:tt + 1, :].rearrange("p t (h c) -> p (t h) c", c=65)
                         [:, :, 0:64],
                    pb[:, 0:128].rearrange("p (h c) -> p h c", c=64)),
                    wait_exact=[("v_mm", p.cnt("v_mm"))], inc=[("v_ev", 1)])
            pending_evac = []
            for qc in range(4):
                HHL = []
                for hh in range(2):
                    HHL.append(dict(
                        hh=hh, hr=slice(hh * 64, hh * 64 + 64), h=pr * 2 + hh,
                        pb_s=(PB[0 + hh], PB[2 + hh]), pb_ctx=PB[4 + hh],
                        pb_bc=PB[6 + hh], ctxr=st["ctxr"][hh],
                        r_row=st["r_row"][hh], exm={}))

                def _pv2(L, kc, exc, qc=qc):
                    hh = L["hh"]
                    ex_s, pvr, cp_s = "ex_%d" % hh, "pvr_%d" % hh, "cp_%d" % hh
                    pb_ctx = L["pb_ctx"]
                    eb = e_sb[:, (hh * 4 + kc % 4) * 512:(hh * 4 + kc % 4 + 1) * 512]
                    wex2 = [(ex_s, exc)]
                    if kc == 0:
                        wex2 += [(cp_s, p.cnt(cp_s))]
                        if qc <= 1:
                            wex2 += [("v_ev", p.cnt("v_ev"))]
                    p.on("tensor", lambda e, pb_ctx=pb_ctx, eb=eb, kc=kc, hh=hh: e.matmul(
                        pb_ctx[0:65, :],
                        v_aug[:, kc * 130 + hh * 65: kc * 130 + hh * 65 + 65],
                        eb, start=(kc == 0), stop=(kc == NT - 1)),
                        wait_exact=wex2, inc=[(pvr, 1)])

                for kc in range(NT):
                    for L in HHL:
                        hh = L["hh"]
                        sc_s, ex_s = "sc_%d" % hh, "ex_%d" % hh
                        pbs = L["pb_s"][kc % 2]
                        wex = [(ex_s, max(0, p.cnt(ex_s) - 1))]
                        if kc == 0 and qc == 0:
                            wex += [("qk_ev", p.cnt("qk_ev")),
                                    ("v_ev", p.cnt("v_ev")),
                                    ("v_ones", p.cnt("v_ones")),
                                    ("c_alibi", 18)]
                        p.on("tensor", lambda e, pbs=pbs, hr=L["hr"], kc=kc, qc=qc: e.matmul(
                            pbs[:], k_pair[hr, ts(kc, 128)], q_pair[hr, ts(qc, 512)],
                            start=True, stop=True),
                            wait_exact=wex, inc=[(sc_s, 1)])
                        eb = e_sb[:, (hh * 4 + kc % 4) * 512:(hh * 4 + kc % 4 + 1) * 512]
                        p.on("scalar", lambda e, pbs=pbs, eb=eb, h=L["h"], kc=kc: e.activation(
                            out=eb, in_=pbs[:], func=AF.Exp, scale=0.125,
                            bias=alibi[:, h * NT + kc: h * NT + kc + 1]),
                            wait_exact=[(sc_s, p.cnt(sc_s))], inc=[(ex_s, 1)])
                        L["exm"][kc] = p.cnt(ex_s)
                    if kc >= 1:
                        for L in HHL:
                            _pv2(L, kc - 1, L["exm"][kc - 1])
                for L in HHL:
                    _pv2(L, NT - 1, L["exm"][NT - 1])

                for L in HHL:
                    hh = L["hh"]
                    pb_ctx = L["pb_ctx"]; pb_bc = L["pb_bc"]
                    ctxr = L["ctxr"]; r_row = L["r_row"]
                    pvr, cx_s = "pvr_%d" % hh, "cx_%d" % hh
                    cp_s = "cp_%d" % hh
                    p.on("vector", lambda e, pb_ctx=pb_ctx, ctxr=ctxr: e.tensor_copy(
                        ctxr[:], pb_ctx[0:65, :]),
                        wait_exact=[(pvr, p.cnt(pvr)),
                                    (cx_s, p.cnt(cx_s))], inc=[(cp_s, 1)])
                    p.vf()
                    p.on("vector", lambda e, ctxr=ctxr: e.reciprocal(
                        ctxr[64:65, :], ctxr[64:65, :]), inc=[("rcp", 1)])
                    p.wait("vector", ["rcp"])
                    p.on("vector", lambda e, ctxr=ctxr, r_row=r_row: e.tensor_copy(
                        r_row[:], ctxr[64:65, :]), inc=[("rc_%d" % hh, 1)])

                    def _evac(hh=hh, qc=qc, pb_bc=pb_bc, ctxr=ctxr, r_row=r_row,
                              rc_t=p.cnt("rc_%d" % hh)):
                        cx_s2 = "cx_%d" % hh
                        p.on("tensor", lambda e: e.matmul(
                            pb_bc[0:64, :], ones_row[0:1, 0:64], r_row[:],
                            start=True, stop=True),
                            wait_exact=[("rc_%d" % hh, rc_t),
                                        (cx_s2, p.cnt(cx_s2))],
                            inc=[("rb_%d" % hh, 1)])
                        p.on("vector", lambda e: e.tensor_mul(
                            ctx_pair[slice(hh * 64, hh * 64 + 64), ts(qc, 512)],
                            ctxr[0:64, :], pb_bc[0:64, :]),
                            wait_exact=[("rb_%d" % hh, p.cnt("rb_%d" % hh))],
                            inc=[(cx_s2, 1)])
                    pending_evac.append(_evac)
                    if len(pending_evac) > 1:
                        pending_evac.pop(0)()
            while pending_evac:
                pending_evac.pop(0)()
            if dbg and pr == 0:
                p.on("sync", lambda e: e.dma_start(
                    out=dbg_t["wt"].ap(), in_=WT["Wq"].ap()),
                    wait_exact=[("wt_done", 64)], inc=[("dbgd", 16)])
                p.on("sync", lambda e: e.dma_start(
                    out=dbg_t["qp"].ap(), in_=q_pair.ap()),
                    wait_exact=[("qk_ev", p.cnt("qk_ev"))], inc=[("dbgd", 16)])
                p.on("sync", lambda e: e.dma_start(
                    out=dbg_t["kp"].ap(), in_=k_pair.ap()), inc=[("dbgd", 16)])
                p.on("sync", lambda e: e.dma_start(
                    out=dbg_t["va"].ap(), in_=v_aug.ap()),
                    wait_exact=[("v_ev", p.cnt("v_ev"))], inc=[("dbgd", 16)])
                p.on("sync", lambda e: e.dma_start(
                    out=dbg_t["ctxp"].ap(), in_=ctx_pair.ap()),
                    wait_exact=[("cx_0", p.cnt("cx_0")), ("cx_1", p.cnt("cx_1"))],
                    inc=[("dbgd", 16)])
            for tt in range(NT):
                for oc in range(2):
                    j = tt * 2 + oc
                    pbo = PB[j % 2]
                    wex = []
                    if j == 0:
                        wex = [("cx_0", p.cnt("cx_0")), ("cx_1", p.cnt("cx_1")),
                               ("wt_done", 4 * 64),
                               ("ex_0", p.cnt("ex_0")), ("ex_1", p.cnt("ex_1"))]
                    elif j >= 2:
                        wex = [("wo_ev", p.cnt("wo_ev") - 1)]
                    if pr == 0:
                        p.on("tensor", lambda e, pbo=pbo, oc=oc: e.matmul(
                            pbo[:], ones_row[0:1, 0:128],
                            bo_row[0:1, oc * 512:(oc + 1) * 512],
                            start=True, stop=False),
                            wait=(["d_smg"] if j == 0 else None), wait_exact=wex)
                        wex = []
                    p.on("tensor", lambda e, pbo=pbo, tt=tt, oc=oc, pr=pr: e.matmul(
                        pbo[:], ctx_pair[:, ts(tt, 128)],
                        WT["Wo"][:, pr * D + oc * 512: pr * D + (oc + 1) * 512],
                        start=(pr != 0), stop=True),
                        wait_exact=wex, inc=[("wo_mm", 1)])
                    p.on("vector", lambda e, pbo=pbo, tt=tt, oc=oc: e.scalar_tensor_tensor(
                        x_res[:, tt * D + oc * 512: tt * D + (oc + 1) * 512],
                        pbo[:], 1.0,
                        x_res[:, tt * D + oc * 512: tt * D + (oc + 1) * 512],
                        ALU.mult, ALU.add),
                        wait_exact=[("wo_mm", p.cnt("wo_mm"))], inc=[("wo_ev", 1)])

    def emit_moe(st, filler=None):
        gw_stage = st["gw_stage"]; gwT = st["gwT"]
        logit_sb = st["logit_sb"]; comb = st["comb"]
        g_t0 = st["g_t0"]; g_t1 = st["g_t1"]; g_c0 = st["g_c0"]; g_c1 = st["g_c1"]
        w1g = st["w1g"]; w3g = st["w3g"]; w2g = st["w2g"]
        h_sb = st["h_sb"]
        hsil = st["hsil"]
        tri_sb = st["tri_sb"]; stri_sb = st["stri_sb"]; iota_sb = st["iota_sb"]
        route_m = st["route_m"]; t_sb = st["t_sb"]; pos = st["pos"]
        onesb = st["onesb"]; ccol_sb = st["ccol_sb"]; base_sb = st["base_sb"]
        pt_sb = st["pt_sb"]; ptT_sb = st["ptT_sb"]
        xt_sb = st["xt_sb"]; gth_sb = st["gth_sb"]
        gath_fm = st["gath_fm"]; y_sb = st["y_sb"]
        yb = gath_fm

        p.on("gpsimd", lambda e: e.dma_start(
            out=gw_stage[0:MOE_E, :], in_=dram["gate_w"].ap()),
            inc=[("d_gw", 16)])
        gw_mark = p.cnt("d_gw")
        for kc in range(ND):
            p.on("tensor", lambda e, kc=kc: e.matmul(
                PB[0][:, 0:8], gw_stage[0:MOE_E, ts(kc, 128)], ident[0:8, 0:8],
                start=True, stop=True),
                wait_exact=([("d_gw", gw_mark)] if kc == 0 else
                            [("gw_ev", p.cnt("gw_ev"))]),
                inc=[("gw_tr", 1)])
            p.on("vector", lambda e, kc=kc: e.tensor_copy(
                gwT[:, kc * 8:(kc + 1) * 8], PB[0][:, 0:8]),
                wait_exact=[("gw_tr", p.cnt("gw_tr"))], inc=[("gw_ev", 1)])
        for tt in range(NT):
            for kc in range(ND):
                p.on("tensor", lambda e, tt=tt, kc=kc: e.matmul(
                    PB[1][:, 0:8], xn_fm[:, kc * S + tt * 128: kc * S + (tt + 1) * 128],
                    gwT[:, kc * 8:(kc + 1) * 8],
                    start=(kc == 0), stop=(kc == ND - 1)),
                    wait_exact=([("gw_ev", p.cnt("gw_ev"))] if (tt == 0 and kc == 0) else
                                ([("gl_ev", p.cnt("gl_ev"))] if kc == 0 else [])),
                    inc=([("gl_mm", 1)] if kc == ND - 1 else None))
            p.on("vector", lambda e: e.tensor_copy(logit_sb[:], PB[1][:, 0:8]),
                 wait_exact=[("gl_mm", p.cnt("gl_mm"))], inc=[("gl_ev", 1)])
            p.vf()
            p.on("vector", lambda e: e.reduce_max(g_c0[:], logit_sb[:], axis=AXX))
            p.vf()
            p.on("vector", lambda e: e.tensor_scalar_mul(g_c0[:], g_c0[:], -1.0),
                 inc=[("g_nm", 1)])
            p.on("scalar", lambda e: e.activation(
                out=g_t0[:], in_=logit_sb[:], func=AF.Exp, bias=g_c0[:]),
                wait_exact=[("g_nm", p.cnt("g_nm"))], inc=[("g_ex", 1)])
            p.on("vector", lambda e: e.reduce_sum(g_c0[:], g_t0[:], axis=AXX),
                 wait_exact=[("g_ex", p.cnt("g_ex"))])
            p.vf()
            p.on("vector", lambda e: e.reciprocal(g_c1[:], g_c0[:]),
                 inc=[("rcp", 1)])
            p.wait("vector", ["rcp"])
            p.on("vector", lambda e: e.tensor_scalar_mul(g_t0[:], g_t0[:], g_c1[:]))
            p.vf()
            p.on("vector", lambda e: e.reduce_max(g_c0[:], g_t0[:], axis=AXX))
            p.vf()
            p.on("vector", lambda e: e.tensor_single_scalar(
                g_t1[:], g_t0[:], g_c0[:], ALU.is_lt))
            p.vf()
            p.on("vector", lambda e: e.tensor_mul(g_t1[:], g_t0[:], g_t1[:]))
            p.vf()
            p.on("vector", lambda e: e.reduce_max(g_c1[:], g_t1[:], axis=AXX))
            p.vf()
            p.on("vector", lambda e: e.tensor_single_scalar(
                g_t1[:], g_t0[:], g_c1[:], ALU.is_ge))
            p.vf()
            p.on("vector", lambda e, tt=tt: e.tensor_mul(
                comb[:, tt * 8:(tt + 1) * 8], g_t0[:], g_t1[:]),
                inc=[("comb_d", 1)])

        for nm, dst in (("tri", tri_sb), ("stri", stri_sb), ("iotaw", iota_sb)):
            p.on("sync", lambda e, nm=nm, dst=dst: e.dma_start(
                out=dst.ap(), in_=dram[nm].ap()), inc=[("d_ct", 16)])
        ct_mark = p.cnt("d_ct")
        p.on("vector", lambda e: e.memset(onesb[:], 1.0), inc=[("rt", 1)])
        p.on("vector", lambda e: e.tensor_scalar(
            route_m[:], comb[:], 0.0, 0.0, ALU.is_gt, ALU.bypass),
            wait_exact=[("comb_d", 16), ("d_ct", ct_mark)], inc=[("rt", 1)])
        p.on("tensor", lambda e: e.matmul(PB[0][:, 0:128], tri_sb[:], route_m[:],
                                          start=True, stop=True),
             wait_exact=[("rt", p.cnt("rt")), ("dq_ev", p.cnt("dq_ev"))],
             inc=[("g_tr", 1)])
        p.on("tensor", lambda e: e.matmul(PB[1][:, 0:1], route_m[:], onesb[:],
                                          start=True, stop=True),
             wait_exact=[("gl_ev", p.cnt("gl_ev"))], inc=[("g_tr", 1)])
        p.on("vector", lambda e: e.tensor_sub(t_sb[:], PB[0][:, 0:128], route_m[:]),
             wait_exact=[("g_tr", p.cnt("g_tr"))], inc=[("rt", 1)])
        p.on("vector", lambda e: e.tensor_copy(ccol_sb[:], PB[1][:, 0:1]),
             inc=[("rt", 1)])
        p.on("tensor", lambda e: e.matmul(PB[2][0:1, 0:128], ccol_sb[:], stri_sb[:],
                                          start=True, stop=True),
             wait_exact=[("rt", p.cnt("rt"))], inc=[("g_tr", 1)])
        p.on("vector", lambda e: e.tensor_copy(base_sb[0:1, :], PB[2][0:1, 0:128]),
             wait_exact=[("g_tr", p.cnt("g_tr"))], inc=[("rt", 1)])
        p.on("tensor", lambda e: e.matmul(
            PB[3][:, 0:128], onesf[0:1, 0:128], base_sb[0:1, :],
            start=True, stop=True),
             wait_exact=[("rt", p.cnt("rt"))], inc=[("g_tr", 1)])
        p.on("vector", lambda e: e.tensor_add(t_sb[:], t_sb[:], PB[3][:, 0:128]),
             wait_exact=[("g_tr", p.cnt("g_tr"))])
        p.vf()
        p.on("vector", lambda e: e.scalar_tensor_tensor(
            pos[:], t_sb[:], 1.0, route_m[:], ALU.add, ALU.mult))
        p.vf()
        p.on("vector", lambda e: e.tensor_scalar_add(pos[:], pos[:], -1.0),
             inc=[("pos_d", 1)])
        pmark = p.cnt("pos_d")
        p.vf()

        CAP = 768
        state = {"drained": 0, "gi": 0, "wy": 0, "hseq": 0}
        fh = [(0, 0), (0, 0)]
        NGRP = 8 * 8 + 4

        def drain():
            tgt = (144 * (state["gi"]) + NGRP - 1) // NGRP
            while filler is not None and state["drained"] < tgt:
                if next(filler, None) is None and state["drained"] < 144:
                    pass
                state["drained"] += 1

        for e_i in range(MOE_E):
            for dc in range(2):
                for tt in range(NT):
                    s = e_i * 32 + dc * NT + tt
                    ptb = pt_sb[tt % 2]
                    xt = xt_sb[tt % 2]
                    wex = [("gstep", s - 1)] if s >= 2 else [("d_xn", 16 * NT)]
                    p.on("sync", lambda e, xt=xt, tt=tt, dc=dc: e.dma_start(
                        out=xt[:], in_=scr["xn"].ap()[tt][:, dc * 512:(dc + 1) * 512]),
                        wait_exact=wex, inc=[("d_xt", 16)])
                    xtmark = p.cnt("d_xt")
                    for sc in range(6):
                        wex = []
                        if sc == 0:
                            if s >= 2:
                                wex = [("gstep", s - 1)]
                            if dc == 0 and tt == 0:
                                wex += [("pos_d", pmark), ("s_tr", p.cnt("s_tr"))]
                        p.on("vector", lambda e, ptb=ptb, sc=sc, tt=tt, e_i=e_i:
                             e.tensor_single_scalar(
                                 ptb[:, ts(sc, 128)], iota_sb[:, ts(sc, 128)],
                                 pos[:, tt * 8 + e_i: tt * 8 + e_i + 1], ALU.is_equal),
                             wait_exact=wex, inc=([("g_pt", 1)] if sc == 5 else None))
                    for sc in range(6):
                        wex = []
                        if sc == 0:
                            wex = [("g_pt", p.cnt("g_pt")), ("d_xt", xtmark)]
                            if tt == 0 and dc == 0:
                                wex += [("my_ev", p.cnt("my_ev")),
                                        ("mh_ev", p.cnt("mh_ev")),
                                        ("mh_sil", p.cnt("mh_sil"))]
                            if tt == 0 and dc == 1:
                                wex += [("g_ev", p.cnt("g_ev"))]
                        p.on("tensor", lambda e, ptb=ptb, sc=sc, tt=tt, dc=dc, xt=xt:
                             e.matmul(PB[sc][:, 0:512], ptb[:, ts(sc, 128)], xt[:],
                                      start=(tt == 0), stop=(tt == NT - 1)),
                             wait_exact=wex,
                             inc=([("gstep", 1)] if sc == 5 else None))
                evmark = {}
                for sc in range(7):
                    if sc < 6:
                        g = gth_sb[sc % 2]
                        jl = dc * 6 + sc
                        wex = [("gstep", p.cnt("gstep"))]
                        if jl >= 2:
                            wex += [("g_gf", p.cnt("g_gf"))]
                        p.on("vector", lambda e, g=g, sc=sc: e.tensor_copy(
                            g[:], PB[sc][:, 0:512]), wait_exact=wex, inc=[("g_ev", 1)])
                        evmark[sc] = p.cnt("g_ev")
                    if sc >= 1:
                        sp = sc - 1
                        g = gth_sb[sp % 2]
                        jl = dc * 6 + sp
                        for ds in range(4):
                            k4 = jl * 4 + ds
                            pbt = PB[6 + k4 % 2]
                            wex = [("g_ev", evmark[sp])]
                            if jl == 0 and ds == 0:
                                wex += [("dq_ev", p.cnt("dq_ev")),
                                        ("mh_ev", p.cnt("mh_ev")),
                                        ("mh_sil", p.cnt("mh_sil")),
                                        ("g_gf", p.cnt("g_gf"))]
                            elif k4 >= 2:
                                wex += [("g_gf", p.cnt("g_gf") - 1)]
                            p.on("tensor", lambda e, pbt=pbt, g=g, ds=ds: e.matmul(
                                pbt[:, 0:128], g[:, ts(ds, 128)], ident[:],
                                start=True, stop=True),
                                wait_exact=wex, inc=[("g_tr", 1)])
                            kc = dc * 4 + ds
                            p.on("scalar", lambda e, pbt=pbt, kc=kc, sp=sp: e.activation(
                                out=gath_fm[:, kc * CAP + sp * 128: kc * CAP + sp * 128 + 128],
                                in_=pbt[:, 0:128], func=AF.Identity),
                                wait_exact=[("g_tr", p.cnt("g_tr"))], inc=[("g_gf", 1)])

            gplan = [3, 3, 3, 3, 3, 3, 3, 1]
            g0 = 0
            hb = h_sb
            for g, ghc in enumerate(gplan):
                q = state["gi"] % 2
                in1 = dram["w1"].ap()[e_i].rearrange(
                    "(kc p) h -> p kc h", p=128)[:, :, g0 * 128:(g0 + ghc) * 128]
                in3 = dram["w3"].ap()[e_i].rearrange(
                    "(kc p) h -> p kc h", p=128)[:, :, g0 * 128:(g0 + ghc) * 128]
                wex = [("mh1_mm", fh[0][0]), ("mh3_mm", fh[0][1])]
                p.on("gpsimd", lambda e, q=q, ghc=ghc, in1=in1: e.dma_start(
                    out=w1g[q].ap().rearrange("p (kc h) -> p kc h", kc=ND)[:, :, 0:ghc * 128],
                    in_=in1), wait_exact=wex, inc=[("d_wg", 16)])
                p.on("gpsimd", lambda e, q=q, ghc=ghc, in3=in3: e.dma_start(
                    out=w3g[q].ap().rearrange("p (kc h) -> p kc h", kc=ND)[:, :, 0:ghc * 128],
                    in_=in3), inc=[("d_wg", 16)])
                if g % 2 == 0:
                    yhc = ghc + (gplan[g + 1] if g + 1 < len(gplan) else 0)
                    in2 = dram["w2"].ap()[e_i].rearrange(
                        "(hc p) o -> p hc o", p=128)[:, g0:g0 + yhc, :]
                    p.on("gpsimd", lambda e, yhc=yhc, in2=in2: e.dma_start(
                        out=w2g.ap().rearrange("p (hc o) -> p hc o", o=D)[:, 0:yhc, :],
                        in_=in2), wait_exact=[("my_mm", state["wy"])],
                        inc=[("d_wg", 16)])
                dma_tgt = p.cnt("d_wg")
                for tg in range(2):
                    for hcl in range(ghc):
                        hcp = (g % 2) * 3 + hcl
                        hsq = state["hseq"]
                        state["hseq"] += 1
                        pb1 = PB[4 + 2 * (hsq % 2)]
                        pb3 = PB[5 + 2 * (hsq % 2)]
                        for kc in range(ND):
                            wex = [("d_wg", dma_tgt)] if (tg == 0 and hcl == 0 and kc == 0) else []
                            if kc == 0:
                                wex += [("mh_sil", max(0, p.cnt("mh_sil") - 1))]
                                if hsq % 2 == 1:
                                    wex += [("dq_ev", p.cnt("dq_ev")),
                                            ("g_gf", p.cnt("g_gf"))]
                            p.on("tensor", lambda e, pb1=pb1, q=q, kc=kc, hcl=hcl, tg=tg:
                                 e.matmul(pb1[:, 0:384],
                                          w1g[q][:, kc * 384 + hcl * 128: kc * 384 + (hcl + 1) * 128],
                                          gath_fm[:, kc * CAP + tg * 384: kc * CAP + (tg + 1) * 384],
                                          start=(kc == 0), stop=(kc == ND - 1)),
                                 wait_exact=wex,
                                 inc=([("mh1_mm", 1)] if kc == ND - 1 else None))
                        for kc in range(ND):
                            wex = []
                            if kc == 0:
                                wex = [("mh_ev", max(0, p.cnt("mh_ev") - 1))]
                                if hsq % 2 == 1:
                                    wex += [("dq_ev", p.cnt("dq_ev")),
                                            ("g_gf", p.cnt("g_gf"))]
                            p.on("tensor", lambda e, pb3=pb3, q=q, kc=kc, hcl=hcl, tg=tg:
                                 e.matmul(pb3[:, 0:384],
                                          w3g[q][:, kc * 384 + hcl * 128: kc * 384 + (hcl + 1) * 128],
                                          gath_fm[:, kc * CAP + tg * 384: kc * CAP + (tg + 1) * 384],
                                          start=(kc == 0), stop=(kc == ND - 1)),
                                 wait_exact=wex,
                                 inc=([("mh3_mm", 1)] if kc == ND - 1 else None))
                        hs = hsil[hsq % 2]
                        p.on("scalar", lambda e, pb1=pb1, hs=hs: e.activation(
                            out=hs[:, 0:384], in_=pb1[:, 0:384], func=AF.Silu),
                            wait_exact=[("mh1_mm", p.cnt("mh1_mm")),
                                        ("mh_ev", max(0, p.cnt("mh_ev") - 1))],
                            inc=[("mh_sil", 1)])
                        wex = [("mh3_mm", p.cnt("mh3_mm")),
                               ("mh_sil", p.cnt("mh_sil"))]
                        if g % 2 == 0 and tg == 0 and hcl == 0:
                            wex.append(("my_mm", p.cnt("my_mm")))
                        p.on("vector", lambda e, pb3=pb3, hs=hs, hcp=hcp, tg=tg: e.tensor_mul(
                            hb[:, hcp * CAP + tg * 384: hcp * CAP + (tg + 1) * 384],
                            hs[:, 0:384], pb3[:, 0:384]),
                            wait_exact=wex, inc=[("mh_ev", 1)])
                fh = [fh[1], (p.cnt("mh1_mm"), p.cnt("mh3_mm"))]
                g0 += ghc
                state["gi"] += 1
                drain()
                if g % 2 == 1:
                    yp = g // 2
                    yhc = gplan[g - 1] + gplan[g]
                    ysnap = p.cnt("my_ev")
                    ymark = {}
                    for oc in range(2):
                        for m in range(6):
                            k = oc * 6 + m
                            pby = PB[k % 4]
                            for hcp in range(yhc):
                                wex = []
                                if hcp == 0:
                                    if k == 0:
                                        wex = [("mh_ev", p.cnt("mh_ev")),
                                               ("my_ev", ysnap),
                                               ("g_ev", p.cnt("g_ev"))]
                                    elif k >= 4:
                                        wex = [("my_ev", ymark[k - 4])]
                                    else:
                                        wex = [("my_ev", ysnap)]
                                p.on("tensor", lambda e, pby=pby, hcp=hcp, m=m, oc=oc, yhc=yhc:
                                     e.matmul(pby[:, 0:512],
                                              hb[:, hcp * CAP + m * 128: hcp * CAP + (m + 1) * 128],
                                              w2g[:, hcp * D + oc * 512: hcp * D + oc * 512 + 512],
                                              start=(hcp == 0), stop=(hcp == yhc - 1)),
                                     wait_exact=wex,
                                     inc=([("my_mm", 1)] if hcp == yhc - 1 else None))
                            lo = m * D + oc * 512
                            if yp == 0:
                                p.on("vector", lambda e, pby=pby, lo=lo: e.tensor_copy(
                                    y_sb[:, lo:lo + 512], pby[:, 0:512]),
                                    wait_exact=[("my_mm", p.cnt("my_mm"))],
                                    inc=[("my_ev", 1)])
                            elif yp < 3:
                                p.on("vector", lambda e, pby=pby, lo=lo: e.tensor_add(
                                    y_sb[:, lo:lo + 512], y_sb[:, lo:lo + 512],
                                    pby[:, 0:512]),
                                    wait_exact=[("my_mm", p.cnt("my_mm"))],
                                    inc=[("my_ev", 1)])
                            else:
                                p.on("vector", lambda e, pby=pby, lo=lo: e.tensor_add(
                                    yb[:, lo:lo + 512], y_sb[:, lo:lo + 512],
                                    pby[:, 0:512]),
                                    wait_exact=[("my_mm", p.cnt("my_mm"))],
                                    inc=[("my_ev", 1)])
                            ymark[k] = p.cnt("my_ev")
                    state["wy"] = p.cnt("my_mm")

            ssnap = p.cnt("my_ev")
            scmark = {}
            smmark = {}
            strmark = {}
            temark = {}
            for tt in range(NT + 1):
                if tt < NT:
                    ptb = ptT_sb[tt % 2]
                    srcb = pt_sb[tt % 2]
                    for sc in range(6):
                        wex = []
                        if sc == 0:
                            if tt >= 2:
                                wex = [("s_tr", strmark[tt - 2])]
                            else:
                                wex = [("gstep", p.cnt("gstep"))]
                        p.on("vector", lambda e, srcb=srcb, sc=sc, tt=tt, e_i=e_i:
                             e.tensor_single_scalar(
                                 srcb[:, ts(sc, 128)], iota_sb[:, ts(sc, 128)],
                                 pos[:, tt * 8 + e_i: tt * 8 + e_i + 1], ALU.is_equal),
                             wait_exact=wex, inc=([("s_pt", 1)] if sc == 5 else None))
                    for sc in range(6):
                        k6 = tt * 6 + sc
                        pbt = PB[6 + k6 % 2]
                        wex = [("s_pt", p.cnt("s_pt"))] if sc == 0 else []
                        if tt == 0 and sc == 0:
                            wex += [("dq_ev", p.cnt("dq_ev")),
                                    ("mh_ev", p.cnt("mh_ev")),
                                    ("mh_sil", p.cnt("mh_sil")),
                                    ("g_gf", p.cnt("g_gf"))]
                        elif k6 >= 2:
                            wex += [("s_te", p.cnt("s_te") - 1)]
                        p.on("tensor", lambda e, pbt=pbt, srcb=srcb, sc=sc: e.matmul(
                            pbt[:, 0:128], srcb[:, ts(sc, 128)], ident[:],
                            start=True, stop=True),
                            wait_exact=wex, inc=[("s_tr", 1)])
                        wex = [("s_tr", p.cnt("s_tr"))]
                        if sc == 0 and tt >= 2:
                            wex += [("s_mm", smmark[tt - 2])]
                        p.on("scalar", lambda e, pbt=pbt, ptb=ptb, sc=sc: e.activation(
                            out=ptb[:, ts(sc, 128)], in_=pbt[:, 0:128], func=AF.Identity),
                            wait_exact=wex, inc=[("s_te", 1)])
                    strmark[tt] = p.cnt("s_tr")
                    temark[tt] = p.cnt("s_te")
                if tt >= 1:
                    tp = tt - 1
                    ptb = ptT_sb[tp % 2]
                    for dc in range(2):
                        k2 = tp * 2 + dc
                        pbs = PB[k2 % 4]
                        for sc in range(6):
                            wex = []
                            if sc == 0:
                                wex = [("s_te", temark[tp])]
                                if k2 < 4:
                                    wex += [("my_ev", ssnap)]
                                else:
                                    wex += [("my_ev", scmark[k2 - 4])]
                            p.on("tensor", lambda e, pbs=pbs, ptb=ptb, sc=sc, dc=dc:
                                 e.matmul(pbs[:, 0:512], ptb[:, ts(sc, 128)],
                                          yb[:, sc * D + dc * 512: sc * D + dc * 512 + 512],
                                          start=(sc == 0), stop=(sc == 5)),
                                 wait_exact=wex,
                                 inc=([("s_mm", 1)] if sc == 5 else None))
                        p.on("vector", lambda e, pbs=pbs, tp=tp, dc=dc, e_i=e_i:
                             e.scalar_tensor_tensor(
                                 x_res[:, tp * D + dc * 512: tp * D + dc * 512 + 512],
                                 pbs[:, 0:512],
                                 comb[:, tp * 8 + e_i: tp * 8 + e_i + 1],
                                 x_res[:, tp * D + dc * 512: tp * D + dc * 512 + 512],
                                 ALU.mult, ALU.add),
                             wait_exact=[("s_mm", p.cnt("s_mm"))], inc=[("sc_ev", 1)])
                        scmark[k2] = p.cnt("my_ev")
                    smmark[tp] = p.cnt("s_mm")

        g0 = 0
        for gl, ghc in enumerate([3, 3, 3, 3]):
            q = state["gi"] % 2
            in1 = dram["sw1"].ap().rearrange(
                "(kc p) h -> p kc h", p=128)[:, :, g0 * 128:(g0 + ghc) * 128]
            in3 = dram["sw3"].ap().rearrange(
                "(kc p) h -> p kc h", p=128)[:, :, g0 * 128:(g0 + ghc) * 128]
            in2 = dram["sw2"].ap().rearrange(
                "(hc p) o -> p hc o", p=128)[:, g0:g0 + ghc, :]
            wex = [("mh1_mm", fh[0][0]), ("mh3_mm", fh[0][1])]
            p.on("gpsimd", lambda e, q=q, ghc=ghc, in1=in1: e.dma_start(
                out=w1g[q].ap().rearrange("p (kc h) -> p kc h", kc=ND)[:, :, 0:ghc * 128],
                in_=in1), wait_exact=wex, inc=[("d_wg", 16)])
            p.on("gpsimd", lambda e, q=q, ghc=ghc, in3=in3: e.dma_start(
                out=w3g[q].ap().rearrange("p (kc h) -> p kc h", kc=ND)[:, :, 0:ghc * 128],
                in_=in3), inc=[("d_wg", 16)])
            p.on("gpsimd", lambda e, ghc=ghc, in2=in2: e.dma_start(
                out=w2g.ap().rearrange("p (hc o) -> p hc o", o=D)[:, 0:ghc, :],
                in_=in2), wait_exact=[("my_mm", state["wy"])], inc=[("d_wg", 16)])
            dma_tgt = p.cnt("d_wg")
            for tg in range(4):
                for hcl in range(ghc):
                    hsq = state["hseq"]
                    state["hseq"] += 1
                    pb1 = PB[4 + 2 * (hsq % 2)]
                    pb3 = PB[5 + 2 * (hsq % 2)]
                    for kc in range(ND):
                        wex = [("d_wg", dma_tgt)] if (tg == 0 and hcl == 0 and kc == 0) else []
                        if kc == 0:
                            wex += [("mh_sil", max(0, p.cnt("mh_sil") - 1))]
                            if hsq % 2 == 1:
                                wex += [("dq_ev", p.cnt("dq_ev")),
                                        ("g_gf", p.cnt("g_gf"))]
                        p.on("tensor", lambda e, pb1=pb1, q=q, kc=kc, hcl=hcl, tg=tg:
                             e.matmul(pb1[:, 0:512],
                                      w1g[q][:, kc * 384 + hcl * 128: kc * 384 + (hcl + 1) * 128],
                                      xn_fm[:, kc * S + tg * 512: kc * S + (tg + 1) * 512],
                                      start=(kc == 0), stop=(kc == ND - 1)),
                             wait_exact=wex,
                             inc=([("mh1_mm", 1)] if kc == ND - 1 else None))
                    for kc in range(ND):
                        wex = []
                        if kc == 0:
                            wex = [("mh_ev", max(0, p.cnt("mh_ev") - 1))]
                            if hsq % 2 == 1:
                                wex += [("dq_ev", p.cnt("dq_ev")),
                                        ("g_gf", p.cnt("g_gf"))]
                        p.on("tensor", lambda e, pb3=pb3, q=q, kc=kc, hcl=hcl, tg=tg:
                             e.matmul(pb3[:, 0:512],
                                      w3g[q][:, kc * 384 + hcl * 128: kc * 384 + (hcl + 1) * 128],
                                      xn_fm[:, kc * S + tg * 512: kc * S + (tg + 1) * 512],
                                      start=(kc == 0), stop=(kc == ND - 1)),
                             wait_exact=wex,
                             inc=([("mh3_mm", 1)] if kc == ND - 1 else None))
                    hs = hsil[hsq % 2]
                    p.on("scalar", lambda e, pb1=pb1, hs=hs: e.activation(
                        out=hs[:], in_=pb1[:, 0:512], func=AF.Silu),
                        wait_exact=[("mh1_mm", p.cnt("mh1_mm")),
                                    ("mh_ev", max(0, p.cnt("mh_ev") - 1))],
                        inc=[("mh_sil", 1)])
                    wex = [("mh3_mm", p.cnt("mh3_mm")),
                           ("mh_sil", p.cnt("mh_sil"))]
                    if tg >= 1 and hcl == 0:
                        wex.append(("my_mm", p.cnt("my_mm")))
                    if gl == 0 and tg == 0 and hcl == 0:
                        wex.append(("my_mm", p.cnt("my_mm")))
                    p.on("vector", lambda e, pb3=pb3, hs=hs, hcl=hcl: e.tensor_mul(
                        hb[:, hcl * 512:(hcl + 1) * 512], hs[:], pb3[:, 0:512]),
                        wait_exact=wex, inc=[("mh_ev", 1)])
                for oc in range(2):
                    for hcl in range(ghc):
                        for m4 in range(4):
                            pby = PB[m4]
                            wex = []
                            if hcl == 0 and m4 == 0:
                                wex = [("mh_ev", p.cnt("mh_ev")),
                                       ("my_ev", p.cnt("my_ev"))]
                            p.on("tensor", lambda e, pby=pby, hcl=hcl, m4=m4, oc=oc, ghc=ghc:
                                 e.matmul(pby[:, 0:512],
                                          hb[:, hcl * 512 + m4 * 128: hcl * 512 + (m4 + 1) * 128],
                                          w2g[:, hcl * D + oc * 512: hcl * D + (oc + 1) * 512],
                                          start=(hcl == 0), stop=(hcl == ghc - 1)),
                                 wait_exact=wex,
                                 inc=([("my_mm", 1)] if hcl == ghc - 1 else None))
                    for m4 in range(4):
                        tt4 = tg * 4 + m4
                        lo = tt4 * D + oc * 512
                        p.on("vector", lambda e, m4=m4, lo=lo: e.scalar_tensor_tensor(
                            x_res[:, lo:lo + 512], PB[m4][:, 0:512], 1.0,
                            x_res[:, lo:lo + 512], ALU.mult, ALU.add),
                            wait_exact=[("my_mm", p.cnt("my_mm"))],
                            inc=[("my_ev", 1)])
            fh = [fh[1], (p.cnt("mh1_mm"), p.cnt("mh3_mm"))]
            state["wy"] = p.cnt("my_mm")
            g0 += ghc
            state["gi"] += 1
            drain()

    def emit_dequant(st):
        wq_stage = st["wq_stage"]
        dq_s = st["dq_s"]
        dq_si = st["dq_si"]
        dq_t = st["dq_t"]
        dq_i = st["dq_i"]
        dq_w = st["dq_w"]
        tr_stage = st["tr_stage"]

        for wname, wsrc, O, I in (("up", "w_up", L1_HID, D),
                                  ("gate", "w_gate", L1_HID, D),
                                  ("down", "w_down", D, L1_HID)):
            noc = O // 128
            nh = max(1, I // 512)
            IH = I // nh
            nb = IH // 32
            nic = IH // 128
            for oc in range(noc):
                for ih in range(nh):
                    p.on("sync", lambda e, wsrc=wsrc, oc=oc, ih=ih, IH=IH: e.dma_start(
                        out=wq_stage[:, 0:IH],
                        in_=dram[wsrc].ap().rearrange("(c p) i -> p c i", p=128)
                            [:, oc, ih * IH:(ih + 1) * IH]),
                        wait_exact=[("dq_ev", p.cnt("dq_ev"))],
                        inc=[("d_dq", 16)])
                    w3d = wq_stage.ap().rearrange("p (b k) -> p b k", k=32)[:, 0:nb, :]
                    p.on("vector", lambda e, w3d=w3d, nb=nb: e.reduce_max(
                        dq_s.ap().rearrange("p (b k) -> p b k", k=1)[:, 0:nb, :], w3d,
                        axis=AXX, apply_absolute_value=True),
                        wait_exact=[("d_dq", p.cnt("d_dq"))])
                    p.vf()
                    p.on("vector", lambda e, nb=nb: e.tensor_scalar(
                        dq_s[:, 0:nb], dq_s[:, 0:nb], 1.0 / 7.0, 1e-12, ALU.mult, ALU.max))
                    p.vf()
                    p.on("vector", lambda e, nb=nb: e.reciprocal(dq_si[:, 0:nb], dq_s[:, 0:nb]),
                         inc=[("rcp", 1)])
                    p.wait("vector", ["rcp"])
                    p.on("vector", lambda e, w3d=w3d, nb=nb: e.tensor_mul(
                        dq_t.ap().rearrange("p (b k) -> p b k", k=32)[:, 0:nb, :], w3d,
                        dq_si.ap().rearrange("p (b k) -> p b k", k=1)[:, 0:nb, :]
                        .broadcast_to([128, nb, 32])))
                    p.vf()
                    p.on("vector", lambda e, IH=IH: e.tensor_scalar_add(
                        dq_i[:, 0:IH], dq_t[:, 0:IH], 8.5))
                    p.vf()
                    p.on("vector", lambda e, nb=nb: e.scalar_tensor_tensor(
                        dq_w.ap().rearrange("p (b k) -> p b k", k=32)[:, 0:nb, :],
                        dq_i.ap().rearrange("p (b k) -> p b k", k=32)[:, 0:nb, :],
                        -8.0,
                        dq_s.ap().rearrange("p (b k) -> p b k", k=1)[:, 0:nb, :]
                        .broadcast_to([128, nb, 32]),
                        ALU.add, ALU.mult), inc=[("dq_w", 1)])
                    for ic in range(nic):
                        k = p.cnt("dq_tr")
                        pb = PB[6 + k % 2]
                        we = [("dq_ev", max(0, p.cnt("dq_ev") - 1))]
                        if ic == 0:
                            we += [("mh_ev", p.cnt("mh_ev")),
                                   ("mh_sil", p.cnt("mh_sil")),
                                   ("g_gf", p.cnt("g_gf"))]
                        p.on("tensor", lambda e, pb=pb, ic=ic: e.matmul(
                            pb[:, 0:128], dq_w[:, ts(ic, 128)], ident[:],
                            start=True, stop=True),
                            wait_exact=[("dq_w", p.cnt("dq_w"))] + we,
                            inc=[("dq_tr", 1)])
                        p.on("scalar", lambda e, pb=pb, ic=ic: e.activation(
                            out=tr_stage[:, ts(ic, 128)], in_=pb[:, 0:128],
                            func=AF.Identity),
                            wait_exact=[("dq_tr", p.cnt("dq_tr"))] +
                                       ([("d_dqo", p.cnt("d_dqo"))] if ic == 0 else []),
                            inc=[("dq_ev", 1)])
                    p.on("sync", lambda e, wname=wname, oc=oc, ih=ih, nic=nic: e.dma_start(
                        out=scr[wname].ap()[ih * nic:(ih + 1) * nic, :,
                                            oc * 128:(oc + 1) * 128]
                            .rearrange("c p o -> p c o"),
                        in_=tr_stage.ap().rearrange("p (c o) -> p c o", o=128)[:, 0:nic, :]),
                        wait_exact=[("dq_ev", p.cnt("dq_ev"))], inc=[("d_dqo", 16)])
                    yield
        p.on("sync", None, wait_exact=[("d_dqo", p.cnt("d_dqo"))])

    def emit_attn_l1(st):
        wl1 = st["wl1"]
        fmA = st["fmA"]
        fmB = st["fmB"]
        fres = st["fres"]
        fims = st["fims"]
        sre = st["sre"]
        sim = st["sim"]
        tA = st["tA"]; tB = st["tB"]; u1 = st["u1"]; u2 = st["u2"]
        u3 = st["u3"]; u4 = st["u4"]
        ab_a = st["ab_a"]
        ab_b = st["ab_b"]
        bl1_sb = st["bl1_sb"]

        FW = [128, 128, 128, 128, 1]

        for wi, b in enumerate(("bq", "bk", "bv")):
            p.on("sync", lambda e, wi=wi, b=b: e.dma_start(
                out=bl1_sb[:, wi * ND:(wi + 1) * ND],
                in_=dram["L1_" + b].ap().rearrange("(c p) -> p c", p=128),
                allow_slow_non_contiguous=True), inc=[("d_l1b", 16)])
        p.on("sync", lambda e: e.dma_start(
            out=fres.ap().rearrange("p (c f) -> p c f", f=NF),
            in_=dram["fre"].ap().rearrange("(c p) f -> p c f", p=128)),
            inc=[("d_l1f", 16)])
        p.on("sync", lambda e: e.dma_start(
            out=fims.ap().rearrange("p (c f) -> p c f", f=NF),
            in_=dram["fim"].ap().rearrange("(c p) f -> p c f", p=128)),
            inc=[("d_l1f", 16)])
        f_mark = p.cnt("d_l1f")
        p.on("vector", lambda e: e.memset(sre[:], 0.0), inc=[("l1s_z", 1)])
        p.on("vector", lambda e: e.memset(sim[:], 0.0), inc=[("l1s_z", 1)])

        def project(widx, wname, dst, bias_col, semtag):
            w_marks = []
            for half in range(2):
                p.on("gpsimd", lambda e, wname=wname, half=half: e.dma_start(
                    out=wl1.ap().rearrange("p (c o) -> p c o", o=512),
                    in_=dram[wname].ap().rearrange("(c p) o -> p c o", p=128)
                        [:, :, half * 512:(half + 1) * 512]),
                    wait_exact=([("l1w_free", p.cnt("l1w_free"))] if half == 0
                                else [(semtag + "_mm", p.cnt(semtag + "_mm") + 16)]),
                    inc=[("d_l1wg", 16)])
                w_marks.append(p.cnt("d_l1wg"))
            for oc in range(ND):
                for tc in range(4):
                    pbp = PB[6 + (oc * 4 + tc) % 2]
                    wex = [("d_l1wg", w_marks[oc // 4])] if tc == 0 else []
                    k = p.cnt(semtag + "_mm")
                    if k >= 2:
                        wex += [(semtag + "_ev", k - 1)]
                    for kc in range(ND):
                        p.on("tensor", lambda e, pbp=pbp, kc=kc, oc=oc, tc=tc: e.matmul(
                            pbp[:], wl1[:, kc * 512 + (oc % 4) * 128: kc * 512 + (oc % 4 + 1) * 128],
                            xn_fm[:, kc * S + tc * 512: kc * S + (tc + 1) * 512],
                            start=(kc == 0), stop=(kc == ND - 1)),
                            wait_exact=(wex if kc == 0 else []),
                            inc=([(semtag + "_mm", 1)] if kc == ND - 1 else None))
                    p.on("scalar", lambda e, pbp=pbp, dst=dst, oc=oc, tc=tc, bias_col=bias_col: e.activation(
                        out=dst[:, oc * S + tc * 512: oc * S + (tc + 1) * 512],
                        in_=pbp[:], func=AF.Identity,
                        bias=bl1_sb[:, bias_col + oc: bias_col + oc + 1]),
                        wait=(["d_l1b"] if oc == 0 and tc == 0 else None),
                        wait_exact=[(semtag + "_mm", p.cnt(semtag + "_mm"))],
                        inc=[(semtag + "_ev", 1)])
            p.on("sync", lambda e: e.nop(),
                 wait_exact=[(semtag + "_ev", p.cnt(semtag + "_ev"))],
                 inc=[("l1w_free", 1)])

        project(0, "L1_Wk", fmA, 1 * ND, "l1k")
        project(1, "L1_Wv", fmB, 2 * ND, "l1v")

        f0 = 0
        fcnt = 0
        for fc in range(5):
            fw = FW[fc]
            for tc in range(4):
                par = 4 * (fcnt % 2)
                fcnt += 1
                wex0 = [("l1k_ev", 32), ("l1v_ev", 32), ("l1s_z", p.cnt("l1s_z")),
                        ("d_l1f", f_mark)] if (fc == 0 and tc == 0) else \
                       [("l1h_v", max(0, p.cnt("l1h_v") - 1))]
                for ci, (pb, fsb, xin) in enumerate((
                        (PB[par + 0], fres, fmA), (PB[par + 1], fims, fmA),
                        (PB[par + 2], fres, fmB), (PB[par + 3], fims, fmB))):
                    for kc in range(ND):
                        p.on("tensor", lambda e, pb=pb, fsb=fsb, xin=xin, kc=kc, fc=fc, fw=fw, f0=f0, tc=tc: e.matmul(
                            pb[0:fw, :], fsb[:, kc * NF + f0: kc * NF + f0 + fw],
                            xin[:, kc * S + tc * 512: kc * S + (tc + 1) * 512],
                            start=(kc == 0), stop=(kc == ND - 1)),
                            wait_exact=(wex0 if (ci == 0 and kc == 0) else []),
                            inc=([("l1h_mm", 1)] if (ci == 3 and kc == ND - 1) else None))
                mmt = p.cnt("l1h_mm")
                fwsl = slice(0, fw)
                p.on("scalar", lambda e, fwsl=fwsl, par=par: e.activation(
                    out=tA[fwsl, :], in_=PB[par + 0][fwsl, :], func=AF.Identity),
                    wait_exact=[("l1h_mm", mmt), ("l1h_v", p.cnt("l1h_v"))],
                    inc=[("l1h_s", 1)])
                p.on("scalar", lambda e, fwsl=fwsl, par=par: e.activation(
                    out=tB[fwsl, :], in_=PB[par + 1][fwsl, :], func=AF.Identity),
                    inc=[("l1h_s", 1)])
                p.on("vector", lambda e, fwsl=fwsl, par=par: e.tensor_mul(
                    u1[fwsl, :], tA[fwsl, :], PB[par + 2][fwsl, :]),
                    wait_exact=[("l1h_s", p.cnt("l1h_s"))])
                p.on("vector", lambda e, fwsl=fwsl, par=par: e.tensor_mul(
                    u2[fwsl, :], tB[fwsl, :], PB[par + 3][fwsl, :]))
                p.on("vector", lambda e, fwsl=fwsl, par=par: e.tensor_mul(
                    u3[fwsl, :], tA[fwsl, :], PB[par + 3][fwsl, :]))
                p.on("vector", lambda e, fwsl=fwsl, par=par: e.tensor_mul(
                    u4[fwsl, :], tB[fwsl, :], PB[par + 2][fwsl, :]))
                p.on("vector", lambda e, fwsl=fwsl: e.tensor_sub(
                    u1[fwsl, :], u1[fwsl, :], u2[fwsl, :]))
                p.on("vector", lambda e, fwsl=fwsl: e.tensor_add(
                    u3[fwsl, :], u3[fwsl, :], u4[fwsl, :]))
                p.on("vector", lambda e, fwsl=fwsl: e.reduce_sum(
                    u2[fwsl, 0:1], u1[fwsl, :], axis=AXX))
                p.on("vector", lambda e, fwsl=fwsl: e.reduce_sum(
                    u4[fwsl, 0:1], u3[fwsl, :], axis=AXX))
                p.on("vector", lambda e, fwsl=fwsl, fc=fc: e.tensor_add(
                    sre[fwsl, fc:fc + 1], sre[fwsl, fc:fc + 1], u2[fwsl, 0:1]))
                p.on("vector", lambda e, fwsl=fwsl, fc=fc: e.tensor_add(
                    sim[fwsl, fc:fc + 1], sim[fwsl, fc:fc + 1], u4[fwsl, 0:1]),
                    inc=[("l1h_v", 1)])
            f0 += fw

        for gi, gnm in ((0, "gre"), (1, "gim")):
            p.on("sync", lambda e, gi=gi, gnm=gnm: e.dma_start(
                out=fmB.ap().rearrange("p (c o) -> p c o", o=D)[:, gi * 5:gi * 5 + 4, :],
                in_=dram[gnm].ap()[0:512, :].rearrange("(c p) o -> p c o", p=128)),
                wait_exact=([("l1h_v", 20)] if gi == 0 else []),
                inc=[("d_l1g", 16)])
            p.on("sync", lambda e, gi=gi, gnm=gnm: e.dma_start(
                out=fmB[0:1, (gi * 5 + 4) * D:(gi * 5 + 5) * D],
                in_=dram[gnm].ap()[512:513, :]),
                inc=[("d_l1g", 16)])
        g_mark = p.cnt("d_l1g")
        p.on("sync", lambda e: e.nop(), wait_exact=[("l1h_v", 20)],
             inc=[("l1w_free", 1)])
        project(2, "L1_Wq", fmA, 0 * ND, "l1q")

        qstep = 0
        for tc in range(4):
            f0 = 0
            for fc in range(5):
                fw = FW[fc]
                fwsl = slice(0, fw)
                wex0 = [("l1q_ev", 32)] if (tc == 0 and fc == 0) else \
                       [("l1a_v", max(0, p.cnt("l1a_v") - 1))]
                qpar = 2 * (qstep % 2)
                qstep += 1
                for ci, (pb, fsb) in enumerate(((PB[qpar + 0], fres),
                                                (PB[qpar + 1], fims))):
                    for kc in range(ND):
                        p.on("tensor", lambda e, pb=pb, fsb=fsb, kc=kc, fw=fw, f0=f0, tc=tc: e.matmul(
                            pb[0:fw, :], fsb[:, kc * NF + f0: kc * NF + f0 + fw],
                            fmA[:, kc * S + tc * 512: kc * S + (tc + 1) * 512],
                            start=(kc == 0), stop=(kc == ND - 1)),
                            wait_exact=(wex0 if (ci == 0 and kc == 0) else []),
                            inc=([("l1a_mm", 1)] if (ci == 1 and kc == ND - 1) else None))
                mmt = p.cnt("l1a_mm")
                p.on("vector", lambda e, fwsl=fwsl, fc=fc, qpar=qpar: e.tensor_scalar_mul(
                    u1[fwsl, :], PB[qpar + 1][fwsl, :], sim[fwsl, fc:fc + 1]),
                    wait_exact=[("l1a_mm", mmt)])
                p.on("vector", lambda e, fwsl=fwsl, fc=fc, qpar=qpar: e.tensor_scalar_mul(
                    u2[fwsl, :], PB[qpar + 1][fwsl, :], sre[fwsl, fc:fc + 1]))
                p.on("vector", lambda e, fwsl=fwsl, fc=fc, qpar=qpar: e.scalar_tensor_tensor(
                    ab_a[fwsl, fc * 512:(fc + 1) * 512], PB[qpar + 0][fwsl, :],
                    sre[fwsl, fc:fc + 1], u1[fwsl, :], ALU.mult, ALU.add))
                p.on("vector", lambda e, fwsl=fwsl, fc=fc, qpar=qpar: e.scalar_tensor_tensor(
                    ab_b[fwsl, fc * 512:(fc + 1) * 512], PB[qpar + 0][fwsl, :],
                    sim[fwsl, fc:fc + 1], u2[fwsl, :], ALU.mult, ALU.subtract),
                    inc=[("l1a_v", 1)])
                f0 += fw
            for m4 in range(4):
                tt = tc * 4 + m4
                for oc in range(2):
                    j = p.cnt("l1o_mm")
                    pbo = PB[4 + (tt * 2 + oc) % 2]
                    first = True
                    f0 = 0
                    for fc in range(5):
                        fw = FW[fc]
                        for gi in range(2):
                            ab = ab_a if gi == 0 else ab_b
                            goff = gi * 5 * D
                            wex = []
                            if first:
                                wex = [("l1a_v", p.cnt("l1a_v")),
                                       ("l1o_ev", max(0, p.cnt("l1o_ev") - 1))]
                                if tc == 0 and tt == 0 and oc == 0:
                                    wex += [("d_l1g", g_mark)]
                            p.on("tensor", lambda e, pbo=pbo, ab=ab, fc=fc, fw=fw, m4=m4, goff=goff, oc=oc, first=first, gi=gi: e.matmul(
                                pbo[:], ab[0:fw, fc * 512 + m4 * 128: fc * 512 + (m4 + 1) * 128],
                                fmB[:, goff + fc * D + oc * 512: goff + fc * D + (oc + 1) * 512]
                                [0:fw, :] if False else
                                fmB[0:fw, goff + fc * D + oc * 512: goff + fc * D + (oc + 1) * 512],
                                start=first, stop=(fc == 4 and gi == 1)),
                                wait_exact=wex,
                                inc=([("l1o_mm", 1)] if (fc == 4 and gi == 1) else None))
                            first = False
                    p.on("vector", lambda e, pbo=pbo, tt=tt, oc=oc: e.scalar_tensor_tensor(
                        x_res[:, tt * D + oc * 512: tt * D + (oc + 1) * 512],
                        pbo[:], 1.0,
                        x_res[:, tt * D + oc * 512: tt * D + (oc + 1) * 512],
                        ALU.mult, ALU.add),
                        wait_exact=[("l1o_mm", p.cnt("l1o_mm"))],
                        inc=[("l1o_ev", 1)])

    def emit_ffn_l1(st):
        gate_sb = st["gate_sb"]
        whg = st["whg"]
        wdn = st["wdn"]
        i32t = st["i32t"]
        sgt = st["sgt"]
        i32u = st["i32u"]
        i32o = i32u
        amx = st["amx"]
        parb = st["parb"]
        paro = st["paro"]
        scal = st["scal"]
        scbc = st["scbc"]
        NH = L1_HID // 128

        def allreduce(cols, tag):
            p.on("tensor", lambda e: e.matmul(
                PB[7][0:2, 0:128], parb[:, 0:2], identF[:], start=True, stop=True),
                wait_exact=[(tag + "_pin", p.cnt(tag + "_pin"))],
                inc=[(tag + "_par", 1)])
            p.on("vector", lambda e: e.reduce_max(
                paro[0:2, 0:1], PB[7][0:2, 0:128], axis=AXX),
                wait_exact=[(tag + "_par", p.cnt(tag + "_par"))],
                inc=[(tag + "_par", 1)])
            p.on("gpsimd", lambda e: e.dma_start(
                out=cc_in.ap().rearrange("a b -> b a"), in_=paro[0:2, 0:1]),
                wait_exact=[(tag + "_par", p.cnt(tag + "_par"))],
                inc=[("d_cc", 16)])
            p.on("gpsimd", lambda e: e.collective_compute(
                "AllReduce", ALU.max, replica_groups=[list(range(8))],
                ins=[cc_in.ap()], outs=[cc_out.ap()]),
                wait_exact=[("d_cc", p.cnt("d_cc"))],
                inc=[("cc_done", 1)])
            p.on("gpsimd", lambda e: e.dma_start(
                out=scal[0:1, 0:2], in_=cc_out.ap()),
                wait_exact=[("cc_done", p.cnt("cc_done"))],
                inc=[("d_cc2", 16)])
            p.on("vector", None, wait_exact=[("d_cc2", p.cnt("d_cc2"))])

        p.on("vector", lambda e: e.memset(scbc[:, 3:4], 32768.5), inc=[("cbias", 1)])
        for c in range(ND):
            p.on("vector", lambda e, c=c: e.reduce_max(
                amx[:, c:c + 1], xn_fm[:, ts(c, S)], axis=AXX,
                apply_absolute_value=True))
        p.on("vector", lambda e: e.reduce_max(
            parb.ap().rearrange("p (a b) -> p a b", b=1)[:, 0:1, :],
            amx.ap().rearrange("p (a b) -> p a b", b=8)[:, 0:1, :], axis=AXX))
        p.on("vector", lambda e: e.tensor_copy(parb[:, 1:2], parb[:, 0:1]),
             inc=[("x_pin", 1)])
        allreduce(1, "x")
        p.on("vector", lambda e: e.tensor_scalar(
            scal[0:1, 2:3], scal[0:1, 0:1], 1.0 / 32767.0, 1e-12, ALU.mult, ALU.max))
        p.on("vector", lambda e: e.reciprocal(scal[0:1, 3:4], scal[0:1, 2:3]),
             inc=[("x_sc", 1)])
        p.wait("vector", ["x_sc"])
        p.on("tensor", lambda e: e.matmul(
            PB[7][:, 0:2], onesf[:], scal[0:1, 2:4], start=True, stop=True),
            wait_exact=[("x_sc", p.cnt("x_sc"))], inc=[("x_sc", 1)])
        p.on("vector", lambda e: e.tensor_copy(scbc[:, 0:2], PB[7][:, 0:2]),
             wait_exact=[("x_sc", p.cnt("x_sc"))], inc=[("x_bc", 1)])

        nxq = 0
        for tcq in range(4):
            for kcq in range(ND):
                j = kcq * 4 + tcq
                buf = i32t if nxq % 2 == 0 else i32u
                p.on("scalar", lambda e, j=j, buf=buf: e.activation(
                    out=buf[:], in_=xn_fm[:, j * 512:(j + 1) * 512], func=AF.Identity,
                    scale=scbc[:, 1:2], bias=scbc[:, 3:4]),
                    wait_exact=[("x_bc", p.cnt("x_bc")), ("l1q_ev", 32)] +
                               ([("xq_v", nxq - 1)] if nxq >= 2 else []),
                    inc=[("xq_s", 1)])
                p.on("vector", lambda e, j=j, buf=buf: e.scalar_tensor_tensor(
                    xn_fm[:, j * 512:(j + 1) * 512], buf[:], -32768.0,
                    scbc[:, 0:1].broadcast_to([128, 512]), ALU.add, ALU.mult),
                    wait_exact=[("xq_s", p.cnt("xq_s"))],
                    inc=[("xq_v", 1)])
                nxq += 1

        p.on("vector", lambda e: e.memset(amx[:, 0:2], 0.0), inc=[("ga_z", 1)])
        wg1_marks = []
        for hc in range(NH):
            p.on("sync", lambda e, hc=hc: e.dma_start(
                out=whg[hc % 2].ap().rearrange("p (c o) -> p c o", o=128),
                in_=scr["gate"].ap()[:, :, hc * 128:(hc + 1) * 128]),
                wait_exact=([("gt_mm", (hc - 1) * 4)] if hc >= 2 else []),
                inc=[("d_wg1", 16)])
            wg1_marks.append(p.cnt("d_wg1"))
        for hc in range(NH):
            for tc in range(4):
                pbg = PB[0 if tc % 2 == 0 else 1]
                wex = [("d_wg1", wg1_marks[hc])] if tc == 0 else []
                k = p.cnt("gt_ev")
                if k >= 2:
                    wex += [("gt_ev", k - 1)]
                for kc in range(ND):
                    p.on("tensor", lambda e, pbg=pbg, hc=hc, kc=kc, tc=tc: e.matmul(
                        pbg[:], whg[hc % 2][:, kc * 128:(kc + 1) * 128],
                        xn_fm[:, kc * S + tc * 512: kc * S + (tc + 1) * 512],
                        start=(kc == 0), stop=(kc == ND - 1)),
                        wait_exact=(wex if kc == 0 else []) +
                                   ([("xq_v", 8 * (tc + 1))] if (hc == 0 and kc == 0) else []),
                        inc=([("gt_mm", 1)] if kc == ND - 1 else None))
                p.on("scalar", lambda e, pbg=pbg, hc=hc, tc=tc: e.activation(
                    out=gate_sb[:, hc * S + tc * 512: hc * S + (tc + 1) * 512],
                    in_=pbg[:], func=AF.Identity),
                    wait_exact=[("gt_mm", p.cnt("gt_mm"))], inc=[("gt_ev", 1)])
            p.on("vector", lambda e, hc=hc: e.reduce_max(
                amx[:, 2:3], gate_sb[:, ts(hc, S)], axis=AXX,
                apply_absolute_value=True),
                wait=(["ga_z"] if hc == 0 else None),
                wait_exact=[("gt_ev", p.cnt("gt_ev"))])
            p.vf()
            p.on("vector", lambda e: e.tensor_max(
                amx[:, 0:1], amx[:, 0:1], amx[:, 2:3]), inc=[("ga_m", 1)])

        p.vf()
        p.on("vector", lambda e: e.tensor_copy(parb[:, 0:2],
             amx.ap().rearrange("p (a b) -> p a b", b=1)[:, 0:2, :]
             .broadcast_to([128, 2, 1]).rearrange("p a b -> p (a b)") if False
             else amx[:, 0:2]),
             wait_exact=[("ga_m", NH)], inc=[("gu_pin", 1)])
        allreduce(1, "gu")
        p.on("vector", lambda e: e.tensor_scalar(
            scal[0:1, 4:5], scal[0:1, 0:1], 1.0 / 32767.0, 1e-12, ALU.mult, ALU.max))
        p.on("vector", lambda e: e.reciprocal(scal[0:1, 5:6], scal[0:1, 4:5]),
             inc=[("gu_sc", 1)])
        p.wait("vector", ["gu_sc"])
        p.on("tensor", lambda e: e.matmul(
            PB[7][:, 0:2], onesf[:], scal[0:1, 4:6], start=True, stop=True),
            wait_exact=[("gu_sc", p.cnt("gu_sc"))], inc=[("gu_sc", 1)])
        p.on("vector", lambda e: e.tensor_copy(scbc[:, 4:6], PB[7][:, 0:2]),
             wait_exact=[("gu_sc", p.cnt("gu_sc"))], inc=[("gu_bc", 1)])

        f32t2 = [PB[2], PB[3]]
        p.on("vector", lambda e: e.memset(amx[:, 2:3], 0.0), inc=[("g2_z", 1)])
        NJ = NH * 4
        wu1_marks = []
        for hc in range(NH):
            p.on("sync", lambda e, hc=hc: e.dma_start(
                out=whg[hc % 2].ap().rearrange("p (c o) -> p c o", o=128),
                in_=scr["up"].ap()[:, :, hc * 128:(hc + 1) * 128]),
                wait_exact=([("up1_mm", (hc - 1) * 4)] if hc >= 2 else
                            [("gt_mm", NH * 4)]),
                inc=[("d_wu1", 16)])
            wu1_marks.append(p.cnt("d_wu1"))
        for s_ in range(NJ + 5):
            if s_ < NJ:
                hc, tc = s_ // 4, s_ % 4
                pbg = PB[0 if tc % 2 == 0 else 1]
                wex = [("d_wu1", wu1_marks[hc])] if tc == 0 else []
                if p.cnt("up1_v") >= 2:
                    wex += [("up1_v", p.cnt("up1_v") - 1)]
                for kc in range(ND):
                    p.on("tensor", lambda e, pbg=pbg, hc=hc, kc=kc, tc=tc: e.matmul(
                        pbg[:], whg[hc % 2][:, kc * 128:(kc + 1) * 128],
                        xn_fm[:, kc * S + tc * 512: kc * S + (tc + 1) * 512],
                        start=(kc == 0), stop=(kc == ND - 1)),
                        wait_exact=(wex if kc == 0 else []),
                        inc=([("up1_mm", 1)] if kc == ND - 1 else None))
                p.on("vector", lambda e, pbg=pbg, s_=s_: e.reduce_max(
                    amx[:, 6 + s_ % 2:7 + s_ % 2], pbg[:], axis=AXX,
                    apply_absolute_value=True),
                    wait_exact=[("up1_mm", p.cnt("up1_mm"))])
                if s_ >= 1:
                    p.on("vector", lambda e, s_=s_: e.tensor_max(
                        amx[:, 1:2], amx[:, 1:2],
                        amx[:, 6 + (s_ - 1) % 2:7 + (s_ - 1) % 2]),
                        inc=[("up1_v", 1)])
            if s_ < NJ:
                sl = slice(s_ * 512, (s_ + 1) * 512)
                p.on("scalar", lambda e, sl=sl: e.activation(
                    out=i32t[:], in_=gate_sb[:, sl], func=AF.Identity,
                    scale=scbc[:, 5:6], bias=scbc[:, 3:4]),
                    wait_exact=[("gu_bc", p.cnt("gu_bc")),
                                ("g2_dq", p.cnt("g2_dq"))],
                    inc=[("g2_s", 1)])
                p.on("vector", lambda e, sl=sl: e.scalar_tensor_tensor(
                    gate_sb[:, sl], i32t[:], -32768.0,
                    scbc[:, 4:5].broadcast_to([128, 512]), ALU.add, ALU.mult),
                    wait_exact=[("g2_s", p.cnt("g2_s"))], inc=[("g2_dq", 1)])
            if 0 <= s_ - 1 < NJ:
                t = s_ - 1
                p.on("vector", lambda e, t=t: e.tensor_scalar(
                    f32t2[t % 2][:, 0:512], sgt[:], 65536.0, 65535.49,
                    ALU.mult, ALU.min),
                    wait_exact=[("g2_s2", t + 1)], inc=[("g2_ts", 1)])
            if s_ < NJ:
                sl = slice(s_ * 512, (s_ + 1) * 512)
                p.on("scalar", lambda e, sl=sl: e.activation(
                    out=sgt[:], in_=gate_sb[:, sl], func=AF.Sigmoid),
                    wait_exact=[("g2_dq", p.cnt("g2_dq")),
                                ("g2_ts", p.cnt("g2_ts"))],
                    inc=[("g2_s2", 1)])
            if 0 <= s_ - 3 < NJ:
                m_ = s_ - 3
                slm = slice(m_ * 512, (m_ + 1) * 512)
                p.on("vector", lambda e, slm=slm, m_=m_: e.scalar_tensor_tensor(
                    gate_sb[:, slm], i32u[:], 1.0 / 65536.0,
                    gate_sb[:, slm], ALU.mult, ALU.mult),
                    wait_exact=[("g2_s2", m_ + 1)], inc=[("g2_p", 1)])
            if 0 <= s_ - 2 < NJ:
                a_ = s_ - 2
                p.on("vector", lambda e, a_=a_: e.tensor_scalar_add(
                    i32u[:], f32t2[a_ % 2][:, 0:512], 0.5))
            if 0 <= s_ - 4 < NJ:
                r_ = s_ - 4
                slr = slice(r_ * 512, (r_ + 1) * 512)
                p.on("vector", lambda e, slr=slr, r_=r_: e.reduce_max(
                    amx[:, 3 + 2 * (r_ % 2):4 + 2 * (r_ % 2)], gate_sb[:, slr],
                    axis=AXX, apply_absolute_value=True),
                    wait=(["g2_z"] if r_ == 0 else None))
            if 0 <= s_ - 5 < NJ:
                x_ = s_ - 5
                p.on("vector", lambda e, x_=x_: e.tensor_max(
                    amx[:, 2:3], amx[:, 2:3],
                    amx[:, 3 + 2 * (x_ % 2):4 + 2 * (x_ % 2)]), inc=[("g2_m", 1)])
        p.vf()
        p.on("vector", lambda e: e.tensor_max(
            amx[:, 1:2], amx[:, 1:2], amx[:, 6 + (NJ - 1) % 2:7 + (NJ - 1) % 2]),
            inc=[("up1_v", 1)])

        p.vf()
        p.on("vector", lambda e: e.tensor_copy(parb[:, 0:1], amx[:, 1:2]),
             wait_exact=[("up1_v", p.cnt("up1_v"))])
        p.on("vector", lambda e: e.tensor_copy(parb[:, 1:2], amx[:, 1:2]),
             inc=[("gu_pin", 1)])
        allreduce(1, "gu")
        p.on("vector", lambda e: e.tensor_scalar(
            scal[0:1, 6:7], scal[0:1, 0:1], 1.0 / 32767.0, 1e-12, ALU.mult, ALU.max))
        p.on("vector", lambda e: e.reciprocal(scal[0:1, 7:8], scal[0:1, 6:7]),
             inc=[("gu_sc", 1)])
        p.wait("vector", ["gu_sc"])
        p.on("tensor", lambda e: e.matmul(
            PB[7][:, 0:2], onesf[:], scal[0:1, 6:8], start=True, stop=True),
            wait_exact=[("gu_sc", p.cnt("gu_sc"))], inc=[("gu_sc", 1)])
        p.on("vector", lambda e: e.tensor_copy(scbc[:, 6:8], PB[7][:, 0:2]),
             wait_exact=[("gu_sc", p.cnt("gu_sc"))], inc=[("gu_bc", 1)])

        p.vf()
        p.on("vector", lambda e: e.tensor_copy(parb[:, 0:1], amx[:, 2:3]),
             wait_exact=[("g2_m", NH * 4)])
        p.on("vector", lambda e: e.tensor_copy(parb[:, 1:2], amx[:, 2:3]),
             inc=[("t2_pin", 1)])
        allreduce(1, "t2")
        p.on("vector", lambda e: e.tensor_scalar(
            scal[0:1, 4:5], scal[0:1, 0:1], 1.0 / 32767.0, 1e-12, ALU.mult, ALU.max))
        p.on("vector", lambda e: e.reciprocal(scal[0:1, 5:6], scal[0:1, 4:5]),
             inc=[("t2_sc", 1)])
        p.wait("vector", ["t2_sc"])
        p.on("tensor", lambda e: e.matmul(
            PB[7][:, 0:2], onesf[:], scal[0:1, 4:6], start=True, stop=True),
            wait_exact=[("t2_sc", p.cnt("t2_sc"))], inc=[("t2_sc", 1)])
        p.on("vector", lambda e: e.tensor_copy(scbc[:, 4:6], PB[7][:, 0:2]),
             wait_exact=[("t2_sc", p.cnt("t2_sc"))], inc=[("t2_bc", 1)])
        def _q2(j):
            sl = slice(j * 512, (j + 1) * 512)
            p.on("scalar", lambda e, sl=sl: e.activation(
                out=i32t[:], in_=gate_sb[:, sl], func=AF.Identity,
                scale=scbc[:, 5:6], bias=scbc[:, 3:4]),
                wait_exact=[("t2_bc", p.cnt("t2_bc")), ("q2_v", p.cnt("q2_v"))],
                inc=[("q2_s", 1)])
            p.on("vector", lambda e, sl=sl: e.scalar_tensor_tensor(
                gate_sb[:, sl], i32t[:], -32768.0,
                scbc[:, 4:5].broadcast_to([128, 512]), ALU.add, ALU.mult),
                wait_exact=[("q2_s", p.cnt("q2_s"))], inc=[("q2_v", 1)])
        _q2(0)

        p.on("vector", lambda e: e.memset(amx[:, 3:4], 0.0), inc=[("o_z", 1)])
        wu2_marks = []
        for hc in range(NH):
            p.on("sync", lambda e, hc=hc: e.dma_start(
                out=whg[hc % 2].ap().rearrange("p (c o) -> p c o", o=128),
                in_=scr["up"].ap()[:, :, hc * 128:(hc + 1) * 128]),
                wait_exact=([("o_mm", (hc - 1) * 4)] if hc >= 2 else
                            [("up1_mm", NH * 4)]),
                inc=[("d_wu2", 16)])
            wu2_marks.append(p.cnt("d_wu2"))
        for hc in range(NH):
            for tc in range(4):
                j = hc * 4 + tc
                if j + 1 < NH * 4:
                    _q2(j + 1)
                pbg = PB[0 if tc % 2 == 0 else 1]
                wex = [("d_wu2", wu2_marks[hc])] if tc == 0 else []
                if p.cnt("o_s") >= 2:
                    wex += [("o_s", p.cnt("o_s") - 1)]
                if hc == 0 and tc == 0:
                    wex += [("o_z", p.cnt("o_z"))]
                for kc in range(ND):
                    p.on("tensor", lambda e, pbg=pbg, hc=hc, kc=kc, tc=tc: e.matmul(
                        pbg[:], whg[hc % 2][:, kc * 128:(kc + 1) * 128],
                        xn_fm[:, kc * S + tc * 512: kc * S + (tc + 1) * 512],
                        start=(kc == 0), stop=(kc == ND - 1)),
                        wait_exact=(wex if kc == 0 else []),
                        inc=([("o_mm", 1)] if kc == ND - 1 else None))
                p.on("scalar", lambda e, pbg=pbg: e.activation(
                    out=i32o[:], in_=pbg[:], func=AF.Identity,
                    scale=scbc[:, 7:8], bias=scbc[:, 3:4]),
                    wait_exact=[("o_mm", p.cnt("o_mm")),
                                ("o_stt", p.cnt("o_stt"))],
                    inc=[("o_s", 1)])
                lo = hc * S + tc * 512
                p.on("vector", lambda e, lo=lo: e.scalar_tensor_tensor(
                    gate_sb[:, lo:lo + 512], i32o[:], -32768.0,
                    gate_sb[:, lo:lo + 512], ALU.add, ALU.mult),
                    wait_exact=[("o_s", p.cnt("o_s")), ("q2_v", j + 1)],
                    inc=[("o_stt", 1)])
                if j >= 1:
                    lp = lo - 512 if tc >= 1 else (hc - 1) * S + 3 * 512
                    p.on("vector", lambda e, lp=lp, j=j: e.reduce_max(
                        amx[:, 4 + (j - 1) % 2:5 + (j - 1) % 2],
                        gate_sb[:, lp:lp + 512], axis=AXX,
                        apply_absolute_value=True))
                if j >= 2:
                    p.on("vector", lambda e, j=j: e.tensor_max(
                        amx[:, 3:4], amx[:, 3:4],
                        amx[:, 4 + (j - 2) % 2:5 + (j - 2) % 2]), inc=[("o_v", 1)])
        p.vf()
        lp = (NH - 1) * S + 3 * 512
        p.on("vector", lambda e, lp=lp: e.reduce_max(
            amx[:, 4 + 95 % 2:5 + 95 % 2], gate_sb[:, lp:lp + 512], axis=AXX,
            apply_absolute_value=True))
        p.vf()
        p.on("vector", lambda e: e.tensor_max(
            amx[:, 3:4], amx[:, 3:4], amx[:, 4 + 94 % 2:5 + 94 % 2]),
            inc=[("o_v", 1)])
        p.vf()
        p.on("vector", lambda e: e.tensor_max(
            amx[:, 3:4], amx[:, 3:4], amx[:, 4 + 95 % 2:5 + 95 % 2]),
            inc=[("o_v", 1)])

        p.vf()
        p.on("vector", lambda e: e.tensor_copy(parb[:, 0:1], amx[:, 3:4]),
             wait_exact=[("o_v", p.cnt("o_v"))])
        p.on("vector", lambda e: e.tensor_copy(parb[:, 1:2], amx[:, 3:4]),
             inc=[("oo_pin", 1)])
        allreduce(1, "oo")
        p.on("vector", lambda e: e.tensor_mul(scal[0:1, 0:1], scal[0:1, 0:1],
                                              scal[0:1, 6:7]))
        p.on("vector", lambda e: e.tensor_scalar(
            scal[0:1, 4:5], scal[0:1, 0:1], 1.0 / 32767.0, 1e-12, ALU.mult, ALU.max))
        p.on("vector", lambda e: e.reciprocal(scal[0:1, 5:6], scal[0:1, 4:5]),
             inc=[("rcp", 1)])
        p.wait("vector", ["rcp"])
        p.on("vector", lambda e: e.tensor_mul(scal[0:1, 5:6], scal[0:1, 5:6],
                                              scal[0:1, 6:7]), inc=[("oq_sc", 1)])
        p.on("tensor", lambda e: e.matmul(
            PB[7][:, 0:2], onesf[:], scal[0:1, 4:6], start=True, stop=True),
            wait_exact=[("oq_sc", p.cnt("oq_sc"))], inc=[("oq_sc", 1)])
        p.on("vector", lambda e: e.tensor_copy(scbc[:, 4:6], PB[7][:, 0:2]),
             wait_exact=[("oq_sc", p.cnt("oq_sc"))], inc=[("oq_bc", 1)])
        def _oq(j):
            sl = slice(j * 512, (j + 1) * 512)
            buf = i32t if j % 2 == 0 else i32u
            p.on("scalar", lambda e, sl=sl, buf=buf: e.activation(
                out=buf[:], in_=gate_sb[:, sl], func=AF.Identity,
                scale=scbc[:, 5:6], bias=scbc[:, 3:4]),
                wait_exact=[("oq_bc", p.cnt("oq_bc"))] +
                           ([("oq_v", j - 1)] if j >= 2 else []),
                inc=[("oq_s", 1)])
            p.on("vector", lambda e, sl=sl, buf=buf: e.scalar_tensor_tensor(
                gate_sb[:, sl], buf[:], -32768.0,
                scbc[:, 4:5].broadcast_to([128, 512]), ALU.add, ALU.mult),
                wait_exact=[("oq_s", p.cnt("oq_s"))], inc=[("oq_v", 1)])
        for j in range(4):
            _oq(j)

        for ttg in range(4):
            for hc in range(NH):
                if ttg == 0:
                    for j in range(4 * (hc + 1), min(4 * (hc + 2), NH * 4)):
                        _oq(j)
                p.on("sync", lambda e, hc=hc: e.dma_start(
                    out=wdn[hc % 2].ap(),
                    in_=scr["down"].ap()[hc, :, :]),
                    wait_exact=([("dn_hc", max(0, p.cnt("dn_hc") - 1))]
                                if (ttg > 0 or hc >= 2) else
                                ([("up1_mm", NH * 4)] if ttg == 0 else [])),
                    inc=[("d_dn", 16)])
                for m4 in range(4):
                    for oc in range(2):
                        pbd = PB[m4 * 2 + oc]
                        wex = []
                        if hc == 0 and m4 == 0 and oc == 0:
                            wex = [("oq_v", 4 if ttg == 0 else NH * 4),
                                   ("dn_ev", p.cnt("dn_ev")),
                                   ("d_dn", p.cnt("d_dn"))]
                        elif hc > 0 and m4 == 0 and oc == 0:
                            wex = [("d_dn", p.cnt("d_dn"))]
                            if ttg == 0:
                                wex += [("oq_v", min((hc + 1) * 4, NH * 4))]
                        p.on("tensor", lambda e, pbd=pbd, hc=hc, m4=m4, oc=oc, ttg=ttg: e.matmul(
                            pbd[:], gate_sb[:, hc * S + (ttg * 4 + m4) * 128:
                                            hc * S + (ttg * 4 + m4) * 128 + 128],
                            wdn[hc % 2][:, oc * 512:(oc + 1) * 512],
                            start=(hc == 0), stop=(hc == NH - 1)),
                            wait_exact=wex,
                            inc=([("dn_mm", 1)] if hc == NH - 1 else None))
                        if m4 == 3 and oc == 1:
                            p.on("tensor", lambda e: e.nop(), inc=[("dn_hc", 1)])
            for m4 in range(4):
                tt = ttg * 4 + m4
                for oc in range(2):
                    pbd = PB[m4 * 2 + oc]
                    lo = tt * D + oc * 512
                    p.on("vector", lambda e, pbd=pbd, lo=lo: e.scalar_tensor_tensor(
                        x_res[:, lo:lo + 512], pbd[:], 1.0,
                        x_res[:, lo:lo + 512], ALU.mult, ALU.add),
                        wait_exact=[("dn_mm", p.cnt("dn_mm"))],
                        inc=[("dn_ev", 1)])

    import contextlib

    def emit_dump(key, src_ap):
        if not dbg:
            return
        p.on("sync", lambda e, key=key, src_ap=src_ap: e.dma_start(
            out=dbg_t[key].ap(), in_=src_ap), inc=[("d_out", 16)])

    with contextlib.ExitStack() as stack:
        _sqd = stack.enter_context(nc.sbuf_tensor("sqd_a", [128, D], BF16))
        _xtm = stack.enter_context(nc.sbuf_tensor("xtm_a", [128, 2 * D], BF16))
        emit_layernorm("a", _sqd, _xtm, first=True, spill=scr["xn"])
        p.barrier()
        if dbg:
            emit_dump("xn", xn_fm.ap())
            p.on("sync", lambda e: e.dma_start(
                out=dbg_t["stats"].ap()[:, 0:NT], in_=m_sb.ap()), inc=[("d_out", 16)])
            p.on("sync", lambda e: e.dma_start(
                out=dbg_t["stats"].ap()[:, NT:2 * NT], in_=r_sb.ap()), inc=[("d_out", 16)])
            for i_, tn in enumerate((dbg_s1, dbg_s2, dbg_s3, dbg_s4)):
                p.on("sync", lambda e, i_=i_, tn=tn: e.dma_start(
                    out=dbg_t["stats"].ap()[:, (2 + i_) * NT:(3 + i_) * NT],
                    in_=tn.ap()), inc=[("d_out", 16)])
    with contextlib.ExitStack() as stack:
        st = {}
        st["WT"] = {w: stack.enter_context(
            nc.sbuf_tensor("WT_" + w, [128, ND * D], BF16))
            for w in ("Wq", "Wk", "Wv", "Wo")}
        st["wstage"] = stack.enter_context(nc.sbuf_tensor("wstage", [128, 4 * D], BF16))
        st["q_pair"] = stack.enter_context(nc.sbuf_tensor("q_pair", [128, S], BF16))
        st["k_pair"] = stack.enter_context(nc.sbuf_tensor("k_pair", [128, S], BF16))
        st["v_aug"] = stack.enter_context(nc.sbuf_tensor("v_aug", [128, NT * 130], BF16))
        st["ctx_pair"] = stack.enter_context(nc.sbuf_tensor("ctx_pair", [128, S], BF16))
        st["e_sb"] = stack.enter_context(nc.sbuf_tensor("e_sb", [128, 8 * 512], BF16))
        st["ctxr"] = [stack.enter_context(nc.sbuf_tensor("ctxr%d" % i, [65, 512], F32))
                      for i in range(2)]
        st["r_row"] = [stack.enter_context(nc.sbuf_tensor("r_row%d" % i, [1, 512], BF16))
                       for i in range(2)]
        st["ones_row"] = stack.enter_context(nc.sbuf_tensor("ones_row", [1, 512], BF16))
        st["bv_row"] = stack.enter_context(nc.sbuf_tensor("bv_row", [1, D], BF16))
        st["bo_row"] = stack.enter_context(nc.sbuf_tensor("bo_row", [1, D], BF16))
        st["alibi"] = stack.enter_context(nc.sbuf_tensor("alibi", [128, NHEAD * NT], F32))
        st["slopes_bc"] = stack.enter_context(nc.sbuf_tensor("slopes_bc", [128, NHEAD], F32))
        emit_attn_l0(st)
        p.barrier()
    emit_dump("attn", x_res.ap())

    with contextlib.ExitStack() as stack:
        st = {}
        st["gw_stage"] = stack.enter_context(nc.sbuf_tensor("gw_stage", [8, D], BF16))
        st["gwT"] = stack.enter_context(nc.sbuf_tensor("gwT", [128, ND * 8], BF16))
        st["logit_sb"] = stack.enter_context(nc.sbuf_tensor("logit_sb", [128, 8], F32))
        st["comb"] = stack.enter_context(nc.sbuf_tensor("comb", [128, NT * 8], F32))
        st["g_t0"] = stack.enter_context(nc.sbuf_tensor("g_t0", [128, 8], F32))
        st["g_t1"] = stack.enter_context(nc.sbuf_tensor("g_t1", [128, 8], F32))
        st["g_c0"] = stack.enter_context(nc.sbuf_tensor("g_c0", [128, 1], F32))
        st["g_c1"] = stack.enter_context(nc.sbuf_tensor("g_c1", [128, 1], F32))
        st["w1g"] = [stack.enter_context(nc.sbuf_tensor("w1g%d" % i, [128, ND * 384], BF16))
                     for i in range(2)]
        st["w3g"] = [stack.enter_context(nc.sbuf_tensor("w3g%d" % i, [128, ND * 384], BF16))
                     for i in range(2)]
        st["w2g"] = stack.enter_context(nc.sbuf_tensor("w2g", [128, 6 * D], BF16))
        st["h_sb"] = stack.enter_context(nc.sbuf_tensor("h_sb0", [128, 6 * 768], BF16))
        st["hsil"] = [stack.enter_context(nc.sbuf_tensor("hsil%d" % i, [128, 512], BF16))
                      for i in range(2)]
        st["wq_stage"] = stack.enter_context(nc.sbuf_tensor("wq_stage", [128, 512], F32))
        st["dq_s"] = stack.enter_context(nc.sbuf_tensor("dq_s", [128, 16], F32))
        st["dq_si"] = stack.enter_context(nc.sbuf_tensor("dq_si", [128, 16], F32))
        st["dq_t"] = stack.enter_context(nc.sbuf_tensor("dq_t", [128, 512], F32))
        st["dq_i"] = stack.enter_context(nc.sbuf_tensor("dq_i", [128, 512], I32))
        st["dq_w"] = stack.enter_context(nc.sbuf_tensor("dq_w", [128, 512], BF16))
        st["tr_stage"] = stack.enter_context(nc.sbuf_tensor("tr_stage", [128, 4 * 128], BF16))
        st["tri_sb"] = stack.enter_context(nc.sbuf_tensor("tri_sb", [128, 128], BF16))
        st["stri_sb"] = stack.enter_context(nc.sbuf_tensor("stri_sb", [128, 128], BF16))
        st["iota_sb"] = stack.enter_context(nc.sbuf_tensor("iota_sb", [128, 768], F32))
        st["route_m"] = stack.enter_context(nc.sbuf_tensor("route_m", [128, 128], BF16))
        st["t_sb"] = stack.enter_context(nc.sbuf_tensor("t_sb", [128, 128], F32))
        st["pos"] = stack.enter_context(nc.sbuf_tensor("pos", [128, 128], F32))
        st["onesb"] = stack.enter_context(nc.sbuf_tensor("onesb", [128, 1], BF16))
        st["ccol_sb"] = stack.enter_context(nc.sbuf_tensor("ccol_sb", [128, 1], BF16))
        st["base_sb"] = stack.enter_context(nc.sbuf_tensor("base_sb", [1, 128], F32))
        st["pt_sb"] = [stack.enter_context(nc.sbuf_tensor("pt%d" % i, [128, 768], BF16))
                       for i in range(2)]
        st["ptT_sb"] = [stack.enter_context(nc.sbuf_tensor("ptT%d" % i, [128, 768], BF16))
                        for i in range(2)]
        st["xt_sb"] = [stack.enter_context(nc.sbuf_tensor("xt%d" % i, [128, 512], BF16))
                       for i in range(2)]
        st["gth_sb"] = [stack.enter_context(nc.sbuf_tensor("gth%d" % i, [128, 512], BF16))
                        for i in range(2)]
        st["gath_fm"] = stack.enter_context(nc.sbuf_tensor("gath_fm", [128, 8 * 768], BF16))
        st["y_sb"] = stack.enter_context(nc.sbuf_tensor("y_sb", [128, 6 * D], F32))
        dq_gen = emit_dequant(st)
        emit_moe(st, filler=dq_gen)
        for _ in dq_gen:
            pass
        p.barrier()
    emit_dump("moe", x_res.ap())
    if dbg:
        p.on("sync", lambda e: e.dma_start(
            out=dbg_t["scr"].ap(), in_=scr["up"].ap()), inc=[("d_out", 16)])

    with contextlib.ExitStack() as stack:
        _sqd = stack.enter_context(nc.sbuf_tensor("sqd_b", [128, D], BF16))
        _xtm = stack.enter_context(nc.sbuf_tensor("xtm_b", [128, 2 * D], BF16))
        emit_layernorm("b", _sqd, _xtm)
        p.barrier()

    with contextlib.ExitStack() as stack:
        st = {}
        st["wl1"] = stack.enter_context(nc.sbuf_tensor("wl1", [128, ND * 512], BF16))
        st["fmA"] = stack.enter_context(nc.sbuf_tensor("fmA", [128, ND * S], BF16))
        st["fmB"] = stack.enter_context(nc.sbuf_tensor("fmB", [128, ND * S], BF16))
        st["fres"] = stack.enter_context(nc.sbuf_tensor("fres", [128, ND * NF], BF16))
        st["fims"] = stack.enter_context(nc.sbuf_tensor("fims", [128, ND * NF], BF16))
        st["sre"] = stack.enter_context(nc.sbuf_tensor("sre", [128, 8], F32))
        st["sim"] = stack.enter_context(nc.sbuf_tensor("sim", [128, 8], F32))
        st["tA"] = stack.enter_context(nc.sbuf_tensor("tA", [128, 512], BF16))
        st["tB"] = stack.enter_context(nc.sbuf_tensor("tB", [128, 512], BF16))
        st["u1"] = stack.enter_context(nc.sbuf_tensor("u1", [128, 512], BF16))
        st["u2"] = stack.enter_context(nc.sbuf_tensor("u2", [128, 512], F32))
        st["u3"] = stack.enter_context(nc.sbuf_tensor("u3", [128, 512], BF16))
        st["u4"] = stack.enter_context(nc.sbuf_tensor("u4", [128, 512], F32))
        st["ab_a"] = stack.enter_context(nc.sbuf_tensor("ab_a", [128, 5 * 512], BF16))
        st["ab_b"] = stack.enter_context(nc.sbuf_tensor("ab_b", [128, 5 * 512], BF16))
        st["bl1_sb"] = stack.enter_context(nc.sbuf_tensor("bl1_sb", [128, 3 * ND], F32))
        emit_attn_l1(st)
        p.barrier()
    emit_dump("l1a", x_res.ap())

    with contextlib.ExitStack() as stack2:
        st2 = {}
        st2["whg"] = [stack2.enter_context(
            nc.sbuf_tensor("whg%d" % i, [128, ND * 128], BF16)) for i in range(2)]
        st2["wdn"] = [stack2.enter_context(
            nc.sbuf_tensor("wdn%d" % i, [128, D], BF16)) for i in range(2)]
        st2["i32t"] = stack2.enter_context(nc.sbuf_tensor("i32t", [128, 512], I32))
        st2["i32u"] = stack2.enter_context(nc.sbuf_tensor("i32u", [128, 512], I32))
        st2["sgt"] = stack2.enter_context(nc.sbuf_tensor("sgt", [128, 512], BF16))
        st2["amx"] = stack2.enter_context(nc.sbuf_tensor("amx", [128, 8], F32))
        st2["parb"] = stack2.enter_context(nc.sbuf_tensor("parb", [128, 2], F32))
        st2["paro"] = stack2.enter_context(nc.sbuf_tensor("paro", [128, 2], F32))
        st2["scal"] = stack2.enter_context(nc.sbuf_tensor("scal", [1, 8], F32))
        st2["scbc"] = stack2.enter_context(nc.sbuf_tensor("scbc", [128, 8], F32))
        st2["gate_sb"] = stack2.enter_context(
            nc.sbuf_tensor("gate_sb", [128, (L1_HID // 128) * S], BF16))
        emit_ffn_l1(st2)
        p.barrier()

    p.on("sync", lambda e: e.dma_start(
        out=out.ap().rearrange("(t p) d -> p t d", p=128),
        in_=x_res.ap().rearrange("p (t d) -> p t d", d=D)),
        inc=[("d_out", 16)])
    p.on("gpsimd", None, wait=["d_out"])

    p.finalize()
    return nc


_NC_CACHE = None


def _consts():
    n = np.arange(D)[:, None].astype(np.float64)
    f = np.arange(NF)[None, :].astype(np.float64)
    th = 2.0 * np.pi * n * f / D
    fre = np.cos(th)
    fim = -np.sin(th)
    c = np.full(NF, 2.0); c[0] = 1.0; c[NF - 1] = 1.0
    nn = np.arange(D)[None, :].astype(np.float64)
    thg = 2.0 * np.pi * f.T * nn / D
    gre = (c[:, None] * np.cos(thg)) / D
    gim = (-c[:, None] * np.sin(thg)) / D
    bf = ml_dtypes.bfloat16
    tri = np.triu(np.ones((128, 128))).astype(bf)
    kk = np.arange(128)
    stri = ((kk[:, None] % 8 == kk[None, :] % 8) &
            (kk[:, None] // 8 < kk[None, :] // 8)).astype(bf)
    iotaw = np.broadcast_to(np.arange(768, dtype=np.float32), (128, 768)).copy()
    return (fre.astype(bf), fim.astype(bf), gre.astype(bf), gim.astype(bf),
            tri, stri, iotaw)


def kernel(**inputs):
    global _NC_CACHE
    if _NC_CACHE is None:
        _NC_CACHE = build_nc()
    nc = _NC_CACHE
    fre, fim, gre, gim, tri, stri, iotaw = _consts()
    base = {
        "L0_Wq": inputs["L0_attn_Wq"], "L0_Wk": inputs["L0_attn_Wk"],
        "L0_Wv": inputs["L0_attn_Wv"], "L0_Wo": inputs["L0_attn_Wo"],
        "L0_bq": inputs["L0_attn_bq"], "L0_bk": inputs["L0_attn_bk"],
        "L0_bv": inputs["L0_attn_bv"], "L0_bo": inputs["L0_attn_bo"],
        "L0_slopes": inputs["L0_attn_slopes"],
        "gate_w": inputs["L0_ffn_gate_w"],
        "w1": inputs["L0_ffn_w1"], "w2": inputs["L0_ffn_w2"], "w3": inputs["L0_ffn_w3"],
        "sw1": inputs["L0_ffn_sw1"], "sw2": inputs["L0_ffn_sw2"], "sw3": inputs["L0_ffn_sw3"],
        "L1_Wq": inputs["L1_attn_Wq"], "L1_Wk": inputs["L1_attn_Wk"],
        "L1_Wv": inputs["L1_attn_Wv"],
        "L1_bq": inputs["L1_attn_bq"], "L1_bk": inputs["L1_attn_bk"],
        "L1_bv": inputs["L1_attn_bv"],
        "w_gate": inputs["L1_ffn_w_gate"], "w_up": inputs["L1_ffn_w_up"],
        "w_down": inputs["L1_ffn_w_down"],
        "fre": fre, "fim": fim, "gre": gre, "gim": gim,
        "tri": tri, "stri": stri, "iotaw": iotaw,
    }
    base = {k: np.ascontiguousarray(np.asarray(v)) for k, v in base.items()}
    x = np.asarray(inputs["L0_attn_x"])
    in_maps = []
    for c in range(8):
        m = dict(base)
        m["x"] = np.ascontiguousarray(x[c])
        in_maps.append(m)
    res = run_bass_kernel_spmd(nc, in_maps, core_ids=list(range(8)))
    return np.stack([res.results[c]["out"] for c in range(8)], axis=0)


if __name__ == "__main__":
    nc = build_nc()
    print("built OK, instructions:", nc.n_instructions())
